# version 27
# baseline (speedup 1.0000x reference)
import contextlib
import math

import numpy as np
import ml_dtypes

import concourse.bass as bass
import concourse.mybir as mybir
from concourse.bass_utils import run_bass_kernel_spmd

F32 = mybir.dt.float32
BF16 = mybir.dt.bfloat16
I32 = mybir.dt.int32
AF = mybir.ActivationFunctionType
OP = mybir.AluOpType
AX = mybir.AxisListType


def full_cfg():
    return dict(DM=2048, DI=4096, DS=16, DC=4, FF=8192, LOWN=1024)


def small_cfg():
    return dict(DM=256, DI=512, DS=16, DC=4, FF=1024, LOWN=256)


class DStream:

    def __init__(self, mksem, name, nbuf=2):
        self.sems = [mksem(f"{name}_{i}") for i in range(nbuf)]
        self.n = nbuf

    def inc(self, instr, i):
        instr.then_inc(self.sems[i % self.n], 16)

    def wait(self, eng, i):
        eng.wait_ge(self.sems[i % self.n], 16 * (i // self.n + 1))

    def wait_all(self, eng, total):
        for p in range(self.n):
            cnt = (total - p + self.n - 1) // self.n
            if cnt > 0:
                eng.wait_ge(self.sems[p], 16 * cnt)


def build_program(C):
    DM, DI, DS, DC = C["DM"], C["DI"], C["DS"], C["DC"]
    FF, LOWN = C["FF"], C["LOWN"]
    DTR = 128
    L = 2 * LOWN
    KT = DM // 128
    MT = DI // 128
    FFT = FF // 128
    TT = L // 128
    TTO = LOWN // 128
    CW_L = min(512, L)
    CW_LO = min(512, LOWN)
    CW_DM = min(512, DM)
    CW_FF = min(512, FF)
    LC = L // CW_L
    LCO = LOWN // CW_LO
    DMC = DM // CW_DM
    FFC = FF // CW_FF
    NPRE = TT - TTO
    NSTG = int(math.log2(FF))
    assert 2 ** NSTG == FF
    KT8 = (KT + 7) // 8
    FFT8 = (FFT + 7) // 8
    WXC = DTR + 2 * DS

    nc = bass.Bass("TRN2", target_bir_lowering=False, debug=False)

    din = lambda n, s, d=F32: nc.dram_tensor(n, s, d, kind="ExternalInput").ap()
    x_in = din("x_in", [L, DM])
    w_in4 = din("w_in4", [2 * MT, 128, KT, 128])
    conv_w = din("conv_w", [128, MT * DC])
    conv_b = din("conv_b", [128, MT])
    w_x4 = din("w_x4", [128, MT * WXC])
    w_dtT = din("w_dtT", [DTR, DI])
    dt_b = din("dt_b", [128, MT])
    a_log = din("a_log", [128, MT * DS])
    d_vec = din("d_vec", [128, MT])
    w_oT = din("w_oT", [DI, DM])
    w_gT = din("w_gT", [DM, FF])
    w_uT = din("w_uT", [DM, FF])
    w_dT = din("w_dT", [FF, DM])
    ident = din("ident", [128, 128], BF16)
    maskrow = din("maskrow", [1, L])

    out_t = nc.dram_tensor("out", [LOWN, DM], F32, kind="ExternalOutput").ap()

    dint = lambda n, s, d: nc.dram_tensor(n, s, d, kind="Internal").ap()
    xld_sp = dint("xld_sp", [MT, 128, L], BF16)
    sz_sp = dint("sz_sp", [MT, 128, LOWN], BF16)
    q1t_sp = dint("q1t_sp", [TTO, 128, KT * 128], BF16)
    y2_sp = dint("y2_sp", [MT, 128, LOWN], BF16)
    tern_g = dint("tern_g", [FFC, 128, KT, CW_FF], BF16)
    tern_u = dint("tern_u", [FFC, 128, KT, CW_FF], BF16)
    tern_d = dint("tern_d", [FF, DM], BF16)
    wo_bf = dint("wo_bf", [DI, DM], BF16)
    brep_sp = dint("brep_sp", [DS, 128, L], BF16)
    sg_sp = dint("sg_sp", [TTO, 128, FF], BF16)
    up_sp = dint("up_sp", [TTO, 128, FF], BF16)
    crep_sp = dint("crep_sp", [DS, 128, LOWN], BF16)
    outm_sp = dint("outm_sp", [TTO, 128, DM], F32)

    s0 = contextlib.ExitStack()
    sb0 = lambda n, s, d=F32: s0.enter_context(nc.sbuf_tensor(n, s, d))
    sem = lambda n: s0.enter_context(nc.semaphore(n))

    ident_s = sb0("ident_s", [128, 128], BF16)
    ones_f = sb0("ones_f", [1, 128], F32)
    onescol = sb0("onescol", [128, 1], F32)
    ones_b = sb0("ones_b", [1, 128], BF16)

    convw_s = sb0("convw_s", [128, MT * DC], F32)
    convb_s = sb0("convb_s", [128, MT], F32)
    dtb_s = sb0("dtb_s", [128, MT], F32)
    dvec_s = sb0("dvec_s", [128, MT], F32)

    teff_s = sb0("teff_s", [128, TTO], F32)
    rs_s = sb0("rs_s", [128, TTO], F32)
    mxg_c = sb0("mxg_c", [128, 1], F32)
    mxu_c = sb0("mxu_c", [128, 1], F32)
    mxd_c = sb0("mxd_c", [128, 1], F32)
    swcol_d = sb0("swcol_d", [128, 1], F32)
    one1 = sb0("one1", [1, 1], F32)
    gg_s = sb0("gg_s", [128, TTO], F32)
    kk_s = sb0("kk_s", [128, TTO], F32)
    eps_c = sb0("eps_c", [128, 1], F32)

    s_early = contextlib.ExitStack()
    nega = s_early.enter_context(nc.sbuf_tensor("nega", [128, MT * DS], F32))
    mask32 = s_early.enter_context(
        nc.sbuf_tensor("mask32", [2 * DS, L], BF16))

    with s0:
        with contextlib.ExitStack() as sp:
            psb = lambda n, s, d=F32: sp.enter_context(nc.sbuf_tensor(n, s, d))
            pps = lambda n, s, d=F32: sp.enter_context(nc.psum_tensor(n, s, d))
            mrow_s = psb("mrow_s", [1, L], F32)
            ps_m32 = pps("ps_m32", [2 * DS, L], F32)
            ld = sem("p0_ld")
            ac = sem("p0_ac")
            ve = sem("p0_ve")
            te = sem("p0_te")
            with nc.Block("p0") as block:
                @block.sync
                def _(sync):
                    for dst, src in [
                        (ident_s[:, :], ident),
                        (nega[:, :], a_log),
                        (convw_s[:, :], conv_w),
                        (dtb_s[:, :], dt_b),
                        (convb_s[:, :], conv_b),
                        (dvec_s[:, :], d_vec),
                        (mrow_s[0:1, :], maskrow),
                    ]:
                        sync.dma_start(out=dst, in_=src).then_inc(ld, 16)

                @block.scalar
                def _(scalar):
                    scalar.wait_ge(ld, 7 * 16)
                    scalar.activation(nega[:, :], nega[:, :], AF.Exp)
                    scalar.drain().then_inc(ac, 1)
                    scalar.wait_ge(te, 1)
                    scalar.activation(mask32[:, :], ps_m32[0:2 * DS, :],
                                      AF.Copy)
                    scalar.drain()

                @block.vector
                def _(vector):
                    vector.wait_ge(ld, 7 * 16)
                    vector.memset(ones_f[:, :], 1.0)
                    vector.memset(eps_c[:, :], 1e-5)
                    vector.memset(onescol[:, :], 1.0)
                    vector.memset(ones_b[:, :], 1.0)
                    vector.wait_ge(ac, 1)
                    vector.tensor_scalar_mul(nega[:, :], nega[:, :], -1.0)
                    vector.drain().then_inc(ve, 1)

                @block.tensor
                def _(tensor):
                    tensor.wait_ge(ve, 1)
                    mi = None
                    for c in range(LC):
                        mi = tensor.matmul(
                            ps_m32[0:2 * DS, c * CW_L:(c + 1) * CW_L],
                            ones_f[0:1, 0:2 * DS],
                            mrow_s[0:1, c * CW_L:(c + 1) * CW_L],
                            start=True, stop=True)
                    mi.then_inc(te, 1)

        with contextlib.ExitStack() as sp:
            psb = lambda n, s, d=F32: sp.enter_context(nc.sbuf_tensor(n, s, d))
            pps = lambda n, s, d=F32: sp.enter_context(nc.psum_tensor(n, s, d))
            CB = max(DM, FF)
            wbuf = [psb(f"pw_w{i}", [128, CB], F32) for i in range(2)]
            cbuf = [psb(f"pw_c{i}", [128, CB], F32) for i in range(2)]
            pbuf = psb("pw_p", [128, CB], BF16)
            qbuf = psb("pw_q", [128, CB], BF16)
            tbuf = [psb(f"pw_t{i}", [128, CB], BF16) for i in range(2)]
            JW = min(CB, 2048)
            NJC = (CB + JW - 1) // JW
            junkb = psb("pw_junk", [128, JW], BF16)
            rcol = psb("pw_rcol", [128, NJC], F32)
            racc2 = psb("pw_racc2", [128, 2], F32)
            swcol = psb("pw_swcol", [128, 1], F32)
            ps_tot = pps("pw_pstot", [1, 1], F32)
            ps_bc = pps("pw_psbc", [128, 1], F32)

            lda = DStream(sem, "pw_lda")
            ldb = DStream(sem, "pw_ldb")
            ldo2 = DStream(sem, "pw_ldo2")
            sto = DStream(sem, "pw_sto")
            ac = sem("pw_ac")
            aco = sem("pw_aco")
            ve = sem("pw_ve")
            te = sem("pw_te")
            stp = DStream(sem, "pw_st")

            MA = {"g": (w_gT, DM // 128, FF, mxg_c, 0),
                  "u": (w_uT, DM // 128, FF, mxu_c, 1),
                  "d": (w_dT, FF // 128, DM, mxd_c, 0)}
            MC = {"g": (w_gT, tern_g, DM // 128, FF, FFC),
                  "u": (w_uT, tern_u, DM // 128, FF, FFC),
                  "d": (w_dT, tern_d, FF // 128, DM, 1)}
            RT_O = DI // 128
            pwb = {"a": 0, "v": 0, "t": 0, "l0": 0, "l1": 0,
                   "b0": 0, "b1": 0, "s0": 0, "s1": 0}

            def pw_block(name, mc, ma, wo=False):
                bA, bV, bT = pwb["a"], pwb["v"], pwb["t"]
                bL = [pwb["l0"], pwb["l1"]]
                bB = [pwb["b0"], pwb["b1"]]
                bS = [pwb["s0"], pwb["s1"]]
                RTc = MC[mc][2] if mc else 0
                per = MC[mc][4] if mc else 1
                RTa = MA[ma][1] if ma else 0
                nb2 = 1 if mc else 0

                def ve_tern(r):
                    return (bV + nb2 + 2 * r + 1 if r < RTa
                            else bV + nb2 + 2 * RTa + (r - RTa) + 1)

                def ve_racc(r):
                    return (bV + nb2 + 2 * r + 2 if r < RTc
                            else bV + nb2 + 2 * RTc + (r - RTc) + 1)

                with nc.Block(name) as block:
                    @block.sync
                    def _(sync):
                        for r in range(max(RTc, RTa, RT_O if wo else 0)):
                            if mc and r < RTc:
                                if r >= 2:
                                    sync.wait_ge(ve, ve_tern(r - 2))
                                sync.dma_start(
                                    out=cbuf[r % 2][:, 0:MC[mc][3]],
                                    in_=MC[mc][0][r * 128:(r + 1) * 128, :]
                                ).then_inc(ldb.sems[r % 2], 16)
                            if ma and r < RTa:
                                if r >= 2:
                                    sync.wait_ge(ac, bA + nb2 + r - 1)
                                sync.dma_start(
                                    out=wbuf[r % 2][:, 0:MA[ma][2]],
                                    in_=MA[ma][0][r * 128:(r + 1) * 128, :]
                                ).then_inc(lda.sems[r % 2], 16)
                            if wo and r < RT_O:
                                if r >= 2:
                                    sync.wait_ge(aco, r - 1)
                                sync.dma_start(
                                    out=cbuf[r % 2][:, 0:DM],
                                    in_=w_oT[r * 128:(r + 1) * 128, :]
                                ).then_inc(ldo2.sems[r % 2], 16)

                    @block.tensor
                    def _(tensor):
                        if not mc:
                            return
                        col = MA[mc][4]
                        tensor.wait_ge(ve, bV)
                        tensor.matmul(ps_tot[0:1, 0:1],
                                      racc2[:, col:col + 1],
                                      onescol[:, 0:1],
                                      start=True, stop=True).then_inc(te, 1)
                        tensor.wait_ge(ve, bV + 1)
                        tensor.matmul(ps_bc[:, 0:1], ones_f[0:1, :],
                                      one1[0:1, 0:1], start=True, stop=True
                                      ).then_inc(te, 1)

                    @block.scalar
                    def _(scalar):
                        if mc:
                            scalar.wait_ge(te, bT + 2)
                            scalar.activation(MA[mc][3][:, :],
                                              ps_bc[:, 0:1], AF.Copy)
                            scalar.drain().then_inc(ac, 1)
                        for r in range(max(RTa, RT_O if wo else 0)):
                            if ma and r < RTa:
                                scalar.wait_ge(lda.sems[r % 2],
                                               16 * (bL[r % 2]
                                                     + r // 2 + 1))
                                if r >= 1:
                                    scalar.wait_ge(ve, ve_racc(r - 1))
                                Ca = MA[ma][2]
                                for jc in range(0, Ca, JW):
                                    cw = min(JW, Ca - jc)
                                    if jc > 0:
                                        scalar.drain()
                                    scalar.activation(
                                        junkb[:, 0:cw],
                                        wbuf[r % 2][:, jc:jc + cw],
                                        AF.Abs,
                                        accum_out=rcol[:, jc // JW:
                                                       jc // JW + 1])
                                scalar.drain().then_inc(ac, 1)
                            if wo and r < RT_O:
                                scalar.wait_ge(ldo2.sems[r % 2],
                                               16 * (r // 2 + 1))
                                if r >= 2:
                                    scalar.wait_ge(
                                        sto.sems[(r - 2) % 2],
                                        16 * ((r - 2) // 2 + 1))
                                scalar.activation(tbuf[r % 2][:, 0:DM],
                                                  cbuf[r % 2][:, 0:DM],
                                                  AF.Copy)
                                scalar.drain().then_inc(aco, 1)

                    @block.vector
                    def _(vector):
                        if mc:
                            vector.wait_ge(te, bT + 1)
                            vector.tensor_copy(one1[:, :], ps_tot[0:1, 0:1])
                            vector.drain()
                            vector.tensor_scalar(
                                one1[:, :], one1[:, :],
                                1.0 / (MC[mc][2] * 128 * MC[mc][3]), 1e-5,
                                OP.mult, OP.max)
                            vector.drain().then_inc(ve, 1)
                            vector.wait_ge(ac, bA + 1)
                            vector.reciprocal(swcol[:, :], MA[mc][3][:, :])
                            vector.drain()
                        if ma:
                            vector.memset(racc2[:, MA[ma][4]:
                                                MA[ma][4] + 1], 0.0)
                            vector.drain()
                        for r in range(max(RTc, RTa)):
                            if mc and r < RTc:
                                Cc = MC[mc][3]
                                vector.wait_ge(ldb.sems[r % 2],
                                               16 * (bB[r % 2]
                                                     + r // 2 + 1))
                                w = cbuf[r % 2][:, 0:Cc]
                                vector.tensor_scalar(pbuf[:, 0:Cc], w,
                                                     swcol[:, 0:1], 0.5,
                                                     OP.mult, OP.is_gt)
                                vector.tensor_scalar(qbuf[:, 0:Cc], w,
                                                     swcol[:, 0:1], -0.5,
                                                     OP.mult, OP.is_lt)
                                vector.drain()
                                if r >= 2:
                                    vector.wait_ge(
                                        stp.sems[(r - 2) % 2],
                                        16 * (bS[(r - 2) % 2]
                                              + per * ((r - 2) // 2 + 1)))
                                vector.tensor_sub(tbuf[r % 2][:, 0:Cc],
                                                  pbuf[:, 0:Cc],
                                                  qbuf[:, 0:Cc])
                                vector.drain().then_inc(ve, 1)
                            if ma and r < RTa:
                                vector.wait_ge(ac, bA + nb2 + r + 1)
                                for jc in range((MA[ma][2] + JW - 1) // JW):
                                    vector.tensor_add(
                                        racc2[:, MA[ma][4]:MA[ma][4] + 1],
                                        racc2[:, MA[ma][4]:MA[ma][4] + 1],
                                        rcol[:, jc:jc + 1])
                                    vector.drain()
                                vector.drain().then_inc(ve, 1)

                    @block.gpsimd
                    def _(gpsimd):
                        for r in range(max(RTc, RT_O if wo else 0)):
                            if mc and r < RTc:
                                gpsimd.wait_ge(ve, ve_tern(r))
                                dst = MC[mc][1]
                                if per > 1:
                                    for cc in range(FFC):
                                        gpsimd.dma_start(
                                            out=dst[cc, :, r, :],
                                            in_=tbuf[r % 2][
                                                :, cc * CW_FF:
                                                (cc + 1) * CW_FF]
                                        ).then_inc(stp.sems[r % 2], 16)
                                else:
                                    gpsimd.dma_start(
                                        out=dst[r * 128:(r + 1) * 128, :],
                                        in_=tbuf[r % 2][:, 0:MC[mc][3]]
                                    ).then_inc(stp.sems[r % 2], 16)
                            if wo and r < RT_O:
                                gpsimd.wait_ge(aco, r + 1)
                                gpsimd.dma_start(
                                    out=wo_bf[r * 128:(r + 1) * 128, :],
                                    in_=tbuf[r % 2][:, 0:DM]
                                ).then_inc(sto.sems[r % 2], 16)
                        for p in range(2):
                            if mc:
                                cnt = per * ((RTc - p + 1) // 2)
                                if cnt:
                                    gpsimd.wait_ge(stp.sems[p],
                                                   16 * (bS[p] + cnt))
                            if wo:
                                cnt = (RT_O - p + 1) // 2
                                if cnt:
                                    gpsimd.wait_ge(sto.sems[p], 16 * cnt)

                pwb["a"] += nb2 + RTa
                pwb["v"] += nb2 + RTc + RTa
                pwb["t"] += 2 if mc else 0
                pwb["l0"] += (RTa + 1) // 2
                pwb["l1"] += RTa // 2
                pwb["b0"] += (RTc + 1) // 2
                pwb["b1"] += RTc // 2
                pwb["s0"] += per * ((RTc + 1) // 2)
                pwb["s1"] += per * (RTc // 2)

            pw_block("pw0", None, "g", wo=True)
            pw_block("pw1", "g", "u")
            pw_block("pw2", "u", "d")

            bA3, bV3, bT3 = pwb["a"], pwb["v"], pwb["t"]
            with nc.Block("pw3h") as block:
                @block.tensor
                def _(tensor):
                    tensor.matmul(ps_tot[0:1, 0:1], racc2[:, 0:1],
                                  onescol[:, 0:1], start=True, stop=True
                                  ).then_inc(te, 1)
                    tensor.wait_ge(ve, bV3 + 1)
                    tensor.matmul(ps_bc[:, 0:1], ones_f[0:1, :],
                                  one1[0:1, 0:1], start=True, stop=True
                                  ).then_inc(te, 1)

                @block.scalar
                def _(scalar):
                    scalar.wait_ge(te, bT3 + 2)
                    scalar.activation(mxd_c[:, :], ps_bc[:, 0:1], AF.Copy)
                    scalar.drain().then_inc(ac, 1)

                @block.vector
                def _(vector):
                    vector.wait_ge(te, bT3 + 1)
                    vector.tensor_copy(one1[:, :], ps_tot[0:1, 0:1])
                    vector.drain()
                    vector.tensor_scalar(one1[:, :], one1[:, :],
                                         1.0 / (FF * DM), 1e-5,
                                         OP.mult, OP.max)
                    vector.drain().then_inc(ve, 1)
                    vector.wait_ge(ac, bA3 + 1)
                    vector.reciprocal(swcol_d[:, :], mxd_c[:, :])
                    vector.drain()

        sa = contextlib.ExitStack()
        xnT = sa.enter_context(nc.sbuf_tensor("xnT", [128, KT * L], BF16))
        xnT_v = xnT.ap().rearrange("p (k l) -> p k l", k=KT)
        with sa:
            with contextlib.ExitStack() as sp:
                psb = lambda n, s, d=F32: sp.enter_context(
                    nc.sbuf_tensor(n, s, d))
                pps = lambda n, s, d=F32: sp.enter_context(
                    nc.psum_tensor(n, s, d))
                xs_b = [psb(f"p1_x{i}", [128, DM], F32) for i in range(2)]
                xc_b = psb("p1_xc", [128, DM], F32)
                jnk = psb("p1_junk", [128, DM], BF16)
                c_m = psb("p1_cm", [128, 1], F32)
                c_den = psb("p1_cden", [128, 1], F32)
                c_rstd = psb("p1_crstd", [128, 1], F32)
                c_amax = psb("p1_camax", [128, 1], F32)
                c_t = psb("p1_ct", [128, 1], F32)
                xn_b = psb("p1_xn", [128, DM], BF16)
                qv_b = psb("p1_qv", [128, DM], F32)
                qi_b = psb("p1_qi", [128, DM], I32)
                q0_b = psb("p1_q0", [128, DM], F32)
                qr_b = psb("p1_qr", [128, DM], F32)
                q1_b = psb("p1_q1", [128, DM], BF16)
                q1st = [psb(f"p1_q1st{i}", [128, KT * 128], BF16)
                        for i in range(2)]
                ps_tr = pps("p1_pstr", [128, 8 * 128], BF16)
                ps_tr2 = pps("p1_pstr2", [128, 8 * 128], BF16)
                lda = DStream(sem, "p1_ld")
                ve = sem("p1_ve")
                ac = sem("p1_ac")
                te = sem("p1_te")
                ev = sem("p1_ev")
                ev2 = sem("p1_ev2")
                stt_ = DStream(sem, "p1_st")

                def nown(ti):
                    return max(0, min(ti + 1, TT) - NPRE)

                def vaft(ti):
                    return 0 if ti < 0 else 2 * (ti + 1) + nown(ti)

                def taft(ti):
                    return 0 if ti < 0 else KT8 * (ti + 1) + KT8 * nown(ti)

                with nc.Block("p1") as block:
                    @block.sync
                    def _(sync):
                        for ti in range(TT):
                            if ti >= 2:
                                sync.wait_ge(ve, vaft(ti - 3) + 1)
                            lda.inc(sync.dma_start(
                                out=xs_b[ti % 2][:, :],
                                in_=x_in[ti * 128:(ti + 1) * 128, :]), ti)

                    @block.vector
                    def _(vector):
                        for ti in range(TT):
                            own = ti >= NPRE
                            lt = ti - NPRE
                            xt = xs_b[ti % 2]
                            lda.wait(vector, ti)
                            if ti >= 1:
                                vector.wait_ge(ac, ti)
                                vector.wait_ge(te, taft(ti - 2) + KT8)
                            vector.reduce_sum(c_m[:, :], xt[:, :], AX.X)
                            vector.drain()
                            vector.tensor_scalar_mul(c_m[:, :], c_m[:, :],
                                                     1.0 / DM)
                            vector.drain()
                            vector.tensor_scalar_sub(xc_b[:, :], xt[:, :],
                                                     c_m[:, 0:1])
                            vector.drain().then_inc(ve, 1)
                            vector.reduce_max(c_amax[:, :], xc_b[:, :], AX.X,
                                              apply_absolute_value=True)
                            vector.wait_ge(ac, ti + 1)
                            vector.reciprocal(c_rstd[:, :], c_den[:, :])
                            vector.drain()
                            if own:
                                vector.tensor_mul(c_t[:, :], c_amax[:, :],
                                                  c_rstd[:, :])
                                vector.drain()
                                vector.tensor_scalar_max(
                                    teff_s[:, lt:lt + 1], c_t[:, :], 1e-5)
                                vector.drain()
                                vector.reciprocal(c_t[:, :],
                                                  teff_s[:, lt:lt + 1])
                                vector.drain()
                                vector.tensor_scalar_mul(c_t[:, :],
                                                         c_t[:, :], 128.0)
                                vector.drain()
                                vector.tensor_mul(rs_s[:, lt:lt + 1],
                                                  c_rstd[:, :], c_t[:, :])
                                vector.drain()
                            vector.tensor_scalar_mul(xn_b[:, :], xc_b[:, :],
                                                     c_rstd[:, 0:1])
                            vector.drain().then_inc(ve, 1)
                            if own:
                                if lt >= 1:
                                    vector.wait_ge(te, taft(ti - 1))
                                vector.tensor_scalar_mul(qv_b[:, :],
                                                         xc_b[:, :],
                                                         rs_s[:, lt:lt + 1])
                                vector.drain()
                                vector.tensor_copy(qi_b[:, :], qv_b[:, :])
                                vector.drain()
                                vector.tensor_copy(q0_b[:, :], qi_b[:, :])
                                vector.drain()
                                vector.tensor_sub(qr_b[:, :], qv_b[:, :],
                                                  q0_b[:, :])
                                vector.drain()
                                vector.scalar_tensor_tensor(
                                    q0_b[:, :], qr_b[:, :], 0.5, q0_b[:, :],
                                    OP.is_ge, OP.add)
                                vector.drain()
                                vector.scalar_tensor_tensor(
                                    q0_b[:, :], qr_b[:, :], -0.5, q0_b[:, :],
                                    OP.is_gt, OP.add)
                                vector.drain()
                                vector.tensor_scalar(q1_b[:, :], q0_b[:, :],
                                                     -1.0, 127.0,
                                                     OP.add, OP.min)
                                vector.drain().then_inc(ve, 1)

                    @block.scalar
                    def _(scalar):
                        for ti in range(TT):
                            own = ti >= NPRE
                            lt = ti - NPRE
                            scalar.wait_ge(ve, vaft(ti - 1) + 1)
                            scalar.activation(jnk[:, :], xc_b[:, :],
                                              AF.Square,
                                              accum_out=c_den[:, :])
                            scalar.drain()
                            scalar.activation(c_den[:, :], c_den[:, :],
                                              AF.Sqrt, bias=eps_c[:, 0:1],
                                              scale=1.0 / DM)
                            scalar.drain().then_inc(ac, 1)
                            for g in range(KT8):
                                nk = min(8, KT - g * 8)
                                scalar.wait_ge(te, taft(ti - 1) + g + 1)
                                scalar.activation(
                                    xnT_v[:, g * 8:g * 8 + nk,
                                          ti * 128:(ti + 1) * 128],
                                    ps_tr[:, 0:nk * 128].rearrange(
                                        "p (k l) -> p k l", k=nk),
                                    AF.Copy)
                                scalar.drain().then_inc(ev, 1)
                            if own:
                                if lt >= 2:
                                    stt_.wait(scalar, lt - 2)
                                for g in range(KT8):
                                    nk = min(8, KT - g * 8)
                                    scalar.wait_ge(te,
                                                   taft(ti - 1) + KT8 + g + 1)
                                    scalar.activation(
                                        q1st[lt % 2][:, g * 8 * 128:
                                                     (g * 8 + nk) * 128],
                                        ps_tr2[:, 0:nk * 128], AF.Copy)
                                    scalar.drain().then_inc(ev2, 1)

                    @block.tensor
                    def _(tensor):
                        for ti in range(TT):
                            own = ti >= NPRE
                            lt = ti - NPRE
                            tensor.wait_ge(ve, vaft(ti - 1) + 2)
                            for g in range(KT8):
                                nk = min(8, KT - g * 8)
                                gidx = KT8 * ti + g
                                if gidx >= 1:
                                    tensor.wait_ge(ev, gidx)
                                mi = None
                                for j in range(nk):
                                    k = g * 8 + j
                                    mi = tensor.transpose(
                                        ps_tr[:, j * 128:(j + 1) * 128],
                                        xn_b[:, k * 128:(k + 1) * 128],
                                        ident_s[:, :])
                                mi.then_inc(te, 1)
                            if own:
                                tensor.wait_ge(ve, vaft(ti))
                                for g in range(KT8):
                                    nk = min(8, KT - g * 8)
                                    hidx = KT8 * lt + g
                                    if hidx >= 1:
                                        tensor.wait_ge(ev2, hidx)
                                    mi = None
                                    for j in range(nk):
                                        k = g * 8 + j
                                        mi = tensor.transpose(
                                            ps_tr2[:, j * 128:(j + 1) * 128],
                                            q1_b[:, k * 128:(k + 1) * 128],
                                            ident_s[:, :])
                                    mi.then_inc(te, 1)

                    @block.gpsimd
                    def _(gpsimd):
                        for lt in range(TTO):
                            gpsimd.wait_ge(ev2, KT8 * (lt + 1))
                            stt_.inc(gpsimd.dma_start(
                                out=q1t_sp[lt],
                                in_=q1st[lt % 2][:, :]), lt)
                        stt_.wait_all(gpsimd, TTO)

            with contextlib.ExitStack() as sp:
                psb = lambda n, s, d=F32: sp.enter_context(
                    nc.sbuf_tensor(n, s, d))
                pps = lambda n, s, d=F32: sp.enter_context(
                    nc.psum_tensor(n, s, d))
                wf = [psb(f"p2_wf{i}", [128, KT * 128], F32) for i in range(2)]
                wb = [psb(f"p2_wb{i}", [128, KT * 128], BF16)
                      for i in range(2)]
                xb = [psb(f"p2_xb{i}", [128, DC - 1 + L], BF16)
                      for i in range(2)]
                cvac = psb("p2_cvac", [128, L], F32)
                xld = [psb(f"p2_xld{i}", [128, L], BF16) for i in range(2)]
                szb = [psb(f"p2_szb{i}", [128, LOWN], BF16) for i in range(2)]
                ps_in = pps("p2_psin", [128, 2 * 512], F32)
                RTD = FF // 128
                cdb = [psb(f"p2_cd{i}", [128, DM], F32) for i in range(2)]
                pb_d = psb("p2_pbd", [128, DM], BF16)
                qb_d = psb("p2_qbd", [128, DM], BF16)
                tb_d = [psb(f"p2_tbd{i}", [128, DM], BF16)
                        for i in range(2)]
                lda = DStream(sem, "p2_ld")
                ldd2 = DStream(sem, "p2_ldd2")
                std2 = DStream(sem, "p2_std2")
                vd = sem("p2_vd")
                gp = sem("p2_gp")
                te = sem("p2_te")
                ac = sem("p2_ac")
                ve = sem("p2_ve")
                stx = DStream(sem, "p2_stx")
                stz = DStream(sem, "p2_stz")

                def chunks_of(m):
                    return LC if m < MT else LCO

                def te_aft(m):
                    return 0 if m < 0 else sum(chunks_of(i)
                                               for i in range(m + 1))

                def ac_aft(m):
                    if m < 0:
                        return 0
                    tot = 0
                    for i in range(m + 1):
                        tot += (LC + 1) if i < MT else LCO
                    return tot

                evac_done = []
                for m in range(2 * MT):
                    base = ac_aft(m - 1)
                    for c in range(chunks_of(m)):
                        evac_done.append(base + c + 1)

                with nc.Block("p2") as block:
                    @block.sync
                    def _(sync):
                        for m in range(2 * MT):
                            if m >= 2:
                                sync.wait_ge(gp, m - 1)
                            lda.inc(sync.dma_start(
                                out=wf[m % 2][:, :],
                                in_=w_in4[m].rearrange("p k m -> p (k m)")), m)
                            if m < RTD:
                                if m >= 2:
                                    sync.wait_ge(vd, m - 1)
                                ldd2.inc(sync.dma_start(
                                    out=cdb[m % 2][:, :],
                                    in_=w_dT[m * 128:(m + 1) * 128, :]), m)

                    @block.gpsimd
                    def _(gpsimd):
                        for mm in range(2 * MT):
                            if mm < RTD:
                                gpsimd.wait_ge(vd, mm + 1)
                                std2.inc(gpsimd.dma_start(
                                    out=tern_d[mm * 128:(mm + 1) * 128, :],
                                    in_=tb_d[mm % 2][:, :]), mm)
                            gpsimd.wait_ge(ac, ac_aft(mm))
                            if mm < MT:
                                stx.inc(gpsimd.dma_start(
                                    out=xld_sp[mm],
                                    in_=xld[mm % 2][:, :]), mm)
                            else:
                                stz.inc(gpsimd.dma_start(
                                    out=sz_sp[mm - MT],
                                    in_=szb[mm % 2][:, :]), mm - MT)
                        stx.wait_all(gpsimd, MT)
                        stz.wait_all(gpsimd, MT)
                        std2.wait_all(gpsimd, RTD)

                    @block.tensor
                    def _(tensor):
                        gchunk = 0
                        for m in range(2 * MT):
                            tensor.wait_ge(gp, m + 1)
                            for c in range(chunks_of(m)):
                                if gchunk >= 2:
                                    tensor.wait_ge(ac,
                                                   evac_done[gchunk - 2])
                                cw = CW_L if m < MT else CW_LO
                                coloff = (c * CW_L if m < MT
                                          else LOWN + c * CW_LO)
                                bank = gchunk % 2
                                mi = None
                                for k in range(KT):
                                    mi = tensor.matmul(
                                        ps_in[:, bank * 512:
                                              bank * 512 + cw],
                                        wb[m % 2][:, k * 128:(k + 1) * 128],
                                        xnT_v[:, k, coloff:coloff + cw],
                                        start=(k == 0), stop=(k == KT - 1))
                                mi.then_inc(te, 1)
                                gchunk += 1

                    @block.scalar
                    def _(scalar):
                        gchunk = 0
                        for m in range(2 * MT):
                            if m < MT:
                                for c in range(LC):
                                    scalar.wait_ge(te, gchunk + 1)
                                    if m >= 2 and c == 0:
                                        scalar.wait_ge(ve, m - 1)
                                    bank = gchunk % 2
                                    scalar.activation(
                                        xb[m % 2][:, DC - 1 + c * CW_L:
                                                  DC - 1 + (c + 1) * CW_L],
                                        ps_in[:, bank * 512:
                                              bank * 512 + CW_L],
                                        AF.Copy)
                                    scalar.drain().then_inc(ac, 1)
                                    gchunk += 1
                                scalar.wait_ge(ve, m + 1)
                                if m >= 2:
                                    stx.wait(scalar, m - 2)
                                scalar.activation(
                                    xld[m % 2][:, :], cvac[:, :], AF.Silu,
                                    bias=convb_s[:, m:m + 1])
                                scalar.drain().then_inc(ac, 1)
                            else:
                                for c in range(LCO):
                                    scalar.wait_ge(te, gchunk + 1)
                                    if m >= MT + 2 and c == 0:
                                        stz.wait(scalar, m - MT - 2)
                                    bank = gchunk % 2
                                    scalar.activation(
                                        szb[m % 2][:, c * CW_LO:
                                                  (c + 1) * CW_LO],
                                        ps_in[:, bank * 512:
                                              bank * 512 + CW_LO],
                                        AF.Silu)
                                    scalar.drain().then_inc(ac, 1)
                                    gchunk += 1

                    @block.vector
                    def _(vector):
                        def conv(m):
                            vector.wait_ge(ac, ac_aft(m - 1) + LC)
                            if m >= 1:
                                vector.wait_ge(ac, ac_aft(m - 1))
                            b = xb[m % 2]
                            if m < 2:
                                vector.memset(b[:, 0:DC - 1], 0.0)
                                vector.drain()
                            vector.tensor_scalar_mul(
                                cvac[:, :], b[:, DC - 1:DC - 1 + L],
                                convw_s[:, m * DC + DC - 1:m * DC + DC])
                            vector.drain()
                            for kk in range(DC - 1):
                                vector.scalar_tensor_tensor(
                                    cvac[:, :], b[:, kk:kk + L],
                                    convw_s[:, m * DC + kk:
                                            m * DC + kk + 1],
                                    cvac[:, :], OP.mult, OP.add)
                                vector.drain()
                            vector.drain().then_inc(ve, 1)

                        for m in range(2 * MT):
                            lda.wait(vector, m)
                            if m >= 2:
                                vector.wait_ge(te, te_aft(m - 2))
                            vector.tensor_copy(wb[m % 2][:, :],
                                               wf[m % 2][:, :])
                            vector.drain().then_inc(gp, 1)
                            if m < RTD:
                                ldd2.wait(vector, m)
                                vector.tensor_scalar(pb_d[:, :],
                                                     cdb[m % 2][:, :],
                                                     swcol_d[:, 0:1], 0.5,
                                                     OP.mult, OP.is_gt)
                                vector.tensor_scalar(qb_d[:, :],
                                                     cdb[m % 2][:, :],
                                                     swcol_d[:, 0:1], -0.5,
                                                     OP.mult, OP.is_lt)
                                vector.drain()
                                if m >= 2:
                                    std2.wait(vector, m - 2)
                                vector.tensor_sub(tb_d[m % 2][:, :],
                                                  pb_d[:, :], qb_d[:, :])
                                vector.drain().then_inc(vd, 1)
                            if 1 <= m <= MT:
                                conv(m - 1)

        sbst = contextlib.ExitStack()
        sbb = lambda n, s, d=F32: sbst.enter_context(nc.sbuf_tensor(n, s, d))
        wdt_b = sbb("wdt_b", [128, DI], BF16)
        dtloT = sbb("dtloT", [128, L], BF16)
        sbc = contextlib.ExitStack()
        bc_f = sbc.enter_context(nc.sbuf_tensor("bc_f", [2 * DS, L], F32))
        bc_b = sbc.enter_context(nc.sbuf_tensor("bc_b", [2 * DS, L], BF16))
        with sbst:
            with contextlib.ExitStack() as sp:
                psb = lambda n, s, d=F32: sp.enter_context(
                    nc.sbuf_tensor(n, s, d))
                pps = lambda n, s, d=F32: sp.enter_context(
                    nc.psum_tensor(n, s, d))
                wxf = psb("p2b_wxf", [128, MT * WXC], F32)
                wxb = psb("p2b_wxb", [128, MT * WXC], BF16)
                wdtf2 = psb("p2b_wdtf", [128, DI], F32)
                xlb = [psb(f"p2b_xl{i}", [128, L], BF16) for i in range(2)]
                ps_dt = pps("p2b_psdt", [128, L], F32)
                ps_bc2 = pps("p2b_psbc", [2 * DS, L], F32)
                ldw = sem("p2b_ldw")
                lda = DStream(sem, "p2b_ld")
                gp = sem("p2b_gp")
                te = sem("p2b_te")
                ve = sem("p2b_ve")
                ac = sem("p2b_ac")
                with nc.Block("p2b") as block:
                    @block.sync
                    def _(sync):
                        sync.dma_start(out=wxf[:, :], in_=w_x4
                                       ).then_inc(ldw, 16)
                        sync.dma_start(out=wdtf2[:, :], in_=w_dtT
                                       ).then_inc(ldw, 16)
                        for t in range(MT):
                            if t >= 2:
                                sync.wait_ge(te, t - 1)
                            lda.inc(sync.dma_start(out=xlb[t % 2][:, :],
                                                   in_=xld_sp[t]), t)

                    @block.gpsimd
                    def _(gpsimd):
                        gpsimd.wait_ge(ldw, 32)
                        gpsimd.tensor_copy(wxb[:, :], wxf[:, :])
                        gpsimd.drain().then_inc(gp, 1)

                    @block.tensor
                    def _(tensor):
                        tensor.wait_ge(gp, 1)
                        for t in range(MT):
                            lda.wait(tensor, t)
                            mi = None
                            for c in range(LC):
                                tensor.matmul(
                                    ps_dt[:, c * CW_L:(c + 1) * CW_L],
                                    wxb[:, t * WXC:t * WXC + 128],
                                    xlb[t % 2][:, c * CW_L:(c + 1) * CW_L],
                                    start=(t == 0), stop=(t == MT - 1))
                                mi = tensor.matmul(
                                    ps_bc2[0:2 * DS,
                                           c * CW_L:(c + 1) * CW_L],
                                    wxb[:, t * WXC + 128:t * WXC + WXC],
                                    xlb[t % 2][:, c * CW_L:(c + 1) * CW_L],
                                    start=(t == 0), stop=(t == MT - 1))
                            mi.then_inc(te, 1)

                    @block.scalar
                    def _(scalar):
                        scalar.wait_ge(ldw, 32)
                        scalar.activation(wdt_b[:, :], wdtf2[:, :], AF.Copy)
                        scalar.drain()
                        scalar.wait_ge(te, MT)
                        scalar.activation(dtloT[:, :], ps_dt[:, :], AF.Copy)
                        scalar.drain().then_inc(ac, 1)

                    @block.vector
                    def _(vector):
                        vector.wait_ge(te, MT)
                        vector.tensor_mul(bc_f[:, :], ps_bc2[0:2 * DS, :],
                                          mask32[:, :])
                        vector.drain()
                        vector.tensor_copy(bc_b[:, :], bc_f[:, :])
                        vector.drain().then_inc(ve, 1)

            with contextlib.ExitStack() as sp:
                psb = lambda n, s, d=F32: sp.enter_context(
                    nc.sbuf_tensor(n, s, d))
                pps = lambda n, s, d=F32: sp.enter_context(
                    nc.psum_tensor(n, s, d))
                ps_br = pps("p2c_psbr", [128, 2 * 512], F32)
                bcrow = [psb(f"p2c_row{i}", [1, L], BF16) for i in range(2)]
                bst = [psb(f"p2c_bst{i}", [128, 512], BF16) for i in range(2)]
                te = sem("p2c_te")
                ac = sem("p2c_ac")
                gs = DStream(sem, "p2c_gs")
                ldr = DStream(sem, "p2c_ldr")
                rows = []
                for n in range(DS):
                    rows.append((n, 0, L, True, n))
                    rows.append((DS + n, LOWN, LOWN, False, n))
                nch_of = lambda ri: LC if rows[ri][3] else LCO

                def te_aft(ri):
                    return 0 if ri < 0 else sum(nch_of(i)
                                                for i in range(ri + 1))

                with nc.Block("p2c") as block:
                    @block.sync
                    def _(sync):
                        for ri, (p, off, ln, isb, n) in enumerate(rows):
                            if ri >= 2:
                                sync.wait_ge(te, te_aft(ri - 2))
                            ldr.inc(sync.dma_start(
                                out=bcrow[ri % 2][0:1, 0:ln],
                                in_=bc_b[p:p + 1, off:off + ln]), ri)

                    @block.tensor
                    def _(tensor):
                        ci = 0
                        for ri, (p, off, ln, isb, n) in enumerate(rows):
                            ldr.wait(tensor, ri)
                            cwr = CW_L if rows[ri][3] else CW_LO
                            for c in range(nch_of(ri)):
                                if ci >= 2:
                                    tensor.wait_ge(ac, ci - 1)
                                tensor.matmul(
                                    ps_br[:, (ci % 2) * 512:
                                          (ci % 2) * 512 + cwr],
                                    ones_b[0:1, :],
                                    bcrow[ri % 2][0:1,
                                                  c * cwr:(c + 1) * cwr],
                                    start=True, stop=True).then_inc(te, 1)
                                ci += 1

                    @block.scalar
                    def _(scalar):
                        ci = 0
                        for ri, (p, off, ln, isb, n) in enumerate(rows):
                            cwr = CW_L if isb else CW_LO
                            for c in range(nch_of(ri)):
                                scalar.wait_ge(te, ci + 1)
                                if ci >= 2:
                                    gs.wait(scalar, ci - 2)
                                scalar.activation(
                                    bst[ci % 2][:, 0:cwr],
                                    ps_br[:, (ci % 2) * 512:
                                          (ci % 2) * 512 + cwr], AF.Copy)
                                scalar.drain().then_inc(ac, 1)
                                ci += 1

                    @block.gpsimd
                    def _(gpsimd):
                        ci = 0
                        for ri, (p, off, ln, isb, n) in enumerate(rows):
                            cwr = CW_L if isb else CW_LO
                            for c in range(nch_of(ri)):
                                gpsimd.wait_ge(ac, ci + 1)
                                if isb:
                                    dst = brep_sp[n][:, c * cwr:
                                                     (c + 1) * cwr]
                                else:
                                    dst = crep_sp[n][:, c * cwr:
                                                     (c + 1) * cwr]
                                gs.inc(gpsimd.dma_start(
                                    out=dst, in_=bst[ci % 2][:, 0:cwr]), ci)
                                ci += 1
                        gs.wait_all(gpsimd, ci)

            sbc.close()

            with nc.Block("p4s") as block:
                @block.vector
                def _(vector):
                    vector.tensor_scalar_mul(gg_s[:, :], teff_s[:, :],
                                             mxg_c[:, 0:1])
                    vector.tensor_scalar_mul(kk_s[:, :], teff_s[:, :],
                                             mxu_c[:, 0:1])
                    vector.drain()
                    vector.tensor_scalar_mul(gg_s[:, :], gg_s[:, :],
                                             1.0 / 128.0)
                    vector.tensor_scalar_mul(kk_s[:, :], kk_s[:, :],
                                             1.0 / (128.0 * math.sqrt(FF)))
                    vector.drain()

            with contextlib.ExitStack() as sp:
                psb = lambda n, s, d=F32: sp.enter_context(
                    nc.sbuf_tensor(n, s, d))
                pps = lambda n, s, d=F32: sp.enter_context(
                    nc.psum_tensor(n, s, d))
                NP = DS // 2
                brg = [psb(f"p3_brg{i}", [128, 2 * L], BF16)
                       for i in range(2)]
                crg = [psb(f"p3_crg{i}", [128, 2 * LOWN], BF16)
                       for i in range(2)]
                xls = [psb(f"p3_xl{i}", [128, L], BF16) for i in range(2)]
                q1all = psb("p3_q1all", [128, TTO * KT * 128], BF16)
                tgb = psb("p3_tg", [128, KT * CW_FF], BF16)
                tub = psb("p3_tu", [128, KT * CW_FF], BF16)
                mst = [psb(f"p3_mst{i}", [128, CW_FF], BF16)
                       for i in range(2)]
                szs = [psb(f"p3_sz{i}", [128, LOWN], BF16)
                       for i in range(2)]
                ebuf = psb("p3_e", [128, L], F32)
                dab = [psb(f"p3_da{i}", [128, 2 * L], F32) for i in range(2)]
                pb = psb("p3_p", [128, L], BF16)
                ub = psb("p3_u", [128, 2 * L], BF16)
                hb = psb("p3_h", [128, 2 * L], BF16)
                tmpb = [psb(f"p3_tmp{i}", [128, 2 * LOWN], BF16)
                        for i in range(2)]
                yac2 = psb("p3_yac2", [128, 2 * LOWN], F32)
                y2b = [psb(f"p3_y2{i}", [128, LOWN], BF16) for i in range(2)]
                ps_dl = pps("p3_psdl", [128, L], F32)
                ps_m = [pps(f"p3_psm{i}", [128, CW_FF], F32)
                        for i in range(4)]
                lda = DStream(sem, "p3_ld")
                ldz = DStream(sem, "p3_ldz")
                ldb = DStream(sem, "p3_ldb")
                mq = sem("p3_mq")
                mlg = sem("p3_mlg")
                mlu = sem("p3_mlu")
                tem = sem("p3_tem")
                acm = sem("p3_acm")
                gsm = DStream(sem, "p3_gsm")
                NTILE = 2 * TTO
                NTT = FFC * NTILE
                EV_T = max(1, NTT // MT)
                te = sem("p3_te")
                ac = sem("p3_ac")
                ve = sem("p3_ve")
                stt_ = DStream(sem, "p3_st")

                def ac_t(t):
                    return 0 if t < 0 else (2 + NP) * (t + 1)

                def ve_t(t):
                    return 0 if t < 0 else (3 + NP) * (t + 1)

                ub_v = ub.ap().rearrange("p (m l) -> p m l", m=2)
                hb_v = hb.ap().rearrange("p (m l) -> p m l", m=2)

                with nc.Block("p3") as block:
                    @block.sync
                    def _(sync):
                        sync.dma_start(
                            out=q1all[:, :].rearrange(
                                "p (t c) -> p t c", t=TTO),
                            in_=q1t_sp.rearrange("t p c -> p t c")
                        ).then_inc(mq, 16)
                        for t in range(MT):
                            if t % 2 == 0 and t // 2 < FFC:
                                cc = t // 2
                                if cc >= 1:
                                    sync.wait_ge(tem, cc * NTILE - TTO)
                                sync.dma_start(
                                    out=tgb[:, :],
                                    in_=tern_g[cc].rearrange(
                                        "p k f -> p (k f)")).then_inc(mlg, 16)
                                if cc >= 1:
                                    sync.wait_ge(tem, cc * NTILE)
                                sync.dma_start(
                                    out=tub[:, :],
                                    in_=tern_u[cc].rearrange(
                                        "p k f -> p (k f)")).then_inc(mlu, 16)
                            if t >= 2:
                                sync.wait_ge(ve, ve_t(t - 2))
                            lda.inc(sync.dma_start(out=xls[t % 2][:, :],
                                                   in_=xld_sp[t]), t)
                            ldz.inc(sync.dma_start(out=szs[t % 2][:, :],
                                                   in_=sz_sp[t]), t)
                            def spill(j):
                                midx = t * EV_T + j
                                if midx >= NTT:
                                    return
                                cc2, rem = divmod(midx, NTILE)
                                gu, lt2 = divmod(rem, TTO)
                                sync.wait_ge(acm, midx + 1)
                                dstt = (sg_sp if gu == 0 else up_sp)
                                gsm.inc(sync.dma_start(
                                    out=dstt[lt2][:, cc2 * CW_FF:
                                                  (cc2 + 1) * CW_FF],
                                    in_=mst[midx % 2][:, :]), midx)

                            for g in range(NP):
                                gb = t * NP + g
                                if gb >= 2:
                                    tprev, gprev = divmod(gb - 2, NP)
                                    sync.wait_ge(
                                        ve, ve_t(tprev - 1) + 1
                                        + (gprev + 1))
                                sync.dma_start(
                                    out=brg[gb % 2][:, :].rearrange(
                                        "p (m l) -> p m l", m=2),
                                    in_=brep_sp[2 * g:2 * g + 2].rearrange(
                                        "m p l -> p m l")
                                ).then_inc(ldb.sems[gb % 2], 16)
                                sync.dma_start(
                                    out=crg[gb % 2][:, :].rearrange(
                                        "p (m l) -> p m l", m=2),
                                    in_=crep_sp[2 * g:2 * g + 2].rearrange(
                                        "m p l -> p m l")
                                ).then_inc(ldb.sems[gb % 2], 16)
                                if g < EV_T:
                                    spill(g)
                            for j in range(NP, EV_T):
                                spill(j)
                        gsm.wait_all(sync, NTT)

                    @block.tensor
                    def _(tensor):
                        for t in range(MT):
                            if t >= 1:
                                tensor.wait_ge(ac, ac_t(t - 2) + 1)
                            mi = None
                            for c in range(LC):
                                mi = tensor.matmul(
                                    ps_dl[:, c * CW_L:(c + 1) * CW_L],
                                    wdt_b[:, t * 128:(t + 1) * 128],
                                    dtloT[:, c * CW_L:(c + 1) * CW_L],
                                    start=True, stop=True)
                            mi.then_inc(te, 1)
                            if t == 0:
                                tensor.wait_ge(mq, 16)
                            for j in range(EV_T):
                                midx = t * EV_T + j
                                if midx >= NTT:
                                    continue
                                cc2, rem = divmod(midx, NTILE)
                                gu, lt2 = divmod(rem, TTO)
                                if j == 0 or lt2 == 0:
                                    tensor.wait_ge(
                                        mlg if gu == 0 else mlu,
                                        16 * (cc2 + 1))
                                src = tgb if gu == 0 else tub
                                if midx >= 4:
                                    tensor.wait_ge(acm, midx - 3)
                                mi = None
                                for k in range(KT):
                                    mi = tensor.matmul(
                                        ps_m[midx % 4][:, :],
                                        q1all[:, (lt2 * KT + k) * 128:
                                              (lt2 * KT + k + 1) * 128],
                                        src[:, k * CW_FF:
                                            (k + 1) * CW_FF],
                                        start=(k == 0),
                                        stop=(k == KT - 1))
                                mi.then_inc(tem, 1)

                    @block.scalar
                    def _(scalar):
                        for t in range(MT):
                            scalar.wait_ge(te, t + 1)
                            if t >= 1:
                                scalar.wait_ge(ve, ve_t(t - 2) + 1)
                            scalar.activation(ebuf[:, :], ps_dl[:, :],
                                              AF.Exp,
                                              bias=dtb_s[:, t:t + 1])
                            scalar.drain().then_inc(ac, 1)
                            scalar.activation(ebuf[:, :], ebuf[:, :], AF.Ln,
                                              bias=1.0)
                            scalar.drain().then_inc(ac, 1)
                            for p in range(NP):
                                if p >= 2:
                                    scalar.wait_ge(ve, ve_t(t - 1) + p)
                                elif t >= 1:
                                    scalar.wait_ge(ve, ve_t(t - 2) + NP + p)
                                for h in range(2):
                                    n = 2 * p + h
                                    scalar.activation(
                                        dab[p % 2][:, h * L:(h + 1) * L],
                                        ebuf[:, :], AF.Exp,
                                        scale=nega[:, t * DS + n:
                                                   t * DS + n + 1])
                                scalar.drain().then_inc(ac, 1)
                                for j2 in range(EV_T // NP if EV_T >= NP
                                                else 1):
                                    midx = t * EV_T + p * (EV_T // NP) + j2 \
                                        if EV_T >= NP else t * EV_T + p
                                    if EV_T < NP and p >= EV_T:
                                        continue
                                    if midx >= NTT or midx >= (t + 1) * EV_T:
                                        continue
                                    cc2, rem = divmod(midx, NTILE)
                                    gu, lt2 = divmod(rem, TTO)
                                    scalar.wait_ge(tem, midx + 1)
                                    if midx >= 2:
                                        gsm.wait(scalar, midx - 2)
                                    if gu == 0:
                                        scalar.activation(
                                            mst[midx % 2][:, :],
                                            ps_m[midx % 4][:, :], AF.Copy,
                                            scale=gg_s[:, lt2:lt2 + 1])
                                    else:
                                        scalar.activation(
                                            mst[midx % 2][:, :],
                                            ps_m[midx % 4][:, :], AF.Copy)
                                    scalar.drain().then_inc(acm, 1)

                    @block.vector
                    def _(vector):
                        for t in range(MT):
                            lda.wait(vector, t)
                            vector.wait_ge(ac, ac_t(t - 1) + 2)
                            vector.tensor_mul(pb[:, :], ebuf[:, :],
                                              xls[t % 2][:, :])
                            vector.drain().then_inc(ve, 1)
                            for p in range(NP):
                                vector.wait_ge(ac, ac_t(t - 1) + 2 + p + 1)
                                vector.memset(dab[p % 2][:, L:L + 1], 0.0)
                                gb = t * NP + p
                                vector.wait_ge(ldb.sems[gb % 2],
                                               32 * (gb // 2 + 1))
                                for h in range(2):
                                    vector.tensor_mul(
                                        ub[:, h * L:(h + 1) * L], pb[:, :],
                                        brg[gb % 2][:, h * L:(h + 1) * L])
                                vector.drain()
                                vector.tensor_tensor_scan(
                                    hb[:, :], dab[p % 2][:, :], ub[:, :],
                                    0.0, OP.mult, OP.add)
                                vector.drain()
                                if p == 0:
                                    for h in range(2):
                                        vector.tensor_mul(
                                            yac2[:, h * LOWN:
                                                 (h + 1) * LOWN],
                                            hb[:, h * L + LOWN:(h + 1) * L],
                                            crg[gb % 2][
                                                :, h * LOWN:(h + 1) * LOWN])
                                    vector.drain().then_inc(ve, 1)
                                    continue
                                for h in range(2):
                                    vector.tensor_mul(
                                        tmpb[p % 2][:, h * LOWN:
                                                    (h + 1) * LOWN],
                                        hb[:, h * L + LOWN:(h + 1) * L],
                                        crg[gb % 2][
                                            :, h * LOWN:(h + 1) * LOWN])
                                vector.drain()
                                vector.tensor_add(yac2[:, :], yac2[:, :],
                                                  tmpb[p % 2][:, :])
                                vector.drain().then_inc(ve, 1)
                            ldz.wait(vector, t)
                            vector.scalar_tensor_tensor(
                                yac2[:, 0:LOWN], xls[t % 2][:, LOWN:L],
                                dvec_s[:, t:t + 1], yac2[:, 0:LOWN],
                                OP.mult, OP.add)
                            vector.drain()
                            vector.tensor_add(yac2[:, 0:LOWN],
                                              yac2[:, 0:LOWN],
                                              yac2[:, LOWN:2 * LOWN])
                            vector.drain().then_inc(ve, 1)
                            if t >= 2:
                                stt_.wait(vector, t - 2)
                            vector.tensor_mul(y2b[t % 2][:, :],
                                              yac2[:, 0:LOWN],
                                              szs[t % 2][:, :])
                            vector.drain().then_inc(ve, 1)

                    @block.gpsimd
                    def _(gpsimd):
                        for t in range(MT):
                            gpsimd.wait_ge(ve, ve_t(t))
                            ins = gpsimd.dma_start(
                                out=y2_sp[t], in_=y2b[t % 2][:, :])
                            stt_.inc(ins, t)
                        stt_.wait_all(gpsimd, MT)

        s_early.close()

        with contextlib.ExitStack() as sp:
            psb = lambda n, s, d=F32: sp.enter_context(nc.sbuf_tensor(n, s, d))
            pps = lambda n, s, d=F32: sp.enter_context(nc.psum_tensor(n, s, d))
            NDQ = DM // CW_DM
            y2all = psb("p3b_y2all", [128, MT * LOWN], BF16)
            wos = [psb(f"p3b_wo{i}", [128, CW_DM], BF16) for i in range(3)]
            xrs = [psb(f"p3b_xr{i}", [128, CW_DM], F32) for i in range(2)]
            omb = [psb(f"p3b_om{i}", [128, CW_DM], F32) for i in range(2)]
            ps8 = pps("p3b_ps8", [128, TTO * 512], F32)
            ldy = sem("p3b_ldy")
            ldw = DStream(sem, "p3b_ldw", 3)
            ldx = DStream(sem, "p3b_ldx")
            te = sem("p3b_te")
            ve = sem("p3b_ve")
            stt_ = DStream(sem, "p3b_st")
            with nc.Block("p3b") as block:
                @block.sync
                def _(sync):
                    sync.dma_start(
                        out=y2all[:, :].rearrange("p (t l) -> p t l", t=MT),
                        in_=y2_sp.rearrange("t p l -> p t l")
                    ).then_inc(ldy, 16)
                    for dq in range(NDQ):
                        for k in range(MT):
                            gi = dq * MT + k
                            if gi >= 3:
                                sync.wait_ge(te, gi - 2)
                            ldw.inc(sync.dma_start(
                                out=wos[gi % 3][:, :],
                                in_=wo_bf[k * 128:(k + 1) * 128,
                                          dq * CW_DM:(dq + 1) * CW_DM]), gi)
                        for lt in range(TTO):
                            gx = dq * TTO + lt
                            if gx >= 2:
                                sync.wait_ge(ve, gx - 1)
                            ldx.inc(sync.dma_start(
                                out=xrs[gx % 2][:, :],
                                in_=x_in[LOWN + lt * 128:
                                         LOWN + (lt + 1) * 128,
                                         dq * CW_DM:(dq + 1) * CW_DM]), gx)

                @block.tensor
                def _(tensor):
                    tensor.wait_ge(ldy, 16)
                    for dq in range(NDQ):
                        if dq >= 1:
                            tensor.wait_ge(ve, dq * TTO)
                        for k in range(MT):
                            gi = dq * MT + k
                            ldw.wait(tensor, gi)
                            mi = None
                            for lt in range(TTO):
                                mi = tensor.matmul(
                                    ps8[:, lt * 512:lt * 512 + CW_DM],
                                    y2all[:, k * LOWN + lt * 128:
                                          k * LOWN + (lt + 1) * 128],
                                    wos[gi % 3][:, :],
                                    start=(k == 0), stop=(k == MT - 1))
                            mi.then_inc(te, 1)

                @block.vector
                def _(vector):
                    for dq in range(NDQ):
                        vector.wait_ge(te, (dq + 1) * MT)
                        for lt in range(TTO):
                            gx = dq * TTO + lt
                            ldx.wait(vector, gx)
                            if gx >= 2:
                                stt_.wait(vector, gx - 2)
                            vector.tensor_add(
                                omb[gx % 2][:, :],
                                ps8[:, lt * 512:lt * 512 + CW_DM],
                                xrs[gx % 2][:, :])
                            vector.drain().then_inc(ve, 1)

                @block.gpsimd
                def _(gpsimd):
                    for dq in range(NDQ):
                        for lt in range(TTO):
                            gx = dq * TTO + lt
                            gpsimd.wait_ge(ve, gx + 1)
                            stt_.inc(gpsimd.dma_start(
                                out=outm_sp[lt][:, dq * CW_DM:
                                                (dq + 1) * CW_DM],
                                in_=omb[gx % 2][:, :]), gx)
                    stt_.wait_all(gpsimd, NDQ * TTO)

        with contextlib.ExitStack() as sp:
            psb = lambda n, s, d=F32: sp.enter_context(nc.sbuf_tensor(n, s, d))
            pps = lambda n, s, d=F32: sp.enter_context(nc.psum_tensor(n, s, d))
            dsc_s = psb("dsc_s", [128, TTO], F32)
            sgl = [psb(f"p4_sg{i}", [128, FF], BF16) for i in range(2)]
            upl = [psb(f"p4_up{i}", [128, FF], BF16) for i in range(2)]
            midb = psb("p4_mid", [128, FF], F32)
            pongb = psb("p4_pong", [128, FF], F32)
            q3b = psb("p4_q3", [128, FF], BF16)
            q3t = psb("p4_q3t", [128, FFT * 128], BF16)
            NDB = 4
            tdb = [psb(f"p4_td{i}", [128, DM], BF16) for i in range(NDB)]
            finb = [psb(f"p4_fin{i}", [128, DM], F32) for i in range(2)]
            QCH = min(512, FF // 4)
            NQ = FF // QCH
            qscr = psb("p4_qscr", [128, 3 * QCH], F32)
            qib_s = psb("p4_qib", [128, QCH], I32)
            c_am = psb("p4_cam", [128, 1], F32)
            c_vs = psb("p4_cvs", [128, 1], F32)
            ps_d = pps("p4_psd", [128, DM], F32)
            ps_t = pps("p4_pst", [128, 8 * 128], BF16).ap()
            ldg = DStream(sem, "p4_ldg")
            ldu = DStream(sem, "p4_ldu")
            ldd = DStream(sem, "p4_ldd", nbuf=NDB)
            ldo = DStream(sem, "p4_ldo")
            tr = sem("p4_tr")
            ted = sem("p4_ted")
            ve = sem("p4_ve")
            ac = sem("p4_ac")
            asg = sem("p4_asg")
            aq = sem("p4_aq")
            qfull = pongb if NSTG % 2 == 0 else midb
            fwb = midb if NSTG % 2 == 0 else pongb
            stp4 = DStream(sem, "p4_st")

            def ve_mid(lt):
                return 1 if lt == 0 else 4 * lt

            def ve_scales(lt):
                return 2 if lt == 0 else 4 * lt + 1

            def ve_q3(lt):
                return 3 if lt == 0 else 4 * lt + 2

            def ve_fin(lt):
                return 4 * TTO if lt == TTO - 1 else 4 * (lt + 1) + 3

            with nc.Block("p4") as block:
                @block.sync
                def _(sync):
                    for lt in range(TTO):
                        for k in range(FFT):
                            gi2 = lt * FFT + k
                            if gi2 >= NDB:
                                sync.wait_ge(ted, gi2 - NDB + 1)
                            ldd.inc(sync.dma_start(
                                out=tdb[gi2 % NDB][:, :],
                                in_=tern_d[k * 128:(k + 1) * 128, :]), gi2)

                @block.tensor
                def _(tensor):
                    for lt in range(TTO):
                        tensor.wait_ge(ve, ve_q3(lt))
                        for g in range(FFT8):
                            nk = min(8, FFT - g * 8)
                            if g >= 1:
                                tensor.wait_ge(ac, lt * FFT8 + g)
                            mi = None
                            for j in range(nk):
                                k = g * 8 + j
                                mi = tensor.transpose(
                                    ps_t[:, j * 128:(j + 1) * 128],
                                    q3b[:, k * 128:(k + 1) * 128],
                                    ident_s[:, :])
                            mi.then_inc(tr, 1)
                        tensor.wait_ge(ac, (lt + 1) * FFT8)
                        if lt >= 1:
                            tensor.wait_ge(ve, ve_fin(lt - 1))
                        for k in range(FFT):
                            gi2 = lt * FFT + k
                            ldd.wait(tensor, gi2)
                            mi = None
                            for c in range(DMC):
                                mi = tensor.matmul(
                                    ps_d[:, c * CW_DM:(c + 1) * CW_DM],
                                    q3t[:, k * 128:(k + 1) * 128],
                                    tdb[gi2 % NDB][:, c * CW_DM:
                                                   (c + 1) * CW_DM],
                                    start=(k == 0), stop=(k == FFT - 1))
                            mi.then_inc(ted, 1)

                @block.scalar
                def _(scalar):
                    def silu(lt):
                        ldg.wait(scalar, lt)
                        scalar.activation(sgl[lt % 2][:, :],
                                          sgl[lt % 2][:, :], AF.Silu)
                        scalar.drain().then_inc(asg, 1)

                    for lt in range(min(2, TTO)):
                        silu(lt)
                    for lt in range(TTO):
                        scalar.wait_ge(ve, ve_scales(lt))
                        scalar.activation(qfull[:, :], fwb[:, :], AF.Copy,
                                          scale=c_vs[:, 0:1])
                        scalar.drain().then_inc(aq, 1)
                        if lt >= 1:
                            scalar.wait_ge(ted, lt * FFT)
                        for g in range(FFT8):
                            nk = min(8, FFT - g * 8)
                            scalar.wait_ge(tr, lt * FFT8 + g + 1)
                            scalar.activation(
                                q3t[:, g * 8 * 128:(g * 8 + nk) * 128],
                                ps_t[:, 0:nk * 128], AF.Copy)
                            scalar.drain().then_inc(ac, 1)
                        if lt + 2 < TTO:
                            silu(lt + 2)

                def fin(vector, lt):
                    vector.wait_ge(ted, (lt + 1) * FFT)
                    ldo.wait(vector, lt)
                    vector.scalar_tensor_tensor(
                        finb[lt % 2][:, :], ps_d[:, :],
                        dsc_s[:, lt:lt + 1], finb[lt % 2][:, :],
                        OP.mult, OP.add)
                    vector.drain().then_inc(ve, 1)

                @block.vector
                def _(vector):
                    for lt in range(TTO):
                        vector.wait_ge(asg, lt + 1)
                        ldu.wait(vector, lt)
                        vector.tensor_mul(midb[:, :], sgl[lt % 2][:, :],
                                          upl[lt % 2][:, :])
                        vector.drain().then_inc(ve, 1)
                        cur, nxt = midb, pongb
                        for st_i in range(NSTG):
                            h = 1 << st_i
                            a = cur[:, :].rearrange(
                                "p (nb two h) -> p nb two h", two=2, h=h)
                            o = nxt[:, :].rearrange(
                                "p (nb two h) -> p nb two h", two=2, h=h)
                            vector.tensor_add(o[:, :, 0, :], a[:, :, 0, :],
                                              a[:, :, 1, :])
                            vector.tensor_sub(o[:, :, 1, :], a[:, :, 0, :],
                                              a[:, :, 1, :])
                            vector.drain()
                            cur, nxt = nxt, cur
                        fw = cur
                        vector.reduce_max(c_am[:, :], fw[:, :], AX.X,
                                          apply_absolute_value=True)
                        vector.drain()
                        vector.tensor_mul(c_am[:, :], c_am[:, :],
                                          kk_s[:, lt:lt + 1])
                        vector.drain()
                        vector.tensor_scalar_max(c_am[:, :], c_am[:, :],
                                                 1e-5)
                        vector.drain()
                        vector.tensor_mul(dsc_s[:, lt:lt + 1], c_am[:, :],
                                          mxd_c[:, 0:1])
                        vector.drain()
                        vector.tensor_scalar_mul(dsc_s[:, lt:lt + 1],
                                                 dsc_s[:, lt:lt + 1],
                                                 1.0 / 128.0)
                        vector.reciprocal(c_vs[:, :], c_am[:, :])
                        vector.drain()
                        vector.tensor_scalar_mul(c_vs[:, :], c_vs[:, :],
                                                 128.0)
                        vector.drain()
                        vector.tensor_mul(c_vs[:, :], c_vs[:, :],
                                          kk_s[:, lt:lt + 1])
                        vector.drain().then_inc(ve, 1)
                        qib = qib_s[:, :]
                        q0b = qscr[:, 1 * QCH:2 * QCH]
                        qrb = qscr[:, 2 * QCH:3 * QCH]
                        if lt >= 1:
                            vector.wait_ge(tr, lt * FFT8)
                        vector.wait_ge(aq, lt + 1)
                        for qc in range(NQ):
                            sl = slice(qc * QCH, (qc + 1) * QCH)
                            qvb = qfull[:, sl]
                            vector.tensor_copy(qib, qvb)
                            vector.drain()
                            vector.tensor_copy(q0b, qib)
                            vector.drain()
                            vector.tensor_sub(qrb, qvb, q0b)
                            vector.drain()
                            vector.scalar_tensor_tensor(
                                q0b, qrb, 0.5, q0b, OP.is_ge, OP.add)
                            vector.drain()
                            vector.scalar_tensor_tensor(
                                q0b, qrb, -0.5, q0b, OP.is_gt, OP.add)
                            vector.drain()
                            vector.tensor_scalar(q3b[:, sl], q0b,
                                                 -1.0, 127.0,
                                                 OP.add, OP.min)
                            vector.drain()
                        vector.drain().then_inc(ve, 1)
                        if lt >= 1:
                            fin(vector, lt - 1)
                    fin(vector, TTO - 1)

                @block.gpsimd
                def _(gpsimd):
                    for lt in range(min(2, TTO)):
                        ldg.inc(gpsimd.dma_start(out=sgl[lt % 2][:, :],
                                                 in_=sg_sp[lt]), lt)
                        ldu.inc(gpsimd.dma_start(out=upl[lt % 2][:, :],
                                                 in_=up_sp[lt]), lt)
                        ldo.inc(gpsimd.dma_start(out=finb[lt % 2][:, :],
                                                 in_=outm_sp[lt]), lt)
                    for lt in range(TTO):
                        if lt + 2 < TTO:
                            gpsimd.wait_ge(ve, ve_mid(lt))
                            ldg.inc(gpsimd.dma_start(
                                out=sgl[lt % 2][:, :],
                                in_=sg_sp[lt + 2]), lt + 2)
                            ldu.inc(gpsimd.dma_start(
                                out=upl[lt % 2][:, :],
                                in_=up_sp[lt + 2]), lt + 2)
                        gpsimd.wait_ge(ve, ve_fin(lt))
                        stp4.inc(gpsimd.dma_start(
                            out=out_t[lt * 128:(lt + 1) * 128, :],
                            in_=finb[lt % 2][:, :]), lt)
                        if lt + 2 < TTO:
                            stp4.wait(gpsimd, lt)
                            ldo.inc(gpsimd.dma_start(
                                out=finb[lt % 2][:, :],
                                in_=outm_sp[lt + 2]), lt + 2)
                    stp4.wait_all(gpsimd, TTO)

    return nc



_BUILD_CACHE = {}


def _get_program(C):
    key = tuple(sorted(C.items()))
    if key not in _BUILD_CACHE:
        _BUILD_CACHE[key] = build_program(C)
    return _BUILD_CACHE[key]


def _hadamard_ok():
    pass


def make_in_maps(C, inputs):
    DM, DI, DS, DC = C["DM"], C["DI"], C["DS"], C["DC"]
    FF, LOWN = C["FF"], C["LOWN"]
    L = 2 * LOWN
    KT = DM // 128
    MT = DI // 128

    f32 = lambda a: np.ascontiguousarray(a, dtype=np.float32)
    x = f32(inputs["L0_attn_hidden_states"])
    B = x.shape[0]
    in_w = f32(inputs["L0_attn_in_proj_w"])
    convw = f32(inputs["L0_attn_conv_w"])
    convb = f32(inputs["L0_attn_conv_b"])
    xw = f32(inputs["L0_attn_x_proj_w"])
    dtw = f32(inputs["L0_attn_dt_proj_w"])
    dtb = f32(inputs["L0_attn_dt_proj_b"])
    alog = f32(inputs["L0_attn_A_log"])
    dv = f32(inputs["L0_attn_D"])
    ow = f32(inputs["L0_attn_out_proj_w"])
    wg = f32(inputs["L0_ffn_w_gate"])
    wu = f32(inputs["L0_ffn_w_up"])
    wd = f32(inputs["L0_ffn_w_down"])

    w_in4 = np.ascontiguousarray(
        in_w.T.reshape(KT, 128, 2 * MT, 128).transpose(2, 1, 0, 3))
    conv_w4 = np.ascontiguousarray(
        convw.reshape(MT, 128, DC).transpose(1, 0, 2).reshape(128, MT * DC))
    conv_b4 = np.ascontiguousarray(convb.reshape(MT, 128).T)
    w_x4 = np.ascontiguousarray(
        xw.T.reshape(MT, 128, -1).transpose(1, 0, 2).reshape(128, -1))
    w_dtT = np.ascontiguousarray(dtw.T)
    dt_b4 = np.ascontiguousarray(dtb.reshape(MT, 128).T)
    a_log4 = np.ascontiguousarray(
        alog.reshape(MT, 128, DS).transpose(1, 0, 2).reshape(128, MT * DS))
    d_vec4 = np.ascontiguousarray(dv.reshape(MT, 128).T)
    w_oT = np.ascontiguousarray(ow.T)
    w_gT = np.ascontiguousarray(wg.T)
    w_uT = np.ascontiguousarray(wu.T)
    w_dT = np.ascontiguousarray(wd.T)
    identm = np.eye(128, dtype=ml_dtypes.bfloat16)

    in_maps = []
    for c in range(2 * B):
        b, h = c // 2, c % 2
        if h == 0:
            pre = np.zeros((LOWN, DM), np.float32)
            own = x[b, 0:LOWN]
            mrow = np.concatenate([np.zeros(LOWN, np.float32),
                                   np.ones(LOWN, np.float32)])
        else:
            pre = x[b, 0:LOWN]
            own = x[b, LOWN:L]
            mrow = np.ones(L, np.float32)
        x_core = np.ascontiguousarray(np.concatenate([pre, own], axis=0))
        in_maps.append({
            "x_in": x_core,
            "w_in4": w_in4,
            "conv_w": conv_w4,
            "conv_b": conv_b4,
            "w_x4": w_x4,
            "w_dtT": w_dtT,
            "dt_b": dt_b4,
            "a_log": a_log4,
            "d_vec": d_vec4,
            "w_oT": w_oT,
            "w_gT": w_gT,
            "w_uT": w_uT,
            "w_dT": w_dT,
            "ident": identm,
            "maskrow": mrow.reshape(1, L),
        })
    return in_maps


def kernel(**inputs):
    C = full_cfg()
    nc = _get_program(C)
    in_maps = make_in_maps(C, inputs)
    res = run_bass_kernel_spmd(nc, in_maps, core_ids=list(range(8)))
    B = inputs["L0_attn_hidden_states"].shape[0]
    LOWN = C["LOWN"]
    DM = C["DM"]
    out = np.empty((B, 2 * LOWN, DM), np.float32)
    for c in range(2 * B):
        b, h = c // 2, c % 2
        out[b, h * LOWN:(h + 1) * LOWN] = res.results[c]["out"]
    return out


# version 28
# speedup vs baseline: 1.0124x; 1.0124x over previous
import contextlib
import math

import numpy as np
import ml_dtypes

import concourse.bass as bass
import concourse.mybir as mybir
from concourse.bass_utils import run_bass_kernel_spmd

F32 = mybir.dt.float32
BF16 = mybir.dt.bfloat16
I32 = mybir.dt.int32
AF = mybir.ActivationFunctionType
OP = mybir.AluOpType
AX = mybir.AxisListType


def full_cfg():
    return dict(DM=2048, DI=4096, DS=16, DC=4, FF=8192, LOWN=1024)


def small_cfg():
    return dict(DM=256, DI=512, DS=16, DC=4, FF=1024, LOWN=256)


class DStream:

    def __init__(self, mksem, name, nbuf=2):
        self.sems = [mksem(f"{name}_{i}") for i in range(nbuf)]
        self.n = nbuf

    def inc(self, instr, i):
        instr.then_inc(self.sems[i % self.n], 16)

    def wait(self, eng, i):
        eng.wait_ge(self.sems[i % self.n], 16 * (i // self.n + 1))

    def wait_all(self, eng, total):
        for p in range(self.n):
            cnt = (total - p + self.n - 1) // self.n
            if cnt > 0:
                eng.wait_ge(self.sems[p], 16 * cnt)


def build_program(C):
    DM, DI, DS, DC = C["DM"], C["DI"], C["DS"], C["DC"]
    FF, LOWN = C["FF"], C["LOWN"]
    DTR = 128
    L = 2 * LOWN
    KT = DM // 128
    MT = DI // 128
    FFT = FF // 128
    TT = L // 128
    TTO = LOWN // 128
    CW_L = min(512, L)
    CW_LO = min(512, LOWN)
    CW_DM = min(512, DM)
    CW_FF = min(512, FF)
    LC = L // CW_L
    LCO = LOWN // CW_LO
    DMC = DM // CW_DM
    FFC = FF // CW_FF
    NPRE = TT - TTO
    NSTG = int(math.log2(FF))
    assert 2 ** NSTG == FF
    KT8 = (KT + 7) // 8
    FFT8 = (FFT + 7) // 8
    WXC = DTR + 2 * DS

    nc = bass.Bass("TRN2", target_bir_lowering=False, debug=False)

    din = lambda n, s, d=F32: nc.dram_tensor(n, s, d, kind="ExternalInput").ap()
    x_in = din("x_in", [L, DM])
    w_in4 = din("w_in4", [2 * MT, 128, KT, 128])
    conv_w = din("conv_w", [128, MT * DC])
    conv_b = din("conv_b", [128, MT])
    w_x4 = din("w_x4", [128, MT * WXC])
    w_dtT = din("w_dtT", [DTR, DI])
    dt_b = din("dt_b", [128, MT])
    a_log = din("a_log", [128, MT * DS])
    d_vec = din("d_vec", [128, MT])
    w_oT = din("w_oT", [DI, DM])
    w_gT = din("w_gT", [DM, FF])
    w_uT = din("w_uT", [DM, FF])
    w_dT = din("w_dT", [FF, DM])
    ident = din("ident", [128, 128], BF16)
    maskrow = din("maskrow", [1, L])

    out_t = nc.dram_tensor("out", [LOWN, DM], F32, kind="ExternalOutput").ap()

    dint = lambda n, s, d: nc.dram_tensor(n, s, d, kind="Internal").ap()
    xld_sp = dint("xld_sp", [MT, 128, L], BF16)
    sz_sp = dint("sz_sp", [MT, 128, LOWN], BF16)
    q1t_sp = dint("q1t_sp", [TTO, 128, KT * 128], BF16)
    y2_sp = dint("y2_sp", [MT, 128, LOWN], BF16)
    tern_g = dint("tern_g", [FFC, 128, KT, CW_FF], BF16)
    tern_u = dint("tern_u", [FFC, 128, KT, CW_FF], BF16)
    tern_d = dint("tern_d", [FF, DM], BF16)
    wo_bf = dint("wo_bf", [DI, DM], BF16)
    brep_sp = dint("brep_sp", [DS, 128, L], BF16)
    sg_sp = dint("sg_sp", [TTO, 128, FF], BF16)
    up_sp = dint("up_sp", [TTO, 128, FF], BF16)
    crep_sp = dint("crep_sp", [DS, 128, LOWN], BF16)
    outm_sp = dint("outm_sp", [TTO, 128, DM], F32)

    s0 = contextlib.ExitStack()
    sb0 = lambda n, s, d=F32: s0.enter_context(nc.sbuf_tensor(n, s, d))
    sem = lambda n: s0.enter_context(nc.semaphore(n))

    ident_s = sb0("ident_s", [128, 128], BF16)
    ones_f = sb0("ones_f", [1, 128], F32)
    onescol = sb0("onescol", [128, 1], F32)
    ones_b = sb0("ones_b", [1, 128], BF16)

    convw_s = sb0("convw_s", [128, MT * DC], F32)
    convb_s = sb0("convb_s", [128, MT], F32)
    dtb_s = sb0("dtb_s", [128, MT], F32)
    dvec_s = sb0("dvec_s", [128, MT], F32)

    teff_s = sb0("teff_s", [128, TTO], F32)
    rs_s = sb0("rs_s", [128, TTO], F32)
    mxg_c = sb0("mxg_c", [128, 1], F32)
    mxu_c = sb0("mxu_c", [128, 1], F32)
    mxd_c = sb0("mxd_c", [128, 1], F32)
    swcol_d = sb0("swcol_d", [128, 1], F32)
    one1 = sb0("one1", [1, 1], F32)
    gg_s = sb0("gg_s", [128, TTO], F32)
    kk_s = sb0("kk_s", [128, TTO], F32)
    eps_c = sb0("eps_c", [128, 1], F32)

    s_early = contextlib.ExitStack()
    nega = s_early.enter_context(nc.sbuf_tensor("nega", [128, MT * DS], F32))
    mask32 = s_early.enter_context(
        nc.sbuf_tensor("mask32", [2 * DS, L], BF16))

    with s0:
        with contextlib.ExitStack() as sp:
            psb = lambda n, s, d=F32: sp.enter_context(nc.sbuf_tensor(n, s, d))
            pps = lambda n, s, d=F32: sp.enter_context(nc.psum_tensor(n, s, d))
            mrow_s = psb("mrow_s", [1, L], F32)
            ps_m32 = pps("ps_m32", [2 * DS, L], F32)
            ld = sem("p0_ld")
            ac = sem("p0_ac")
            ve = sem("p0_ve")
            te = sem("p0_te")
            with nc.Block("p0") as block:
                @block.sync
                def _(sync):
                    for dst, src in [
                        (ident_s[:, :], ident),
                        (nega[:, :], a_log),
                        (convw_s[:, :], conv_w),
                        (dtb_s[:, :], dt_b),
                        (convb_s[:, :], conv_b),
                        (dvec_s[:, :], d_vec),
                        (mrow_s[0:1, :], maskrow),
                    ]:
                        sync.dma_start(out=dst, in_=src).then_inc(ld, 16)

                @block.scalar
                def _(scalar):
                    scalar.wait_ge(ld, 7 * 16)
                    scalar.activation(nega[:, :], nega[:, :], AF.Exp)
                    scalar.drain().then_inc(ac, 1)
                    scalar.wait_ge(te, 1)
                    scalar.activation(mask32[:, :], ps_m32[0:2 * DS, :],
                                      AF.Copy)
                    scalar.drain()

                @block.vector
                def _(vector):
                    vector.wait_ge(ld, 7 * 16)
                    vector.memset(ones_f[:, :], 1.0)
                    vector.memset(eps_c[:, :], 1e-5)
                    vector.memset(onescol[:, :], 1.0)
                    vector.memset(ones_b[:, :], 1.0)
                    vector.wait_ge(ac, 1)
                    vector.tensor_scalar_mul(nega[:, :], nega[:, :], -1.0)
                    vector.drain().then_inc(ve, 1)

                @block.tensor
                def _(tensor):
                    tensor.wait_ge(ve, 1)
                    mi = None
                    for c in range(LC):
                        mi = tensor.matmul(
                            ps_m32[0:2 * DS, c * CW_L:(c + 1) * CW_L],
                            ones_f[0:1, 0:2 * DS],
                            mrow_s[0:1, c * CW_L:(c + 1) * CW_L],
                            start=True, stop=True)
                    mi.then_inc(te, 1)

        with contextlib.ExitStack() as sp:
            psb = lambda n, s, d=F32: sp.enter_context(nc.sbuf_tensor(n, s, d))
            pps = lambda n, s, d=F32: sp.enter_context(nc.psum_tensor(n, s, d))
            CB = max(DM, FF)
            wbuf = [psb(f"pw_w{i}", [128, CB], F32) for i in range(2)]
            cbuf = [psb(f"pw_c{i}", [128, CB], F32) for i in range(2)]
            pbuf = psb("pw_p", [128, CB], BF16)
            qbuf = psb("pw_q", [128, CB], BF16)
            tbuf = [psb(f"pw_t{i}", [128, CB], BF16) for i in range(2)]
            JW = min(CB, 2048)
            NJC = (CB + JW - 1) // JW
            junkb = psb("pw_junk", [128, JW], BF16)
            rcol = psb("pw_rcol", [128, NJC], F32)
            racc2 = psb("pw_racc2", [128, 2], F32)
            swcol = psb("pw_swcol", [128, 1], F32)
            ps_tot = pps("pw_pstot", [1, 1], F32)
            ps_bc = pps("pw_psbc", [128, 1], F32)

            lda = DStream(sem, "pw_lda")
            ldb = DStream(sem, "pw_ldb")
            ldo2 = DStream(sem, "pw_ldo2")
            sto = DStream(sem, "pw_sto")
            ac = sem("pw_ac")
            aco = sem("pw_aco")
            ve = sem("pw_ve")
            te = sem("pw_te")
            stp = DStream(sem, "pw_st")

            MA = {"g": (w_gT, DM // 128, FF, mxg_c, 0),
                  "u": (w_uT, DM // 128, FF, mxu_c, 1),
                  "d": (w_dT, FF // 128, DM, mxd_c, 0)}
            MC = {"g": (w_gT, tern_g, DM // 128, FF, FFC),
                  "u": (w_uT, tern_u, DM // 128, FF, FFC),
                  "d": (w_dT, tern_d, FF // 128, DM, 1)}
            RT_O = DI // 128
            pwb = {"a": 0, "v": 0, "t": 0, "l0": 0, "l1": 0,
                   "b0": 0, "b1": 0, "s0": 0, "s1": 0}

            def pw_block(name, mc, ma, wo=False):
                bA, bV, bT = pwb["a"], pwb["v"], pwb["t"]
                bL = [pwb["l0"], pwb["l1"]]
                bB = [pwb["b0"], pwb["b1"]]
                bS = [pwb["s0"], pwb["s1"]]
                RTc = MC[mc][2] if mc else 0
                per = MC[mc][4] if mc else 1
                RTa = MA[ma][1] if ma else 0
                nb2 = 1 if mc else 0

                def ve_tern(r):
                    return (bV + nb2 + 2 * r + 1 if r < RTa
                            else bV + nb2 + 2 * RTa + (r - RTa) + 1)

                def ve_racc(r):
                    return (bV + nb2 + 2 * r + 2 if r < RTc
                            else bV + nb2 + 2 * RTc + (r - RTc) + 1)

                with nc.Block(name) as block:
                    @block.sync
                    def _(sync):
                        for r in range(max(RTc, RTa, RT_O if wo else 0)):
                            if mc and r < RTc:
                                if r >= 2:
                                    sync.wait_ge(ve, ve_tern(r - 2))
                                sync.dma_start(
                                    out=cbuf[r % 2][:, 0:MC[mc][3]],
                                    in_=MC[mc][0][r * 128:(r + 1) * 128, :]
                                ).then_inc(ldb.sems[r % 2], 16)
                            if ma and r < RTa:
                                if r >= 2:
                                    sync.wait_ge(ac, bA + nb2 + r - 1)
                                sync.dma_start(
                                    out=wbuf[r % 2][:, 0:MA[ma][2]],
                                    in_=MA[ma][0][r * 128:(r + 1) * 128, :]
                                ).then_inc(lda.sems[r % 2], 16)
                            if wo and r < RT_O:
                                if r >= 2:
                                    sync.wait_ge(aco, r - 1)
                                sync.dma_start(
                                    out=cbuf[r % 2][:, 0:DM],
                                    in_=w_oT[r * 128:(r + 1) * 128, :]
                                ).then_inc(ldo2.sems[r % 2], 16)

                    @block.tensor
                    def _(tensor):
                        if not mc:
                            return
                        col = MA[mc][4]
                        tensor.wait_ge(ve, bV)
                        tensor.matmul(ps_tot[0:1, 0:1],
                                      racc2[:, col:col + 1],
                                      onescol[:, 0:1],
                                      start=True, stop=True).then_inc(te, 1)
                        tensor.wait_ge(ve, bV + 1)
                        tensor.matmul(ps_bc[:, 0:1], ones_f[0:1, :],
                                      one1[0:1, 0:1], start=True, stop=True
                                      ).then_inc(te, 1)

                    @block.scalar
                    def _(scalar):
                        if mc:
                            scalar.wait_ge(te, bT + 2)
                            scalar.activation(MA[mc][3][:, :],
                                              ps_bc[:, 0:1], AF.Copy)
                            scalar.drain().then_inc(ac, 1)
                        for r in range(max(RTa, RT_O if wo else 0)):
                            if ma and r < RTa:
                                scalar.wait_ge(lda.sems[r % 2],
                                               16 * (bL[r % 2]
                                                     + r // 2 + 1))
                                if r >= 1:
                                    scalar.wait_ge(ve, ve_racc(r - 1))
                                Ca = MA[ma][2]
                                for jc in range(0, Ca, JW):
                                    cw = min(JW, Ca - jc)
                                    if jc > 0:
                                        scalar.drain()
                                    scalar.activation(
                                        junkb[:, 0:cw],
                                        wbuf[r % 2][:, jc:jc + cw],
                                        AF.Abs,
                                        accum_out=rcol[:, jc // JW:
                                                       jc // JW + 1])
                                scalar.drain().then_inc(ac, 1)
                            if wo and r < RT_O:
                                scalar.wait_ge(ldo2.sems[r % 2],
                                               16 * (r // 2 + 1))
                                if r >= 2:
                                    scalar.wait_ge(
                                        sto.sems[(r - 2) % 2],
                                        16 * ((r - 2) // 2 + 1))
                                scalar.activation(tbuf[r % 2][:, 0:DM],
                                                  cbuf[r % 2][:, 0:DM],
                                                  AF.Copy)
                                scalar.drain().then_inc(aco, 1)

                    @block.vector
                    def _(vector):
                        if mc:
                            vector.wait_ge(te, bT + 1)
                            vector.tensor_copy(one1[:, :], ps_tot[0:1, 0:1])
                            vector.drain()
                            vector.tensor_scalar(
                                one1[:, :], one1[:, :],
                                1.0 / (MC[mc][2] * 128 * MC[mc][3]), 1e-5,
                                OP.mult, OP.max)
                            vector.drain().then_inc(ve, 1)
                            vector.wait_ge(ac, bA + 1)
                            vector.reciprocal(swcol[:, :], MA[mc][3][:, :])
                            vector.drain()
                        if ma:
                            vector.memset(racc2[:, MA[ma][4]:
                                                MA[ma][4] + 1], 0.0)
                            vector.drain()
                        for r in range(max(RTc, RTa)):
                            if mc and r < RTc:
                                Cc = MC[mc][3]
                                vector.wait_ge(ldb.sems[r % 2],
                                               16 * (bB[r % 2]
                                                     + r // 2 + 1))
                                w = cbuf[r % 2][:, 0:Cc]
                                vector.tensor_scalar(pbuf[:, 0:Cc], w,
                                                     swcol[:, 0:1], 0.5,
                                                     OP.mult, OP.is_gt)
                                vector.tensor_scalar(qbuf[:, 0:Cc], w,
                                                     swcol[:, 0:1], -0.5,
                                                     OP.mult, OP.is_lt)
                                vector.drain()
                                if r >= 2:
                                    vector.wait_ge(
                                        stp.sems[(r - 2) % 2],
                                        16 * (bS[(r - 2) % 2]
                                              + per * ((r - 2) // 2 + 1)))
                                vector.tensor_sub(tbuf[r % 2][:, 0:Cc],
                                                  pbuf[:, 0:Cc],
                                                  qbuf[:, 0:Cc])
                                vector.drain().then_inc(ve, 1)
                            if ma and r < RTa:
                                vector.wait_ge(ac, bA + nb2 + r + 1)
                                for jc in range((MA[ma][2] + JW - 1) // JW):
                                    vector.tensor_add(
                                        racc2[:, MA[ma][4]:MA[ma][4] + 1],
                                        racc2[:, MA[ma][4]:MA[ma][4] + 1],
                                        rcol[:, jc:jc + 1])
                                    vector.drain()
                                vector.drain().then_inc(ve, 1)

                    @block.gpsimd
                    def _(gpsimd):
                        for r in range(max(RTc, RT_O if wo else 0)):
                            if mc and r < RTc:
                                gpsimd.wait_ge(ve, ve_tern(r))
                                dst = MC[mc][1]
                                if per > 1:
                                    for cc in range(FFC):
                                        gpsimd.dma_start(
                                            out=dst[cc, :, r, :],
                                            in_=tbuf[r % 2][
                                                :, cc * CW_FF:
                                                (cc + 1) * CW_FF]
                                        ).then_inc(stp.sems[r % 2], 16)
                                else:
                                    gpsimd.dma_start(
                                        out=dst[r * 128:(r + 1) * 128, :],
                                        in_=tbuf[r % 2][:, 0:MC[mc][3]]
                                    ).then_inc(stp.sems[r % 2], 16)
                            if wo and r < RT_O:
                                gpsimd.wait_ge(aco, r + 1)
                                gpsimd.dma_start(
                                    out=wo_bf[r * 128:(r + 1) * 128, :],
                                    in_=tbuf[r % 2][:, 0:DM]
                                ).then_inc(sto.sems[r % 2], 16)
                        for p in range(2):
                            if mc:
                                cnt = per * ((RTc - p + 1) // 2)
                                if cnt:
                                    gpsimd.wait_ge(stp.sems[p],
                                                   16 * (bS[p] + cnt))
                            if wo:
                                cnt = (RT_O - p + 1) // 2
                                if cnt:
                                    gpsimd.wait_ge(sto.sems[p], 16 * cnt)

                pwb["a"] += nb2 + RTa
                pwb["v"] += nb2 + RTc + RTa
                pwb["t"] += 2 if mc else 0
                pwb["l0"] += (RTa + 1) // 2
                pwb["l1"] += RTa // 2
                pwb["b0"] += (RTc + 1) // 2
                pwb["b1"] += RTc // 2
                pwb["s0"] += per * ((RTc + 1) // 2)
                pwb["s1"] += per * (RTc // 2)

            pw_block("pw0", None, "g", wo=True)
            pw_block("pw1", "g", "u")
            pw_block("pw2", "u", "d")

            bA3, bV3, bT3 = pwb["a"], pwb["v"], pwb["t"]
            with nc.Block("pw3h") as block:
                @block.tensor
                def _(tensor):
                    tensor.matmul(ps_tot[0:1, 0:1], racc2[:, 0:1],
                                  onescol[:, 0:1], start=True, stop=True
                                  ).then_inc(te, 1)
                    tensor.wait_ge(ve, bV3 + 1)
                    tensor.matmul(ps_bc[:, 0:1], ones_f[0:1, :],
                                  one1[0:1, 0:1], start=True, stop=True
                                  ).then_inc(te, 1)

                @block.scalar
                def _(scalar):
                    scalar.wait_ge(te, bT3 + 2)
                    scalar.activation(mxd_c[:, :], ps_bc[:, 0:1], AF.Copy)
                    scalar.drain().then_inc(ac, 1)

                @block.vector
                def _(vector):
                    vector.wait_ge(te, bT3 + 1)
                    vector.tensor_copy(one1[:, :], ps_tot[0:1, 0:1])
                    vector.drain()
                    vector.tensor_scalar(one1[:, :], one1[:, :],
                                         1.0 / (FF * DM), 1e-5,
                                         OP.mult, OP.max)
                    vector.drain().then_inc(ve, 1)
                    vector.wait_ge(ac, bA3 + 1)
                    vector.reciprocal(swcol_d[:, :], mxd_c[:, :])
                    vector.drain()

        sa = contextlib.ExitStack()
        xnT = sa.enter_context(nc.sbuf_tensor("xnT", [128, KT * L], BF16))
        xnT_v = xnT.ap().rearrange("p (k l) -> p k l", k=KT)
        with sa:
            with contextlib.ExitStack() as sp:
                psb = lambda n, s, d=F32: sp.enter_context(
                    nc.sbuf_tensor(n, s, d))
                pps = lambda n, s, d=F32: sp.enter_context(
                    nc.psum_tensor(n, s, d))
                xs_b = [psb(f"p1_x{i}", [128, DM], F32) for i in range(2)]
                xc_b = psb("p1_xc", [128, DM], F32)
                jnk = psb("p1_junk", [128, DM], BF16)
                c_m = psb("p1_cm", [128, 1], F32)
                c_den = psb("p1_cden", [128, 1], F32)
                c_rstd = psb("p1_crstd", [128, 1], F32)
                c_amax = psb("p1_camax", [128, 1], F32)
                c_t = psb("p1_ct", [128, 1], F32)
                xn_b = psb("p1_xn", [128, DM], BF16)
                qv_b = psb("p1_qv", [128, DM], F32)
                qi_b = psb("p1_qi", [128, DM], I32)
                q0_b = psb("p1_q0", [128, DM], F32)
                qr_b = psb("p1_qr", [128, DM], F32)
                q1_b = psb("p1_q1", [128, DM], BF16)
                q1st = [psb(f"p1_q1st{i}", [128, KT * 128], BF16)
                        for i in range(2)]
                ps_tr = pps("p1_pstr", [128, 8 * 128], BF16)
                ps_tr2 = pps("p1_pstr2", [128, 8 * 128], BF16)
                lda = DStream(sem, "p1_ld")
                ve = sem("p1_ve")
                ac = sem("p1_ac")
                te = sem("p1_te")
                ev = sem("p1_ev")
                ev2 = sem("p1_ev2")
                stt_ = DStream(sem, "p1_st")

                def nown(ti):
                    return max(0, min(ti + 1, TT) - NPRE)

                def vaft(ti):
                    return 0 if ti < 0 else 2 * (ti + 1) + nown(ti)

                def taft(ti):
                    return 0 if ti < 0 else KT8 * (ti + 1) + KT8 * nown(ti)

                with nc.Block("p1") as block:
                    @block.sync
                    def _(sync):
                        for ti in range(TT):
                            if ti >= 2:
                                sync.wait_ge(ve, vaft(ti - 3) + 1)
                            lda.inc(sync.dma_start(
                                out=xs_b[ti % 2][:, :],
                                in_=x_in[ti * 128:(ti + 1) * 128, :]), ti)

                    @block.vector
                    def _(vector):
                        for ti in range(TT):
                            own = ti >= NPRE
                            lt = ti - NPRE
                            xt = xs_b[ti % 2]
                            lda.wait(vector, ti)
                            if ti >= 1:
                                vector.wait_ge(ac, ti)
                                vector.wait_ge(te, taft(ti - 2) + KT8)
                            vector.reduce_sum(c_m[:, :], xt[:, :], AX.X)
                            vector.drain()
                            vector.tensor_scalar_mul(c_m[:, :], c_m[:, :],
                                                     1.0 / DM)
                            vector.drain()
                            vector.tensor_scalar_sub(xc_b[:, :], xt[:, :],
                                                     c_m[:, 0:1])
                            vector.drain().then_inc(ve, 1)
                            vector.reduce_max(c_amax[:, :], xc_b[:, :], AX.X,
                                              apply_absolute_value=True)
                            vector.wait_ge(ac, ti + 1)
                            vector.reciprocal(c_rstd[:, :], c_den[:, :])
                            vector.drain()
                            if own:
                                vector.tensor_mul(c_t[:, :], c_amax[:, :],
                                                  c_rstd[:, :])
                                vector.drain()
                                vector.tensor_scalar_max(
                                    teff_s[:, lt:lt + 1], c_t[:, :], 1e-5)
                                vector.drain()
                                vector.reciprocal(c_t[:, :],
                                                  teff_s[:, lt:lt + 1])
                                vector.drain()
                                vector.tensor_scalar_mul(c_t[:, :],
                                                         c_t[:, :], 128.0)
                                vector.drain()
                                vector.tensor_mul(rs_s[:, lt:lt + 1],
                                                  c_rstd[:, :], c_t[:, :])
                                vector.drain()
                            vector.tensor_scalar_mul(xn_b[:, :], xc_b[:, :],
                                                     c_rstd[:, 0:1])
                            vector.drain().then_inc(ve, 1)
                            if own:
                                if lt >= 1:
                                    vector.wait_ge(te, taft(ti - 1))
                                vector.tensor_scalar_mul(qv_b[:, :],
                                                         xc_b[:, :],
                                                         rs_s[:, lt:lt + 1])
                                vector.drain()
                                vector.tensor_copy(qi_b[:, :], qv_b[:, :])
                                vector.drain()
                                vector.tensor_copy(q0_b[:, :], qi_b[:, :])
                                vector.drain()
                                vector.tensor_sub(qr_b[:, :], qv_b[:, :],
                                                  q0_b[:, :])
                                vector.drain()
                                vector.scalar_tensor_tensor(
                                    q0_b[:, :], qr_b[:, :], 0.5, q0_b[:, :],
                                    OP.is_ge, OP.add)
                                vector.drain()
                                vector.scalar_tensor_tensor(
                                    q0_b[:, :], qr_b[:, :], -0.5, q0_b[:, :],
                                    OP.is_gt, OP.add)
                                vector.drain()
                                vector.tensor_scalar(q1_b[:, :], q0_b[:, :],
                                                     -1.0, 127.0,
                                                     OP.add, OP.min)
                                vector.drain().then_inc(ve, 1)

                    @block.scalar
                    def _(scalar):
                        for ti in range(TT):
                            own = ti >= NPRE
                            lt = ti - NPRE
                            scalar.wait_ge(ve, vaft(ti - 1) + 1)
                            scalar.activation(jnk[:, :], xc_b[:, :],
                                              AF.Square,
                                              accum_out=c_den[:, :])
                            scalar.drain()
                            scalar.activation(c_den[:, :], c_den[:, :],
                                              AF.Sqrt, bias=eps_c[:, 0:1],
                                              scale=1.0 / DM)
                            scalar.drain().then_inc(ac, 1)
                            for g in range(KT8):
                                nk = min(8, KT - g * 8)
                                scalar.wait_ge(te, taft(ti - 1) + g + 1)
                                scalar.activation(
                                    xnT_v[:, g * 8:g * 8 + nk,
                                          ti * 128:(ti + 1) * 128],
                                    ps_tr[:, 0:nk * 128].rearrange(
                                        "p (k l) -> p k l", k=nk),
                                    AF.Copy)
                                scalar.drain().then_inc(ev, 1)
                            if own:
                                if lt >= 2:
                                    stt_.wait(scalar, lt - 2)
                                for g in range(KT8):
                                    nk = min(8, KT - g * 8)
                                    scalar.wait_ge(te,
                                                   taft(ti - 1) + KT8 + g + 1)
                                    scalar.activation(
                                        q1st[lt % 2][:, g * 8 * 128:
                                                     (g * 8 + nk) * 128],
                                        ps_tr2[:, 0:nk * 128], AF.Copy)
                                    scalar.drain().then_inc(ev2, 1)

                    @block.tensor
                    def _(tensor):
                        for ti in range(TT):
                            own = ti >= NPRE
                            lt = ti - NPRE
                            tensor.wait_ge(ve, vaft(ti - 1) + 2)
                            for g in range(KT8):
                                nk = min(8, KT - g * 8)
                                gidx = KT8 * ti + g
                                if gidx >= 1:
                                    tensor.wait_ge(ev, gidx)
                                mi = None
                                for j in range(nk):
                                    k = g * 8 + j
                                    mi = tensor.transpose(
                                        ps_tr[:, j * 128:(j + 1) * 128],
                                        xn_b[:, k * 128:(k + 1) * 128],
                                        ident_s[:, :])
                                mi.then_inc(te, 1)
                            if own:
                                tensor.wait_ge(ve, vaft(ti))
                                for g in range(KT8):
                                    nk = min(8, KT - g * 8)
                                    hidx = KT8 * lt + g
                                    if hidx >= 1:
                                        tensor.wait_ge(ev2, hidx)
                                    mi = None
                                    for j in range(nk):
                                        k = g * 8 + j
                                        mi = tensor.transpose(
                                            ps_tr2[:, j * 128:(j + 1) * 128],
                                            q1_b[:, k * 128:(k + 1) * 128],
                                            ident_s[:, :])
                                    mi.then_inc(te, 1)

                    @block.gpsimd
                    def _(gpsimd):
                        for lt in range(TTO):
                            gpsimd.wait_ge(ev2, KT8 * (lt + 1))
                            stt_.inc(gpsimd.dma_start(
                                out=q1t_sp[lt],
                                in_=q1st[lt % 2][:, :]), lt)
                        stt_.wait_all(gpsimd, TTO)

            with contextlib.ExitStack() as sp:
                psb = lambda n, s, d=F32: sp.enter_context(
                    nc.sbuf_tensor(n, s, d))
                pps = lambda n, s, d=F32: sp.enter_context(
                    nc.psum_tensor(n, s, d))
                wf = [psb(f"p2_wf{i}", [128, KT * 128], F32) for i in range(2)]
                wb = [psb(f"p2_wb{i}", [128, KT * 128], BF16)
                      for i in range(2)]
                xb = [psb(f"p2_xb{i}", [128, DC - 1 + L], BF16)
                      for i in range(2)]
                cvac = psb("p2_cvac", [128, L], F32)
                xld = [psb(f"p2_xld{i}", [128, L], BF16) for i in range(2)]
                szb = [psb(f"p2_szb{i}", [128, LOWN], BF16) for i in range(2)]
                ps_in = pps("p2_psin", [128, 2 * 512], F32)
                RTD = FF // 128
                cdb = [psb(f"p2_cd{i}", [128, DM], F32) for i in range(2)]
                pb_d = psb("p2_pbd", [128, DM], BF16)
                qb_d = psb("p2_qbd", [128, DM], BF16)
                tb_d = [psb(f"p2_tbd{i}", [128, DM], BF16)
                        for i in range(2)]
                lda = DStream(sem, "p2_ld")
                ldd2 = DStream(sem, "p2_ldd2")
                std2 = DStream(sem, "p2_std2")
                vd = sem("p2_vd")
                gp = sem("p2_gp")
                te = sem("p2_te")
                ac = sem("p2_ac")
                ve = sem("p2_ve")
                stx = DStream(sem, "p2_stx")
                stz = DStream(sem, "p2_stz")

                def chunks_of(m):
                    return LC if m < MT else LCO

                def te_aft(m):
                    return 0 if m < 0 else sum(chunks_of(i)
                                               for i in range(m + 1))

                def ac_aft(m):
                    if m < 0:
                        return 0
                    tot = 0
                    for i in range(m + 1):
                        tot += (LC + 1) if i < MT else LCO
                    return tot

                evac_done = []
                for m in range(2 * MT):
                    base = ac_aft(m - 1)
                    for c in range(chunks_of(m)):
                        evac_done.append(base + c + 1)

                with nc.Block("p2") as block:
                    @block.sync
                    def _(sync):
                        for m in range(2 * MT):
                            if m >= 2:
                                sync.wait_ge(gp, m - 1)
                            lda.inc(sync.dma_start(
                                out=wf[m % 2][:, :],
                                in_=w_in4[m].rearrange("p k m -> p (k m)")), m)
                            if m < RTD:
                                if m >= 2:
                                    sync.wait_ge(vd, m - 1)
                                ldd2.inc(sync.dma_start(
                                    out=cdb[m % 2][:, :],
                                    in_=w_dT[m * 128:(m + 1) * 128, :]), m)

                    @block.gpsimd
                    def _(gpsimd):
                        for mm in range(2 * MT):
                            if mm < RTD:
                                gpsimd.wait_ge(vd, mm + 1)
                                std2.inc(gpsimd.dma_start(
                                    out=tern_d[mm * 128:(mm + 1) * 128, :],
                                    in_=tb_d[mm % 2][:, :]), mm)
                            gpsimd.wait_ge(ac, ac_aft(mm))
                            if mm < MT:
                                stx.inc(gpsimd.dma_start(
                                    out=xld_sp[mm],
                                    in_=xld[mm % 2][:, :]), mm)
                            else:
                                stz.inc(gpsimd.dma_start(
                                    out=sz_sp[mm - MT],
                                    in_=szb[mm % 2][:, :]), mm - MT)
                        stx.wait_all(gpsimd, MT)
                        stz.wait_all(gpsimd, MT)
                        std2.wait_all(gpsimd, RTD)

                    @block.tensor
                    def _(tensor):
                        gchunk = 0
                        for m in range(2 * MT):
                            tensor.wait_ge(gp, m + 1)
                            for c in range(chunks_of(m)):
                                if gchunk >= 2:
                                    tensor.wait_ge(ac,
                                                   evac_done[gchunk - 2])
                                cw = CW_L if m < MT else CW_LO
                                coloff = (c * CW_L if m < MT
                                          else LOWN + c * CW_LO)
                                bank = gchunk % 2
                                mi = None
                                for k in range(KT):
                                    mi = tensor.matmul(
                                        ps_in[:, bank * 512:
                                              bank * 512 + cw],
                                        wb[m % 2][:, k * 128:(k + 1) * 128],
                                        xnT_v[:, k, coloff:coloff + cw],
                                        start=(k == 0), stop=(k == KT - 1))
                                mi.then_inc(te, 1)
                                gchunk += 1

                    @block.scalar
                    def _(scalar):
                        gchunk = 0
                        for m in range(2 * MT):
                            if m < MT:
                                for c in range(LC):
                                    scalar.wait_ge(te, gchunk + 1)
                                    if m >= 2 and c == 0:
                                        scalar.wait_ge(ve, m - 1)
                                    bank = gchunk % 2
                                    scalar.activation(
                                        xb[m % 2][:, DC - 1 + c * CW_L:
                                                  DC - 1 + (c + 1) * CW_L],
                                        ps_in[:, bank * 512:
                                              bank * 512 + CW_L],
                                        AF.Copy)
                                    scalar.drain().then_inc(ac, 1)
                                    gchunk += 1
                                scalar.wait_ge(ve, m + 1)
                                if m >= 2:
                                    stx.wait(scalar, m - 2)
                                scalar.activation(
                                    xld[m % 2][:, :], cvac[:, :], AF.Silu,
                                    bias=convb_s[:, m:m + 1])
                                scalar.drain().then_inc(ac, 1)
                            else:
                                for c in range(LCO):
                                    scalar.wait_ge(te, gchunk + 1)
                                    if m >= MT + 2 and c == 0:
                                        stz.wait(scalar, m - MT - 2)
                                    bank = gchunk % 2
                                    scalar.activation(
                                        szb[m % 2][:, c * CW_LO:
                                                  (c + 1) * CW_LO],
                                        ps_in[:, bank * 512:
                                              bank * 512 + CW_LO],
                                        AF.Silu)
                                    scalar.drain().then_inc(ac, 1)
                                    gchunk += 1

                    @block.vector
                    def _(vector):
                        def conv(m):
                            vector.wait_ge(ac, ac_aft(m - 1) + LC)
                            if m >= 1:
                                vector.wait_ge(ac, ac_aft(m - 1))
                            b = xb[m % 2]
                            if m < 2:
                                vector.memset(b[:, 0:DC - 1], 0.0)
                                vector.drain()
                            vector.tensor_scalar_mul(
                                cvac[:, :], b[:, DC - 1:DC - 1 + L],
                                convw_s[:, m * DC + DC - 1:m * DC + DC])
                            vector.drain()
                            for kk in range(DC - 1):
                                vector.scalar_tensor_tensor(
                                    cvac[:, :], b[:, kk:kk + L],
                                    convw_s[:, m * DC + kk:
                                            m * DC + kk + 1],
                                    cvac[:, :], OP.mult, OP.add)
                                vector.drain()
                            vector.drain().then_inc(ve, 1)

                        for m in range(2 * MT):
                            lda.wait(vector, m)
                            if m >= 2:
                                vector.wait_ge(te, te_aft(m - 2))
                            vector.tensor_copy(wb[m % 2][:, :],
                                               wf[m % 2][:, :])
                            vector.drain().then_inc(gp, 1)
                            if m < RTD:
                                ldd2.wait(vector, m)
                                vector.tensor_scalar(pb_d[:, :],
                                                     cdb[m % 2][:, :],
                                                     swcol_d[:, 0:1], 0.5,
                                                     OP.mult, OP.is_gt)
                                vector.tensor_scalar(qb_d[:, :],
                                                     cdb[m % 2][:, :],
                                                     swcol_d[:, 0:1], -0.5,
                                                     OP.mult, OP.is_lt)
                                vector.drain()
                                if m >= 2:
                                    std2.wait(vector, m - 2)
                                vector.tensor_sub(tb_d[m % 2][:, :],
                                                  pb_d[:, :], qb_d[:, :])
                                vector.drain().then_inc(vd, 1)
                            if 1 <= m <= MT:
                                conv(m - 1)

        sbst = contextlib.ExitStack()
        sbb = lambda n, s, d=F32: sbst.enter_context(nc.sbuf_tensor(n, s, d))
        wdt_b = sbb("wdt_b", [128, DI], BF16)
        dtloT = sbb("dtloT", [128, L], BF16)
        sbc = contextlib.ExitStack()
        bc_f = sbc.enter_context(nc.sbuf_tensor("bc_f", [2 * DS, L], F32))
        bc_b = sbc.enter_context(nc.sbuf_tensor("bc_b", [2 * DS, L], BF16))
        with sbst:
            with contextlib.ExitStack() as sp:
                psb = lambda n, s, d=F32: sp.enter_context(
                    nc.sbuf_tensor(n, s, d))
                pps = lambda n, s, d=F32: sp.enter_context(
                    nc.psum_tensor(n, s, d))
                wxf = psb("p2b_wxf", [128, MT * WXC], F32)
                wxb = psb("p2b_wxb", [128, MT * WXC], BF16)
                wdtf2 = psb("p2b_wdtf", [128, DI], F32)
                xlb = [psb(f"p2b_xl{i}", [128, L], BF16) for i in range(2)]
                ps_dt = pps("p2b_psdt", [128, L], F32)
                ps_bc2 = pps("p2b_psbc", [2 * DS, L], F32)
                ldw = sem("p2b_ldw")
                lda = DStream(sem, "p2b_ld")
                gp = sem("p2b_gp")
                te = sem("p2b_te")
                ve = sem("p2b_ve")
                ac = sem("p2b_ac")
                with nc.Block("p2b") as block:
                    @block.sync
                    def _(sync):
                        sync.dma_start(out=wxf[:, :], in_=w_x4
                                       ).then_inc(ldw, 16)
                        sync.dma_start(out=wdtf2[:, :], in_=w_dtT
                                       ).then_inc(ldw, 16)
                        for t in range(MT):
                            if t >= 2:
                                sync.wait_ge(te, t - 1)
                            lda.inc(sync.dma_start(out=xlb[t % 2][:, :],
                                                   in_=xld_sp[t]), t)

                    @block.gpsimd
                    def _(gpsimd):
                        gpsimd.wait_ge(ldw, 32)
                        gpsimd.tensor_copy(wxb[:, :], wxf[:, :])
                        gpsimd.drain().then_inc(gp, 1)

                    @block.tensor
                    def _(tensor):
                        tensor.wait_ge(gp, 1)
                        for t in range(MT):
                            lda.wait(tensor, t)
                            mi = None
                            for c in range(LC):
                                tensor.matmul(
                                    ps_dt[:, c * CW_L:(c + 1) * CW_L],
                                    wxb[:, t * WXC:t * WXC + 128],
                                    xlb[t % 2][:, c * CW_L:(c + 1) * CW_L],
                                    start=(t == 0), stop=(t == MT - 1))
                                mi = tensor.matmul(
                                    ps_bc2[0:2 * DS,
                                           c * CW_L:(c + 1) * CW_L],
                                    wxb[:, t * WXC + 128:t * WXC + WXC],
                                    xlb[t % 2][:, c * CW_L:(c + 1) * CW_L],
                                    start=(t == 0), stop=(t == MT - 1))
                            mi.then_inc(te, 1)

                    @block.scalar
                    def _(scalar):
                        scalar.wait_ge(ldw, 32)
                        scalar.activation(wdt_b[:, :], wdtf2[:, :], AF.Copy)
                        scalar.drain()
                        scalar.wait_ge(te, MT)
                        scalar.activation(dtloT[:, :], ps_dt[:, :], AF.Copy)
                        scalar.drain().then_inc(ac, 1)

                    @block.vector
                    def _(vector):
                        vector.wait_ge(te, MT)
                        vector.tensor_mul(bc_f[:, :], ps_bc2[0:2 * DS, :],
                                          mask32[:, :])
                        vector.drain()
                        vector.tensor_copy(bc_b[:, :], bc_f[:, :])
                        vector.drain().then_inc(ve, 1)

            with contextlib.ExitStack() as sp:
                psb = lambda n, s, d=F32: sp.enter_context(
                    nc.sbuf_tensor(n, s, d))
                pps = lambda n, s, d=F32: sp.enter_context(
                    nc.psum_tensor(n, s, d))
                ps_br = pps("p2c_psbr", [128, 2 * 512], F32)
                bcrow = [psb(f"p2c_row{i}", [1, L], BF16) for i in range(2)]
                bst = [psb(f"p2c_bst{i}", [128, 512], BF16) for i in range(2)]
                te = sem("p2c_te")
                ac = sem("p2c_ac")
                gs = DStream(sem, "p2c_gs")
                ldr = DStream(sem, "p2c_ldr")
                rows = []
                for n in range(DS):
                    rows.append((n, 0, L, True, n))
                    rows.append((DS + n, LOWN, LOWN, False, n))
                nch_of = lambda ri: LC if rows[ri][3] else LCO

                def te_aft(ri):
                    return 0 if ri < 0 else sum(nch_of(i)
                                                for i in range(ri + 1))

                with nc.Block("p2c") as block:
                    @block.sync
                    def _(sync):
                        for ri, (p, off, ln, isb, n) in enumerate(rows):
                            if ri >= 2:
                                sync.wait_ge(te, te_aft(ri - 2))
                            ldr.inc(sync.dma_start(
                                out=bcrow[ri % 2][0:1, 0:ln],
                                in_=bc_b[p:p + 1, off:off + ln]), ri)

                    @block.tensor
                    def _(tensor):
                        ci = 0
                        for ri, (p, off, ln, isb, n) in enumerate(rows):
                            ldr.wait(tensor, ri)
                            cwr = CW_L if rows[ri][3] else CW_LO
                            for c in range(nch_of(ri)):
                                if ci >= 2:
                                    tensor.wait_ge(ac, ci - 1)
                                tensor.matmul(
                                    ps_br[:, (ci % 2) * 512:
                                          (ci % 2) * 512 + cwr],
                                    ones_b[0:1, :],
                                    bcrow[ri % 2][0:1,
                                                  c * cwr:(c + 1) * cwr],
                                    start=True, stop=True).then_inc(te, 1)
                                ci += 1

                    @block.scalar
                    def _(scalar):
                        ci = 0
                        for ri, (p, off, ln, isb, n) in enumerate(rows):
                            cwr = CW_L if isb else CW_LO
                            for c in range(nch_of(ri)):
                                scalar.wait_ge(te, ci + 1)
                                if ci >= 2:
                                    gs.wait(scalar, ci - 2)
                                scalar.activation(
                                    bst[ci % 2][:, 0:cwr],
                                    ps_br[:, (ci % 2) * 512:
                                          (ci % 2) * 512 + cwr], AF.Copy)
                                scalar.drain().then_inc(ac, 1)
                                ci += 1

                    @block.gpsimd
                    def _(gpsimd):
                        ci = 0
                        for ri, (p, off, ln, isb, n) in enumerate(rows):
                            cwr = CW_L if isb else CW_LO
                            for c in range(nch_of(ri)):
                                gpsimd.wait_ge(ac, ci + 1)
                                if isb:
                                    dst = brep_sp[n][:, c * cwr:
                                                     (c + 1) * cwr]
                                else:
                                    dst = crep_sp[n][:, c * cwr:
                                                     (c + 1) * cwr]
                                gs.inc(gpsimd.dma_start(
                                    out=dst, in_=bst[ci % 2][:, 0:cwr]), ci)
                                ci += 1
                        gs.wait_all(gpsimd, ci)

            sbc.close()

            with nc.Block("p4s") as block:
                @block.vector
                def _(vector):
                    vector.tensor_scalar_mul(gg_s[:, :], teff_s[:, :],
                                             mxg_c[:, 0:1])
                    vector.tensor_scalar_mul(kk_s[:, :], teff_s[:, :],
                                             mxu_c[:, 0:1])
                    vector.drain()
                    vector.tensor_scalar_mul(gg_s[:, :], gg_s[:, :],
                                             1.0 / 128.0)
                    vector.tensor_scalar_mul(kk_s[:, :], kk_s[:, :],
                                             1.0 / (128.0 * math.sqrt(FF)))
                    vector.drain()

            with contextlib.ExitStack() as sp:
                psb = lambda n, s, d=F32: sp.enter_context(
                    nc.sbuf_tensor(n, s, d))
                pps = lambda n, s, d=F32: sp.enter_context(
                    nc.psum_tensor(n, s, d))
                NP = DS // 2
                brg = [psb(f"p3_brg{i}", [128, 2 * L], BF16)
                       for i in range(2)]
                crg = [psb(f"p3_crg{i}", [128, 2 * LOWN], BF16)
                       for i in range(2)]
                xls = [psb(f"p3_xl{i}", [128, L], BF16) for i in range(2)]
                q1all = psb("p3_q1all", [128, TTO * KT * 128], BF16)
                tgb = psb("p3_tg", [128, KT * CW_FF], BF16)
                tub = psb("p3_tu", [128, KT * CW_FF], BF16)
                mst = [psb(f"p3_mst{i}", [128, CW_FF], BF16)
                       for i in range(2)]
                szs = [psb(f"p3_sz{i}", [128, LOWN], BF16)
                       for i in range(2)]
                ebuf = psb("p3_e", [128, L], F32)
                dab = [psb(f"p3_da{i}", [128, 2 * L], F32) for i in range(2)]
                pb = psb("p3_p", [128, L], BF16)
                ub = psb("p3_u", [128, 2 * L], BF16)
                hb = psb("p3_h", [128, 2 * L], BF16)
                tmpb = [psb(f"p3_tmp{i}", [128, 2 * LOWN], BF16)
                        for i in range(2)]
                yac2 = psb("p3_yac2", [128, 2 * LOWN], F32)
                y2b = [psb(f"p3_y2{i}", [128, LOWN], BF16) for i in range(2)]
                ps_dl = pps("p3_psdl", [128, L], F32)
                ps_m = [pps(f"p3_psm{i}", [128, CW_FF], F32)
                        for i in range(4)]
                lda = DStream(sem, "p3_ld")
                ldz = DStream(sem, "p3_ldz")
                ldb = DStream(sem, "p3_ldb")
                mq = sem("p3_mq")
                mlg = sem("p3_mlg")
                mlu = sem("p3_mlu")
                tem = sem("p3_tem")
                acm = sem("p3_acm")
                gsm = DStream(sem, "p3_gsm")
                NTILE = 2 * TTO
                NTT = FFC * NTILE
                EV_T = max(1, NTT // MT)
                te = sem("p3_te")
                ac = sem("p3_ac")
                ve = sem("p3_ve")
                stt_ = DStream(sem, "p3_st")

                def ac_t(t):
                    return 0 if t < 0 else (2 + NP) * (t + 1)

                def ve_t(t):
                    return 0 if t < 0 else (3 + NP) * (t + 1)

                ub_v = ub.ap().rearrange("p (m l) -> p m l", m=2)
                hb_v = hb.ap().rearrange("p (m l) -> p m l", m=2)

                with nc.Block("p3") as block:
                    @block.sync
                    def _(sync):
                        sync.dma_start(
                            out=q1all[:, :].rearrange(
                                "p (t c) -> p t c", t=TTO),
                            in_=q1t_sp.rearrange("t p c -> p t c")
                        ).then_inc(mq, 16)
                        for t in range(MT):
                            if t % 2 == 0 and t // 2 < FFC:
                                cc = t // 2
                                if cc >= 1:
                                    sync.wait_ge(tem, cc * NTILE - TTO)
                                sync.dma_start(
                                    out=tgb[:, :],
                                    in_=tern_g[cc].rearrange(
                                        "p k f -> p (k f)")).then_inc(mlg, 16)
                                if cc >= 1:
                                    sync.wait_ge(tem, cc * NTILE)
                                sync.dma_start(
                                    out=tub[:, :],
                                    in_=tern_u[cc].rearrange(
                                        "p k f -> p (k f)")).then_inc(mlu, 16)
                            if t >= 2:
                                sync.wait_ge(ve, ve_t(t - 2))
                            lda.inc(sync.dma_start(out=xls[t % 2][:, :],
                                                   in_=xld_sp[t]), t)
                            ldz.inc(sync.dma_start(out=szs[t % 2][:, :],
                                                   in_=sz_sp[t]), t)
                            def spill(j):
                                midx = t * EV_T + j
                                if midx >= NTT:
                                    return
                                cc2, rem = divmod(midx, NTILE)
                                gu, lt2 = divmod(rem, TTO)
                                sync.wait_ge(acm, midx + 1)
                                dstt = (sg_sp if gu == 0 else up_sp)
                                gsm.inc(sync.dma_start(
                                    out=dstt[lt2][:, cc2 * CW_FF:
                                                  (cc2 + 1) * CW_FF],
                                    in_=mst[midx % 2][:, :]), midx)

                            for g in range(NP):
                                gb = t * NP + g
                                if gb >= 2:
                                    tprev, gprev = divmod(gb - 2, NP)
                                    sync.wait_ge(
                                        ve, ve_t(tprev - 1) + 1
                                        + (gprev + 1))
                                sync.dma_start(
                                    out=brg[gb % 2][:, :].rearrange(
                                        "p (m l) -> p m l", m=2),
                                    in_=brep_sp[2 * g:2 * g + 2].rearrange(
                                        "m p l -> p m l")
                                ).then_inc(ldb.sems[gb % 2], 16)
                                sync.dma_start(
                                    out=crg[gb % 2][:, :].rearrange(
                                        "p (m l) -> p m l", m=2),
                                    in_=crep_sp[2 * g:2 * g + 2].rearrange(
                                        "m p l -> p m l")
                                ).then_inc(ldb.sems[gb % 2], 16)
                                if g < EV_T:
                                    spill(g)
                            for j in range(NP, EV_T):
                                spill(j)
                        gsm.wait_all(sync, NTT)

                    @block.tensor
                    def _(tensor):
                        for t in range(MT):
                            if t >= 1:
                                tensor.wait_ge(ac, ac_t(t - 2) + 1)
                            mi = None
                            for c in range(LC):
                                mi = tensor.matmul(
                                    ps_dl[:, c * CW_L:(c + 1) * CW_L],
                                    wdt_b[:, t * 128:(t + 1) * 128],
                                    dtloT[:, c * CW_L:(c + 1) * CW_L],
                                    start=True, stop=True)
                            mi.then_inc(te, 1)
                            if t == 0:
                                tensor.wait_ge(mq, 16)
                            for j in range(EV_T):
                                midx = t * EV_T + j
                                if midx >= NTT:
                                    continue
                                cc2, rem = divmod(midx, NTILE)
                                gu, lt2 = divmod(rem, TTO)
                                if j == 0 or lt2 == 0:
                                    tensor.wait_ge(
                                        mlg if gu == 0 else mlu,
                                        16 * (cc2 + 1))
                                src = tgb if gu == 0 else tub
                                if midx >= 4:
                                    tensor.wait_ge(acm, midx - 3)
                                mi = None
                                for k in range(KT):
                                    mi = tensor.matmul(
                                        ps_m[midx % 4][:, :],
                                        q1all[:, (lt2 * KT + k) * 128:
                                              (lt2 * KT + k + 1) * 128],
                                        src[:, k * CW_FF:
                                            (k + 1) * CW_FF],
                                        start=(k == 0),
                                        stop=(k == KT - 1))
                                mi.then_inc(tem, 1)

                    @block.scalar
                    def _(scalar):
                        for t in range(MT):
                            scalar.wait_ge(te, t + 1)
                            if t >= 1:
                                scalar.wait_ge(ve, ve_t(t - 2) + 1)
                            scalar.activation(ebuf[:, :], ps_dl[:, :],
                                              AF.Exp,
                                              bias=dtb_s[:, t:t + 1])
                            scalar.drain().then_inc(ac, 1)
                            scalar.activation(ebuf[:, :], ebuf[:, :], AF.Ln,
                                              bias=1.0)
                            scalar.drain().then_inc(ac, 1)
                            for p in range(NP):
                                if p >= 2:
                                    scalar.wait_ge(ve, ve_t(t - 1) + p)
                                elif t >= 1:
                                    scalar.wait_ge(ve, ve_t(t - 2) + NP + p)
                                for h in range(2):
                                    n = 2 * p + h
                                    scalar.activation(
                                        dab[p % 2][:, h * L:(h + 1) * L],
                                        ebuf[:, :], AF.Exp,
                                        scale=nega[:, t * DS + n:
                                                   t * DS + n + 1])
                                scalar.drain().then_inc(ac, 1)
                                for j2 in range(EV_T // NP if EV_T >= NP
                                                else 1):
                                    midx = t * EV_T + p * (EV_T // NP) + j2 \
                                        if EV_T >= NP else t * EV_T + p
                                    if EV_T < NP and p >= EV_T:
                                        continue
                                    if midx >= NTT or midx >= (t + 1) * EV_T:
                                        continue
                                    cc2, rem = divmod(midx, NTILE)
                                    gu, lt2 = divmod(rem, TTO)
                                    scalar.wait_ge(tem, midx + 1)
                                    if midx >= 2:
                                        gsm.wait(scalar, midx - 2)
                                    if gu == 0:
                                        scalar.activation(
                                            mst[midx % 2][:, :],
                                            ps_m[midx % 4][:, :], AF.Copy,
                                            scale=gg_s[:, lt2:lt2 + 1])
                                    else:
                                        scalar.activation(
                                            mst[midx % 2][:, :],
                                            ps_m[midx % 4][:, :], AF.Copy)
                                    scalar.drain().then_inc(acm, 1)

                    @block.vector
                    def _(vector):
                        for t in range(MT):
                            lda.wait(vector, t)
                            vector.wait_ge(ac, ac_t(t - 1) + 2)
                            vector.tensor_mul(pb[:, :], ebuf[:, :],
                                              xls[t % 2][:, :])
                            vector.drain().then_inc(ve, 1)
                            for p in range(NP):
                                vector.wait_ge(ac, ac_t(t - 1) + 2 + p + 1)
                                vector.memset(dab[p % 2][:, L:L + 1], 0.0)
                                gb = t * NP + p
                                vector.wait_ge(ldb.sems[gb % 2],
                                               32 * (gb // 2 + 1))
                                for h in range(2):
                                    vector.tensor_mul(
                                        ub[:, h * L:(h + 1) * L], pb[:, :],
                                        brg[gb % 2][:, h * L:(h + 1) * L])
                                vector.drain()
                                vector.tensor_tensor_scan(
                                    hb[:, :], dab[p % 2][:, :], ub[:, :],
                                    0.0, OP.mult, OP.add)
                                vector.drain()
                                if p == 0:
                                    for h in range(2):
                                        vector.tensor_mul(
                                            yac2[:, h * LOWN:
                                                 (h + 1) * LOWN],
                                            hb[:, h * L + LOWN:(h + 1) * L],
                                            crg[gb % 2][
                                                :, h * LOWN:(h + 1) * LOWN])
                                    vector.drain().then_inc(ve, 1)
                                    continue
                                for h in range(2):
                                    vector.tensor_mul(
                                        tmpb[p % 2][:, h * LOWN:
                                                    (h + 1) * LOWN],
                                        hb[:, h * L + LOWN:(h + 1) * L],
                                        crg[gb % 2][
                                            :, h * LOWN:(h + 1) * LOWN])
                                vector.drain()
                                vector.tensor_add(yac2[:, :], yac2[:, :],
                                                  tmpb[p % 2][:, :])
                                vector.drain().then_inc(ve, 1)
                            ldz.wait(vector, t)
                            vector.scalar_tensor_tensor(
                                yac2[:, 0:LOWN], xls[t % 2][:, LOWN:L],
                                dvec_s[:, t:t + 1], yac2[:, 0:LOWN],
                                OP.mult, OP.add)
                            vector.drain()
                            vector.tensor_add(yac2[:, 0:LOWN],
                                              yac2[:, 0:LOWN],
                                              yac2[:, LOWN:2 * LOWN])
                            vector.drain().then_inc(ve, 1)
                            if t >= 2:
                                stt_.wait(vector, t - 2)
                            vector.tensor_mul(y2b[t % 2][:, :],
                                              yac2[:, 0:LOWN],
                                              szs[t % 2][:, :])
                            vector.drain().then_inc(ve, 1)

                    @block.gpsimd
                    def _(gpsimd):
                        for t in range(MT):
                            gpsimd.wait_ge(ve, ve_t(t))
                            ins = gpsimd.dma_start(
                                out=y2_sp[t], in_=y2b[t % 2][:, :])
                            stt_.inc(ins, t)
                        stt_.wait_all(gpsimd, MT)

        s_early.close()

        with contextlib.ExitStack() as sp:
            psb = lambda n, s, d=F32: sp.enter_context(nc.sbuf_tensor(n, s, d))
            pps = lambda n, s, d=F32: sp.enter_context(nc.psum_tensor(n, s, d))
            NDQ = DM // CW_DM
            y2all = psb("p3b_y2all", [128, MT * LOWN], BF16)
            wos = [psb(f"p3b_wo{i}", [128, CW_DM], BF16) for i in range(3)]
            xrs = [psb(f"p3b_xr{i}", [128, CW_DM], F32) for i in range(2)]
            omb = [psb(f"p3b_om{i}", [128, CW_DM], F32) for i in range(2)]
            ps8 = pps("p3b_ps8", [128, TTO * 512], F32)
            ldy = sem("p3b_ldy")
            ldw = DStream(sem, "p3b_ldw", 3)
            ldx = DStream(sem, "p3b_ldx")
            te = sem("p3b_te")
            ve = sem("p3b_ve")
            stt_ = DStream(sem, "p3b_st")
            with nc.Block("p3b") as block:
                @block.sync
                def _(sync):
                    sync.dma_start(
                        out=y2all[:, :].rearrange("p (t l) -> p t l", t=MT),
                        in_=y2_sp.rearrange("t p l -> p t l")
                    ).then_inc(ldy, 16)
                    for dq in range(NDQ):
                        for k in range(MT):
                            gi = dq * MT + k
                            if gi >= 3:
                                sync.wait_ge(te, gi - 2)
                            ldw.inc(sync.dma_start(
                                out=wos[gi % 3][:, :],
                                in_=wo_bf[k * 128:(k + 1) * 128,
                                          dq * CW_DM:(dq + 1) * CW_DM]), gi)
                        for lt in range(TTO):
                            gx = dq * TTO + lt
                            if gx >= 2:
                                sync.wait_ge(ve, gx - 1)
                            ldx.inc(sync.dma_start(
                                out=xrs[gx % 2][:, :],
                                in_=x_in[LOWN + lt * 128:
                                         LOWN + (lt + 1) * 128,
                                         dq * CW_DM:(dq + 1) * CW_DM]), gx)

                @block.tensor
                def _(tensor):
                    tensor.wait_ge(ldy, 16)
                    for dq in range(NDQ):
                        if dq >= 1:
                            tensor.wait_ge(ve, dq * TTO)
                        for k in range(MT):
                            gi = dq * MT + k
                            ldw.wait(tensor, gi)
                            mi = None
                            for lt in range(TTO):
                                mi = tensor.matmul(
                                    ps8[:, lt * 512:lt * 512 + CW_DM],
                                    y2all[:, k * LOWN + lt * 128:
                                          k * LOWN + (lt + 1) * 128],
                                    wos[gi % 3][:, :],
                                    start=(k == 0), stop=(k == MT - 1))
                            mi.then_inc(te, 1)

                @block.vector
                def _(vector):
                    for dq in range(NDQ):
                        vector.wait_ge(te, (dq + 1) * MT)
                        for lt in range(TTO):
                            gx = dq * TTO + lt
                            ldx.wait(vector, gx)
                            if gx >= 2:
                                stt_.wait(vector, gx - 2)
                            vector.tensor_add(
                                omb[gx % 2][:, :],
                                ps8[:, lt * 512:lt * 512 + CW_DM],
                                xrs[gx % 2][:, :])
                            vector.drain().then_inc(ve, 1)

                @block.gpsimd
                def _(gpsimd):
                    for dq in range(NDQ):
                        for lt in range(TTO):
                            gx = dq * TTO + lt
                            gpsimd.wait_ge(ve, gx + 1)
                            stt_.inc(gpsimd.dma_start(
                                out=outm_sp[lt][:, dq * CW_DM:
                                                (dq + 1) * CW_DM],
                                in_=omb[gx % 2][:, :]), gx)
                    stt_.wait_all(gpsimd, NDQ * TTO)

        with contextlib.ExitStack() as sp:
            psb = lambda n, s, d=F32: sp.enter_context(nc.sbuf_tensor(n, s, d))
            pps = lambda n, s, d=F32: sp.enter_context(nc.psum_tensor(n, s, d))
            dsc_s = psb("dsc_s", [128, TTO], F32)
            sgl = [psb(f"p4_sg{i}", [128, FF], BF16) for i in range(2)]
            upl = [psb(f"p4_up{i}", [128, FF], BF16) for i in range(2)]
            midb = psb("p4_mid", [128, FF], F32)
            pongb = psb("p4_pong", [128, FF], F32)
            q3b = psb("p4_q3", [128, FF], BF16)
            q3t = psb("p4_q3t", [128, FFT * 128], BF16)
            NDB = 4
            tdb = [psb(f"p4_td{i}", [128, DM], BF16) for i in range(NDB)]
            finb = [psb(f"p4_fin{i}", [128, DM], F32) for i in range(2)]
            QCH = min(512, FF // 4)
            NQ = FF // QCH
            qscr = psb("p4_qscr", [128, 3 * QCH], F32)
            c_am = psb("p4_cam", [128, 1], F32)
            c_vs = psb("p4_cvs", [128, 1], F32)
            ps_d = pps("p4_psd", [128, DM], F32)
            ps_t = pps("p4_pst", [128, 8 * 128], BF16).ap()
            ldg = DStream(sem, "p4_ldg")
            ldu = DStream(sem, "p4_ldu")
            ldd = DStream(sem, "p4_ldd", nbuf=NDB)
            ldo = DStream(sem, "p4_ldo")
            tr = sem("p4_tr")
            ted = sem("p4_ted")
            ve = sem("p4_ve")
            ac = sem("p4_ac")
            asg = sem("p4_asg")
            aq = sem("p4_aq")
            qfull = pongb if NSTG % 2 == 0 else midb
            fwb = midb if NSTG % 2 == 0 else pongb
            stp4 = DStream(sem, "p4_st")

            def ve_mid(lt):
                return 1 if lt == 0 else 4 * lt

            def ve_scales(lt):
                return 2 if lt == 0 else 4 * lt + 1

            def ve_q3(lt):
                return 3 if lt == 0 else 4 * lt + 2

            def ve_fin(lt):
                return 4 * TTO if lt == TTO - 1 else 4 * (lt + 1) + 3

            with nc.Block("p4") as block:
                @block.sync
                def _(sync):
                    for lt in range(TTO):
                        for k in range(FFT):
                            gi2 = lt * FFT + k
                            if gi2 >= NDB:
                                sync.wait_ge(ted, gi2 - NDB + 1)
                            ldd.inc(sync.dma_start(
                                out=tdb[gi2 % NDB][:, :],
                                in_=tern_d[k * 128:(k + 1) * 128, :]), gi2)

                @block.tensor
                def _(tensor):
                    for lt in range(TTO):
                        tensor.wait_ge(ve, ve_q3(lt))
                        for g in range(FFT8):
                            nk = min(8, FFT - g * 8)
                            if g >= 1:
                                tensor.wait_ge(ac, lt * FFT8 + g)
                            mi = None
                            for j in range(nk):
                                k = g * 8 + j
                                mi = tensor.transpose(
                                    ps_t[:, j * 128:(j + 1) * 128],
                                    q3b[:, k * 128:(k + 1) * 128],
                                    ident_s[:, :])
                            mi.then_inc(tr, 1)
                        tensor.wait_ge(ac, (lt + 1) * FFT8)
                        if lt >= 1:
                            tensor.wait_ge(ve, ve_fin(lt - 1))
                        for k in range(FFT):
                            gi2 = lt * FFT + k
                            ldd.wait(tensor, gi2)
                            mi = None
                            for c in range(DMC):
                                mi = tensor.matmul(
                                    ps_d[:, c * CW_DM:(c + 1) * CW_DM],
                                    q3t[:, k * 128:(k + 1) * 128],
                                    tdb[gi2 % NDB][:, c * CW_DM:
                                                   (c + 1) * CW_DM],
                                    start=(k == 0), stop=(k == FFT - 1))
                            mi.then_inc(ted, 1)

                @block.scalar
                def _(scalar):
                    def silu(lt):
                        ldg.wait(scalar, lt)
                        scalar.activation(sgl[lt % 2][:, :],
                                          sgl[lt % 2][:, :], AF.Silu)
                        scalar.drain().then_inc(asg, 1)

                    for lt in range(min(2, TTO)):
                        silu(lt)
                    for lt in range(TTO):
                        scalar.wait_ge(ve, ve_scales(lt))
                        scalar.activation(qfull[:, :], fwb[:, :], AF.Copy,
                                          scale=c_vs[:, 0:1])
                        scalar.drain().then_inc(aq, 1)
                        if lt >= 1:
                            scalar.wait_ge(ted, lt * FFT)
                        for g in range(FFT8):
                            nk = min(8, FFT - g * 8)
                            scalar.wait_ge(tr, lt * FFT8 + g + 1)
                            scalar.activation(
                                q3t[:, g * 8 * 128:(g * 8 + nk) * 128],
                                ps_t[:, 0:nk * 128], AF.Copy)
                            scalar.drain().then_inc(ac, 1)
                        if lt + 2 < TTO:
                            silu(lt + 2)

                def fin(vector, lt):
                    vector.wait_ge(ted, (lt + 1) * FFT)
                    ldo.wait(vector, lt)
                    vector.scalar_tensor_tensor(
                        finb[lt % 2][:, :], ps_d[:, :],
                        dsc_s[:, lt:lt + 1], finb[lt % 2][:, :],
                        OP.mult, OP.add)
                    vector.drain().then_inc(ve, 1)

                @block.vector
                def _(vector):
                    for lt in range(TTO):
                        vector.wait_ge(asg, lt + 1)
                        ldu.wait(vector, lt)
                        vector.tensor_mul(midb[:, :], sgl[lt % 2][:, :],
                                          upl[lt % 2][:, :])
                        vector.drain().then_inc(ve, 1)
                        cur, nxt = midb, pongb
                        for st_i in range(NSTG):
                            h = 1 << st_i
                            a = cur[:, :].rearrange(
                                "p (nb two h) -> p nb two h", two=2, h=h)
                            o = nxt[:, :].rearrange(
                                "p (nb two h) -> p nb two h", two=2, h=h)
                            vector.tensor_add(o[:, :, 0, :], a[:, :, 0, :],
                                              a[:, :, 1, :])
                            vector.tensor_sub(o[:, :, 1, :], a[:, :, 0, :],
                                              a[:, :, 1, :])
                            vector.drain()
                            cur, nxt = nxt, cur
                        fw = cur
                        vector.reduce_max(c_am[:, :], fw[:, :], AX.X,
                                          apply_absolute_value=True)
                        vector.drain()
                        vector.tensor_mul(c_am[:, :], c_am[:, :],
                                          kk_s[:, lt:lt + 1])
                        vector.drain()
                        vector.tensor_scalar_max(c_am[:, :], c_am[:, :],
                                                 1e-5)
                        vector.drain()
                        vector.tensor_mul(dsc_s[:, lt:lt + 1], c_am[:, :],
                                          mxd_c[:, 0:1])
                        vector.drain()
                        vector.tensor_scalar_mul(dsc_s[:, lt:lt + 1],
                                                 dsc_s[:, lt:lt + 1],
                                                 1.0 / 128.0)
                        vector.reciprocal(c_vs[:, :], c_am[:, :])
                        vector.drain()
                        vector.tensor_scalar_mul(c_vs[:, :], c_vs[:, :],
                                                 128.0)
                        vector.drain()
                        vector.tensor_mul(c_vs[:, :], c_vs[:, :],
                                          kk_s[:, lt:lt + 1])
                        vector.drain().then_inc(ve, 1)
                        qib = qscr[:, 0:QCH].bitcast(I32)
                        q0b = qscr[:, 1 * QCH:2 * QCH]
                        qrb = qscr[:, 2 * QCH:3 * QCH]
                        if lt >= 1:
                            vector.wait_ge(tr, lt * FFT8)
                        vector.wait_ge(aq, lt + 1)
                        for qc in range(NQ):
                            sl = slice(qc * QCH, (qc + 1) * QCH)
                            qvb = qfull[:, sl]
                            vector.tensor_copy(qib, qvb)
                            vector.drain()
                            vector.tensor_copy(q0b, qib)
                            vector.drain()
                            vector.tensor_sub(qrb, qvb, q0b)
                            vector.drain()
                            vector.scalar_tensor_tensor(
                                q0b, qrb, 0.5, q0b, OP.is_ge, OP.add)
                            vector.drain()
                            vector.scalar_tensor_tensor(
                                q0b, qrb, -0.5, q0b, OP.is_gt, OP.add)
                            vector.drain()
                            vector.tensor_scalar(q3b[:, sl], q0b,
                                                 -1.0, 127.0,
                                                 OP.add, OP.min)
                            vector.drain()
                        vector.drain().then_inc(ve, 1)
                        if lt >= 1:
                            fin(vector, lt - 1)
                    fin(vector, TTO - 1)

                @block.gpsimd
                def _(gpsimd):
                    for lt in range(min(2, TTO)):
                        ldg.inc(gpsimd.dma_start(out=sgl[lt % 2][:, :],
                                                 in_=sg_sp[lt]), lt)
                        ldu.inc(gpsimd.dma_start(out=upl[lt % 2][:, :],
                                                 in_=up_sp[lt]), lt)
                        ldo.inc(gpsimd.dma_start(out=finb[lt % 2][:, :],
                                                 in_=outm_sp[lt]), lt)
                    for lt in range(TTO):
                        if lt + 2 < TTO:
                            gpsimd.wait_ge(ve, ve_mid(lt))
                            ldg.inc(gpsimd.dma_start(
                                out=sgl[lt % 2][:, :],
                                in_=sg_sp[lt + 2]), lt + 2)
                            ldu.inc(gpsimd.dma_start(
                                out=upl[lt % 2][:, :],
                                in_=up_sp[lt + 2]), lt + 2)
                        gpsimd.wait_ge(ve, ve_fin(lt))
                        stp4.inc(gpsimd.dma_start(
                            out=out_t[lt * 128:(lt + 1) * 128, :],
                            in_=finb[lt % 2][:, :]), lt)
                        if lt + 2 < TTO:
                            stp4.wait(gpsimd, lt)
                            ldo.inc(gpsimd.dma_start(
                                out=finb[lt % 2][:, :],
                                in_=outm_sp[lt + 2]), lt + 2)
                    stp4.wait_all(gpsimd, TTO)

    return nc



_BUILD_CACHE = {}


def _get_program(C):
    key = tuple(sorted(C.items()))
    if key not in _BUILD_CACHE:
        _BUILD_CACHE[key] = build_program(C)
    return _BUILD_CACHE[key]


def _hadamard_ok():
    pass


def make_in_maps(C, inputs):
    DM, DI, DS, DC = C["DM"], C["DI"], C["DS"], C["DC"]
    FF, LOWN = C["FF"], C["LOWN"]
    L = 2 * LOWN
    KT = DM // 128
    MT = DI // 128

    f32 = lambda a: np.ascontiguousarray(a, dtype=np.float32)
    x = f32(inputs["L0_attn_hidden_states"])
    B = x.shape[0]
    in_w = f32(inputs["L0_attn_in_proj_w"])
    convw = f32(inputs["L0_attn_conv_w"])
    convb = f32(inputs["L0_attn_conv_b"])
    xw = f32(inputs["L0_attn_x_proj_w"])
    dtw = f32(inputs["L0_attn_dt_proj_w"])
    dtb = f32(inputs["L0_attn_dt_proj_b"])
    alog = f32(inputs["L0_attn_A_log"])
    dv = f32(inputs["L0_attn_D"])
    ow = f32(inputs["L0_attn_out_proj_w"])
    wg = f32(inputs["L0_ffn_w_gate"])
    wu = f32(inputs["L0_ffn_w_up"])
    wd = f32(inputs["L0_ffn_w_down"])

    w_in4 = np.ascontiguousarray(
        in_w.T.reshape(KT, 128, 2 * MT, 128).transpose(2, 1, 0, 3))
    conv_w4 = np.ascontiguousarray(
        convw.reshape(MT, 128, DC).transpose(1, 0, 2).reshape(128, MT * DC))
    conv_b4 = np.ascontiguousarray(convb.reshape(MT, 128).T)
    w_x4 = np.ascontiguousarray(
        xw.T.reshape(MT, 128, -1).transpose(1, 0, 2).reshape(128, -1))
    w_dtT = np.ascontiguousarray(dtw.T)
    dt_b4 = np.ascontiguousarray(dtb.reshape(MT, 128).T)
    a_log4 = np.ascontiguousarray(
        alog.reshape(MT, 128, DS).transpose(1, 0, 2).reshape(128, MT * DS))
    d_vec4 = np.ascontiguousarray(dv.reshape(MT, 128).T)
    w_oT = np.ascontiguousarray(ow.T)
    w_gT = np.ascontiguousarray(wg.T)
    w_uT = np.ascontiguousarray(wu.T)
    w_dT = np.ascontiguousarray(wd.T)
    identm = np.eye(128, dtype=ml_dtypes.bfloat16)

    in_maps = []
    for c in range(2 * B):
        b, h = c // 2, c % 2
        if h == 0:
            pre = np.zeros((LOWN, DM), np.float32)
            own = x[b, 0:LOWN]
            mrow = np.concatenate([np.zeros(LOWN, np.float32),
                                   np.ones(LOWN, np.float32)])
        else:
            pre = x[b, 0:LOWN]
            own = x[b, LOWN:L]
            mrow = np.ones(L, np.float32)
        x_core = np.ascontiguousarray(np.concatenate([pre, own], axis=0))
        in_maps.append({
            "x_in": x_core,
            "w_in4": w_in4,
            "conv_w": conv_w4,
            "conv_b": conv_b4,
            "w_x4": w_x4,
            "w_dtT": w_dtT,
            "dt_b": dt_b4,
            "a_log": a_log4,
            "d_vec": d_vec4,
            "w_oT": w_oT,
            "w_gT": w_gT,
            "w_uT": w_uT,
            "w_dT": w_dT,
            "ident": identm,
            "maskrow": mrow.reshape(1, L),
        })
    return in_maps


def kernel(**inputs):
    C = full_cfg()
    nc = _get_program(C)
    in_maps = make_in_maps(C, inputs)
    res = run_bass_kernel_spmd(nc, in_maps, core_ids=list(range(8)))
    B = inputs["L0_attn_hidden_states"].shape[0]
    LOWN = C["LOWN"]
    DM = C["DM"]
    out = np.empty((B, 2 * LOWN, DM), np.float32)
    for c in range(2 * B):
        b, h = c // 2, c % 2
        out[b, h * LOWN:(h + 1) * LOWN] = res.results[c]["out"]
    return out


# version 29 (final)
# speedup vs baseline: 1.0281x; 1.0155x over previous
import contextlib
import math

import numpy as np
import ml_dtypes

import concourse.bass as bass
import concourse.mybir as mybir
from concourse.bass_utils import run_bass_kernel_spmd

F32 = mybir.dt.float32
BF16 = mybir.dt.bfloat16
I32 = mybir.dt.int32
AF = mybir.ActivationFunctionType
OP = mybir.AluOpType
AX = mybir.AxisListType


def full_cfg():
    return dict(DM=2048, DI=4096, DS=16, DC=4, FF=8192, LOWN=1024)


def small_cfg():
    return dict(DM=256, DI=512, DS=16, DC=4, FF=1024, LOWN=256)


class DStream:

    def __init__(self, mksem, name, nbuf=2):
        self.sems = [mksem(f"{name}_{i}") for i in range(nbuf)]
        self.n = nbuf

    def inc(self, instr, i):
        instr.then_inc(self.sems[i % self.n], 16)

    def wait(self, eng, i):
        eng.wait_ge(self.sems[i % self.n], 16 * (i // self.n + 1))

    def wait_all(self, eng, total):
        for p in range(self.n):
            cnt = (total - p + self.n - 1) // self.n
            if cnt > 0:
                eng.wait_ge(self.sems[p], 16 * cnt)


def build_program(C):
    DM, DI, DS, DC = C["DM"], C["DI"], C["DS"], C["DC"]
    FF, LOWN = C["FF"], C["LOWN"]
    DTR = 128
    L = 2 * LOWN
    KT = DM // 128
    MT = DI // 128
    FFT = FF // 128
    TT = L // 128
    TTO = LOWN // 128
    CW_L = min(512, L)
    CW_LO = min(512, LOWN)
    CW_DM = min(512, DM)
    CW_FF = min(512, FF)
    LC = L // CW_L
    LCO = LOWN // CW_LO
    DMC = DM // CW_DM
    FFC = FF // CW_FF
    NPRE = TT - TTO
    NSTG = int(math.log2(FF))
    assert 2 ** NSTG == FF
    KT8 = (KT + 7) // 8
    FFT8 = (FFT + 7) // 8
    WXC = DTR + 2 * DS

    nc = bass.Bass("TRN2", target_bir_lowering=False, debug=False)

    din = lambda n, s, d=F32: nc.dram_tensor(n, s, d, kind="ExternalInput").ap()
    x_in = din("x_in", [L, DM])
    w_in4 = din("w_in4", [2 * MT, 128, KT, 128])
    conv_w = din("conv_w", [128, MT * DC])
    conv_b = din("conv_b", [128, MT])
    w_x4 = din("w_x4", [128, MT * WXC])
    w_dtT = din("w_dtT", [DTR, DI])
    dt_b = din("dt_b", [128, MT])
    a_log = din("a_log", [128, MT * DS])
    d_vec = din("d_vec", [128, MT])
    w_oT = din("w_oT", [DI, DM])
    w_gT = din("w_gT", [DM, FF])
    w_uT = din("w_uT", [DM, FF])
    w_dT = din("w_dT", [FF, DM])
    ident = din("ident", [128, 128], BF16)
    maskrow = din("maskrow", [1, L])

    out_t = nc.dram_tensor("out", [LOWN, DM], F32, kind="ExternalOutput").ap()

    dint = lambda n, s, d: nc.dram_tensor(n, s, d, kind="Internal").ap()
    xld_sp = dint("xld_sp", [MT, 128, L], BF16)
    sz_sp = dint("sz_sp", [MT, 128, LOWN], BF16)
    q1t_sp = dint("q1t_sp", [TTO, 128, KT * 128], BF16)
    y2_sp = dint("y2_sp", [MT, 128, LOWN], BF16)
    tern_g = dint("tern_g", [FFC, 128, KT, CW_FF], BF16)
    tern_u = dint("tern_u", [FFC, 128, KT, CW_FF], BF16)
    tern_d = dint("tern_d", [FF, DM], BF16)
    wo_bf = dint("wo_bf", [DI, DM], BF16)
    brep_sp = dint("brep_sp", [DS, 128, L], BF16)
    sg_sp = dint("sg_sp", [TTO, 128, FF], BF16)
    up_sp = dint("up_sp", [TTO, 128, FF], BF16)
    crep_sp = dint("crep_sp", [DS, 128, LOWN], BF16)
    outm_sp = dint("outm_sp", [TTO, 128, DM], F32)

    s0 = contextlib.ExitStack()
    sb0 = lambda n, s, d=F32: s0.enter_context(nc.sbuf_tensor(n, s, d))
    sem = lambda n: s0.enter_context(nc.semaphore(n))

    ident_s = sb0("ident_s", [128, 128], BF16)
    ones_f = sb0("ones_f", [1, 128], F32)
    onescol = sb0("onescol", [128, 1], F32)
    ones_b = sb0("ones_b", [1, 128], BF16)

    convw_s = sb0("convw_s", [128, MT * DC], F32)
    convb_s = sb0("convb_s", [128, MT], F32)
    dtb_s = sb0("dtb_s", [128, MT], F32)
    dvec_s = sb0("dvec_s", [128, MT], F32)

    teff_s = sb0("teff_s", [128, TTO], F32)
    rs_s = sb0("rs_s", [128, TTO], F32)
    mxg_c = sb0("mxg_c", [128, 1], F32)
    mxu_c = sb0("mxu_c", [128, 1], F32)
    mxd_c = sb0("mxd_c", [128, 1], F32)
    swcol_d = sb0("swcol_d", [128, 1], F32)
    one1 = sb0("one1", [1, 1], F32)
    gg_s = sb0("gg_s", [128, TTO], F32)
    kk_s = sb0("kk_s", [128, TTO], F32)
    eps_c = sb0("eps_c", [128, 1], F32)

    s_early = contextlib.ExitStack()
    nega = s_early.enter_context(nc.sbuf_tensor("nega", [128, MT * DS], F32))
    mask32 = s_early.enter_context(
        nc.sbuf_tensor("mask32", [2 * DS, L], BF16))

    with s0:
        with contextlib.ExitStack() as sp:
            psb = lambda n, s, d=F32: sp.enter_context(nc.sbuf_tensor(n, s, d))
            pps = lambda n, s, d=F32: sp.enter_context(nc.psum_tensor(n, s, d))
            mrow_s = psb("mrow_s", [1, L], F32)
            ps_m32 = pps("ps_m32", [2 * DS, L], F32)
            ld = sem("p0_ld")
            ac = sem("p0_ac")
            ve = sem("p0_ve")
            te = sem("p0_te")
            with nc.Block("p0") as block:
                @block.sync
                def _(sync):
                    for dst, src in [
                        (ident_s[:, :], ident),
                        (nega[:, :], a_log),
                        (convw_s[:, :], conv_w),
                        (dtb_s[:, :], dt_b),
                        (convb_s[:, :], conv_b),
                        (dvec_s[:, :], d_vec),
                        (mrow_s[0:1, :], maskrow),
                    ]:
                        sync.dma_start(out=dst, in_=src).then_inc(ld, 16)

                @block.scalar
                def _(scalar):
                    scalar.wait_ge(ld, 7 * 16)
                    scalar.activation(nega[:, :], nega[:, :], AF.Exp)
                    scalar.drain().then_inc(ac, 1)
                    scalar.wait_ge(te, 1)
                    scalar.activation(mask32[:, :], ps_m32[0:2 * DS, :],
                                      AF.Copy)
                    scalar.drain()

                @block.vector
                def _(vector):
                    vector.wait_ge(ld, 7 * 16)
                    vector.memset(ones_f[:, :], 1.0)
                    vector.memset(eps_c[:, :], 1e-5)
                    vector.memset(onescol[:, :], 1.0)
                    vector.memset(ones_b[:, :], 1.0)
                    vector.wait_ge(ac, 1)
                    vector.tensor_scalar_mul(nega[:, :], nega[:, :], -1.0)
                    vector.drain().then_inc(ve, 1)

                @block.tensor
                def _(tensor):
                    tensor.wait_ge(ve, 1)
                    mi = None
                    for c in range(LC):
                        mi = tensor.matmul(
                            ps_m32[0:2 * DS, c * CW_L:(c + 1) * CW_L],
                            ones_f[0:1, 0:2 * DS],
                            mrow_s[0:1, c * CW_L:(c + 1) * CW_L],
                            start=True, stop=True)
                    mi.then_inc(te, 1)

        with contextlib.ExitStack() as sp:
            psb = lambda n, s, d=F32: sp.enter_context(nc.sbuf_tensor(n, s, d))
            pps = lambda n, s, d=F32: sp.enter_context(nc.psum_tensor(n, s, d))
            CB = max(DM, FF)
            wbuf = [psb(f"pw_w{i}", [128, CB], F32) for i in range(2)]
            cbuf = [psb(f"pw_c{i}", [128, CB], F32) for i in range(2)]
            pbuf = psb("pw_p", [128, CB], BF16)
            qbuf = psb("pw_q", [128, CB], BF16)
            tbuf = [psb(f"pw_t{i}", [128, CB], BF16) for i in range(2)]
            JW = min(CB, 2048)
            NJC = (CB + JW - 1) // JW
            junkb = psb("pw_junk", [128, JW], BF16)
            rcol = psb("pw_rcol", [128, NJC], F32)
            racc2 = psb("pw_racc2", [128, 2], F32)
            swcol = psb("pw_swcol", [128, 1], F32)
            ps_tot = pps("pw_pstot", [1, 1], F32)
            ps_bc = pps("pw_psbc", [128, 1], F32)

            lda = DStream(sem, "pw_lda")
            ldb = DStream(sem, "pw_ldb")
            ldo2 = DStream(sem, "pw_ldo2")
            sto = DStream(sem, "pw_sto")
            ac = sem("pw_ac")
            aco = sem("pw_aco")
            ve = sem("pw_ve")
            te = sem("pw_te")
            stp = DStream(sem, "pw_st")

            MA = {"g": (w_gT, DM // 128, FF, mxg_c, 0),
                  "u": (w_uT, DM // 128, FF, mxu_c, 1),
                  "d": (w_dT, FF // 128, DM, mxd_c, 0)}
            MC = {"g": (w_gT, tern_g, DM // 128, FF, FFC),
                  "u": (w_uT, tern_u, DM // 128, FF, FFC),
                  "d": (w_dT, tern_d, FF // 128, DM, 1)}
            RT_O = DI // 128
            pwb = {"a": 0, "v": 0, "t": 0, "l0": 0, "l1": 0,
                   "b0": 0, "b1": 0, "s0": 0, "s1": 0}

            def pw_block(name, mc, ma, wo=False):
                bA, bV, bT = pwb["a"], pwb["v"], pwb["t"]
                bL = [pwb["l0"], pwb["l1"]]
                bB = [pwb["b0"], pwb["b1"]]
                bS = [pwb["s0"], pwb["s1"]]
                RTc = MC[mc][2] if mc else 0
                per = MC[mc][4] if mc else 1
                RTa = MA[ma][1] if ma else 0
                nb2 = 1 if mc else 0

                def ve_tern(r):
                    return (bV + nb2 + 2 * r + 1 if r < RTa
                            else bV + nb2 + 2 * RTa + (r - RTa) + 1)

                def ve_racc(r):
                    return (bV + nb2 + 2 * r + 2 if r < RTc
                            else bV + nb2 + 2 * RTc + (r - RTc) + 1)

                with nc.Block(name) as block:
                    @block.sync
                    def _(sync):
                        for r in range(max(RTc, RTa, RT_O if wo else 0)):
                            if mc and r < RTc:
                                if r >= 2:
                                    sync.wait_ge(ve, ve_tern(r - 2))
                                sync.dma_start(
                                    out=cbuf[r % 2][:, 0:MC[mc][3]],
                                    in_=MC[mc][0][r * 128:(r + 1) * 128, :]
                                ).then_inc(ldb.sems[r % 2], 16)
                            if ma and r < RTa:
                                if r >= 2:
                                    sync.wait_ge(ac, bA + nb2 + r - 1)
                                sync.dma_start(
                                    out=wbuf[r % 2][:, 0:MA[ma][2]],
                                    in_=MA[ma][0][r * 128:(r + 1) * 128, :]
                                ).then_inc(lda.sems[r % 2], 16)
                            if wo and r < RT_O:
                                if r >= 2:
                                    sync.wait_ge(aco, r - 1)
                                sync.dma_start(
                                    out=cbuf[r % 2][:, 0:DM],
                                    in_=w_oT[r * 128:(r + 1) * 128, :]
                                ).then_inc(ldo2.sems[r % 2], 16)

                    @block.tensor
                    def _(tensor):
                        if not mc:
                            return
                        col = MA[mc][4]
                        tensor.wait_ge(ve, bV)
                        tensor.matmul(ps_tot[0:1, 0:1],
                                      racc2[:, col:col + 1],
                                      onescol[:, 0:1],
                                      start=True, stop=True).then_inc(te, 1)
                        tensor.wait_ge(ve, bV + 1)
                        tensor.matmul(ps_bc[:, 0:1], ones_f[0:1, :],
                                      one1[0:1, 0:1], start=True, stop=True
                                      ).then_inc(te, 1)

                    @block.scalar
                    def _(scalar):
                        if mc:
                            scalar.wait_ge(te, bT + 2)
                            scalar.activation(MA[mc][3][:, :],
                                              ps_bc[:, 0:1], AF.Copy)
                            scalar.drain().then_inc(ac, 1)
                        for r in range(max(RTa, RT_O if wo else 0)):
                            if ma and r < RTa:
                                scalar.wait_ge(lda.sems[r % 2],
                                               16 * (bL[r % 2]
                                                     + r // 2 + 1))
                                if r >= 1:
                                    scalar.wait_ge(ve, ve_racc(r - 1))
                                Ca = MA[ma][2]
                                for jc in range(0, Ca, JW):
                                    cw = min(JW, Ca - jc)
                                    if jc > 0:
                                        scalar.drain()
                                    scalar.activation(
                                        junkb[:, 0:cw],
                                        wbuf[r % 2][:, jc:jc + cw],
                                        AF.Abs,
                                        accum_out=rcol[:, jc // JW:
                                                       jc // JW + 1])
                                scalar.drain().then_inc(ac, 1)
                            if wo and r < RT_O:
                                scalar.wait_ge(ldo2.sems[r % 2],
                                               16 * (r // 2 + 1))
                                if r >= 2:
                                    scalar.wait_ge(
                                        sto.sems[(r - 2) % 2],
                                        16 * ((r - 2) // 2 + 1))
                                scalar.activation(tbuf[r % 2][:, 0:DM],
                                                  cbuf[r % 2][:, 0:DM],
                                                  AF.Copy)
                                scalar.drain().then_inc(aco, 1)

                    @block.vector
                    def _(vector):
                        if mc:
                            vector.wait_ge(te, bT + 1)
                            vector.tensor_copy(one1[:, :], ps_tot[0:1, 0:1])
                            vector.drain()
                            vector.tensor_scalar(
                                one1[:, :], one1[:, :],
                                1.0 / (MC[mc][2] * 128 * MC[mc][3]), 1e-5,
                                OP.mult, OP.max)
                            vector.drain().then_inc(ve, 1)
                            vector.wait_ge(ac, bA + 1)
                            vector.reciprocal(swcol[:, :], MA[mc][3][:, :])
                            vector.drain()
                        if ma:
                            vector.memset(racc2[:, MA[ma][4]:
                                                MA[ma][4] + 1], 0.0)
                            vector.drain()
                        for r in range(max(RTc, RTa)):
                            if mc and r < RTc:
                                Cc = MC[mc][3]
                                vector.wait_ge(ldb.sems[r % 2],
                                               16 * (bB[r % 2]
                                                     + r // 2 + 1))
                                w = cbuf[r % 2][:, 0:Cc]
                                vector.tensor_scalar(pbuf[:, 0:Cc], w,
                                                     swcol[:, 0:1], 0.5,
                                                     OP.mult, OP.is_gt)
                                vector.tensor_scalar(qbuf[:, 0:Cc], w,
                                                     swcol[:, 0:1], -0.5,
                                                     OP.mult, OP.is_lt)
                                vector.drain()
                                if r >= 2:
                                    vector.wait_ge(
                                        stp.sems[(r - 2) % 2],
                                        16 * (bS[(r - 2) % 2]
                                              + per * ((r - 2) // 2 + 1)))
                                vector.tensor_sub(tbuf[r % 2][:, 0:Cc],
                                                  pbuf[:, 0:Cc],
                                                  qbuf[:, 0:Cc])
                                vector.drain().then_inc(ve, 1)
                            if ma and r < RTa:
                                vector.wait_ge(ac, bA + nb2 + r + 1)
                                for jc in range((MA[ma][2] + JW - 1) // JW):
                                    vector.tensor_add(
                                        racc2[:, MA[ma][4]:MA[ma][4] + 1],
                                        racc2[:, MA[ma][4]:MA[ma][4] + 1],
                                        rcol[:, jc:jc + 1])
                                    vector.drain()
                                vector.drain().then_inc(ve, 1)

                    @block.gpsimd
                    def _(gpsimd):
                        for r in range(max(RTc, RT_O if wo else 0)):
                            if mc and r < RTc:
                                gpsimd.wait_ge(ve, ve_tern(r))
                                dst = MC[mc][1]
                                if per > 1:
                                    for cc in range(FFC):
                                        gpsimd.dma_start(
                                            out=dst[cc, :, r, :],
                                            in_=tbuf[r % 2][
                                                :, cc * CW_FF:
                                                (cc + 1) * CW_FF]
                                        ).then_inc(stp.sems[r % 2], 16)
                                else:
                                    gpsimd.dma_start(
                                        out=dst[r * 128:(r + 1) * 128, :],
                                        in_=tbuf[r % 2][:, 0:MC[mc][3]]
                                    ).then_inc(stp.sems[r % 2], 16)
                            if wo and r < RT_O:
                                gpsimd.wait_ge(aco, r + 1)
                                gpsimd.dma_start(
                                    out=wo_bf[r * 128:(r + 1) * 128, :],
                                    in_=tbuf[r % 2][:, 0:DM]
                                ).then_inc(sto.sems[r % 2], 16)
                        for p in range(2):
                            if mc:
                                cnt = per * ((RTc - p + 1) // 2)
                                if cnt:
                                    gpsimd.wait_ge(stp.sems[p],
                                                   16 * (bS[p] + cnt))
                            if wo:
                                cnt = (RT_O - p + 1) // 2
                                if cnt:
                                    gpsimd.wait_ge(sto.sems[p], 16 * cnt)

                pwb["a"] += nb2 + RTa
                pwb["v"] += nb2 + RTc + RTa
                pwb["t"] += 2 if mc else 0
                pwb["l0"] += (RTa + 1) // 2
                pwb["l1"] += RTa // 2
                pwb["b0"] += (RTc + 1) // 2
                pwb["b1"] += RTc // 2
                pwb["s0"] += per * ((RTc + 1) // 2)
                pwb["s1"] += per * (RTc // 2)

            pw_block("pw0", None, "g", wo=True)
            pw_block("pw1", "g", "u")
            pw_block("pw2", "u", "d")

            bA3, bV3, bT3 = pwb["a"], pwb["v"], pwb["t"]
            with nc.Block("pw3h") as block:
                @block.tensor
                def _(tensor):
                    tensor.matmul(ps_tot[0:1, 0:1], racc2[:, 0:1],
                                  onescol[:, 0:1], start=True, stop=True
                                  ).then_inc(te, 1)
                    tensor.wait_ge(ve, bV3 + 1)
                    tensor.matmul(ps_bc[:, 0:1], ones_f[0:1, :],
                                  one1[0:1, 0:1], start=True, stop=True
                                  ).then_inc(te, 1)

                @block.scalar
                def _(scalar):
                    scalar.wait_ge(te, bT3 + 2)
                    scalar.activation(mxd_c[:, :], ps_bc[:, 0:1], AF.Copy)
                    scalar.drain().then_inc(ac, 1)

                @block.vector
                def _(vector):
                    vector.wait_ge(te, bT3 + 1)
                    vector.tensor_copy(one1[:, :], ps_tot[0:1, 0:1])
                    vector.drain()
                    vector.tensor_scalar(one1[:, :], one1[:, :],
                                         1.0 / (FF * DM), 1e-5,
                                         OP.mult, OP.max)
                    vector.drain().then_inc(ve, 1)
                    vector.wait_ge(ac, bA3 + 1)
                    vector.reciprocal(swcol_d[:, :], mxd_c[:, :])
                    vector.drain()

        sa = contextlib.ExitStack()
        xnT = sa.enter_context(nc.sbuf_tensor("xnT", [128, KT * L], BF16))
        xnT_v = xnT.ap().rearrange("p (k l) -> p k l", k=KT)
        with sa:
            with contextlib.ExitStack() as sp:
                psb = lambda n, s, d=F32: sp.enter_context(
                    nc.sbuf_tensor(n, s, d))
                pps = lambda n, s, d=F32: sp.enter_context(
                    nc.psum_tensor(n, s, d))
                xs_b = [psb(f"p1_x{i}", [128, DM], F32) for i in range(2)]
                xc_b = psb("p1_xc", [128, DM], F32)
                jnk = psb("p1_junk", [128, DM], BF16)
                c_m = psb("p1_cm", [128, 1], F32)
                c_den = psb("p1_cden", [128, 1], F32)
                c_rstd = psb("p1_crstd", [128, 1], F32)
                c_amax = psb("p1_camax", [128, 1], F32)
                c_t = psb("p1_ct", [128, 1], F32)
                xn_b = psb("p1_xn", [128, DM], BF16)
                qv_b = psb("p1_qv", [128, DM], F32)
                qi_b = psb("p1_qi", [128, DM], I32)
                q0_b = psb("p1_q0", [128, DM], F32)
                qr_b = psb("p1_qr", [128, DM], F32)
                q1_b = psb("p1_q1", [128, DM], BF16)
                q1st = [psb(f"p1_q1st{i}", [128, KT * 128], BF16)
                        for i in range(2)]
                ps_tr = pps("p1_pstr", [128, 8 * 128], BF16)
                ps_tr2 = pps("p1_pstr2", [128, 8 * 128], BF16)
                lda = DStream(sem, "p1_ld")
                ve = sem("p1_ve")
                ac = sem("p1_ac")
                te = sem("p1_te")
                ev = sem("p1_ev")
                ev2 = sem("p1_ev2")
                stt_ = DStream(sem, "p1_st")

                def nown(ti):
                    return max(0, min(ti + 1, TT) - NPRE)

                def vaft(ti):
                    return 0 if ti < 0 else 2 * (ti + 1) + nown(ti)

                def taft(ti):
                    return 0 if ti < 0 else KT8 * (ti + 1) + KT8 * nown(ti)

                with nc.Block("p1") as block:
                    @block.sync
                    def _(sync):
                        for ti in range(TT):
                            if ti >= 2:
                                sync.wait_ge(ve, vaft(ti - 3) + 1)
                            lda.inc(sync.dma_start(
                                out=xs_b[ti % 2][:, :],
                                in_=x_in[ti * 128:(ti + 1) * 128, :]), ti)

                    @block.vector
                    def _(vector):
                        for ti in range(TT):
                            own = ti >= NPRE
                            lt = ti - NPRE
                            xt = xs_b[ti % 2]
                            lda.wait(vector, ti)
                            if ti >= 1:
                                vector.wait_ge(ac, ti)
                                vector.wait_ge(te, taft(ti - 2) + KT8)
                            vector.reduce_sum(c_m[:, :], xt[:, :], AX.X)
                            vector.drain()
                            vector.tensor_scalar_mul(c_m[:, :], c_m[:, :],
                                                     1.0 / DM)
                            vector.drain()
                            vector.tensor_scalar_sub(xc_b[:, :], xt[:, :],
                                                     c_m[:, 0:1])
                            vector.drain().then_inc(ve, 1)
                            vector.reduce_max(c_amax[:, :], xc_b[:, :], AX.X,
                                              apply_absolute_value=True)
                            vector.wait_ge(ac, ti + 1)
                            vector.reciprocal(c_rstd[:, :], c_den[:, :])
                            vector.drain()
                            if own:
                                vector.tensor_mul(c_t[:, :], c_amax[:, :],
                                                  c_rstd[:, :])
                                vector.drain()
                                vector.tensor_scalar_max(
                                    teff_s[:, lt:lt + 1], c_t[:, :], 1e-5)
                                vector.drain()
                                vector.reciprocal(c_t[:, :],
                                                  teff_s[:, lt:lt + 1])
                                vector.drain()
                                vector.tensor_scalar_mul(c_t[:, :],
                                                         c_t[:, :], 128.0)
                                vector.drain()
                                vector.tensor_mul(rs_s[:, lt:lt + 1],
                                                  c_rstd[:, :], c_t[:, :])
                                vector.drain()
                            vector.tensor_scalar_mul(xn_b[:, :], xc_b[:, :],
                                                     c_rstd[:, 0:1])
                            vector.drain().then_inc(ve, 1)
                            if own:
                                if lt >= 1:
                                    vector.wait_ge(te, taft(ti - 1))
                                vector.tensor_scalar_mul(qv_b[:, :],
                                                         xc_b[:, :],
                                                         rs_s[:, lt:lt + 1])
                                vector.drain()
                                vector.tensor_copy(qi_b[:, :], qv_b[:, :])
                                vector.drain()
                                vector.tensor_copy(q0_b[:, :], qi_b[:, :])
                                vector.drain()
                                vector.tensor_sub(qr_b[:, :], qv_b[:, :],
                                                  q0_b[:, :])
                                vector.drain()
                                vector.scalar_tensor_tensor(
                                    q0_b[:, :], qr_b[:, :], 0.5, q0_b[:, :],
                                    OP.is_ge, OP.add)
                                vector.drain()
                                vector.scalar_tensor_tensor(
                                    q0_b[:, :], qr_b[:, :], -0.5, q0_b[:, :],
                                    OP.is_gt, OP.add)
                                vector.drain()
                                vector.tensor_scalar(q1_b[:, :], q0_b[:, :],
                                                     -1.0, 127.0,
                                                     OP.add, OP.min)
                                vector.drain().then_inc(ve, 1)

                    @block.scalar
                    def _(scalar):
                        for ti in range(TT):
                            own = ti >= NPRE
                            lt = ti - NPRE
                            scalar.wait_ge(ve, vaft(ti - 1) + 1)
                            scalar.activation(jnk[:, :], xc_b[:, :],
                                              AF.Square,
                                              accum_out=c_den[:, :])
                            scalar.drain()
                            scalar.activation(c_den[:, :], c_den[:, :],
                                              AF.Sqrt, bias=eps_c[:, 0:1],
                                              scale=1.0 / DM)
                            scalar.drain().then_inc(ac, 1)
                            for g in range(KT8):
                                nk = min(8, KT - g * 8)
                                scalar.wait_ge(te, taft(ti - 1) + g + 1)
                                scalar.activation(
                                    xnT_v[:, g * 8:g * 8 + nk,
                                          ti * 128:(ti + 1) * 128],
                                    ps_tr[:, 0:nk * 128].rearrange(
                                        "p (k l) -> p k l", k=nk),
                                    AF.Copy)
                                scalar.drain().then_inc(ev, 1)
                            if own:
                                if lt >= 2:
                                    stt_.wait(scalar, lt - 2)
                                for g in range(KT8):
                                    nk = min(8, KT - g * 8)
                                    scalar.wait_ge(te,
                                                   taft(ti - 1) + KT8 + g + 1)
                                    scalar.activation(
                                        q1st[lt % 2][:, g * 8 * 128:
                                                     (g * 8 + nk) * 128],
                                        ps_tr2[:, 0:nk * 128], AF.Copy)
                                    scalar.drain().then_inc(ev2, 1)

                    @block.tensor
                    def _(tensor):
                        for ti in range(TT):
                            own = ti >= NPRE
                            lt = ti - NPRE
                            tensor.wait_ge(ve, vaft(ti - 1) + 2)
                            for g in range(KT8):
                                nk = min(8, KT - g * 8)
                                gidx = KT8 * ti + g
                                if gidx >= 1:
                                    tensor.wait_ge(ev, gidx)
                                mi = None
                                for j in range(nk):
                                    k = g * 8 + j
                                    mi = tensor.transpose(
                                        ps_tr[:, j * 128:(j + 1) * 128],
                                        xn_b[:, k * 128:(k + 1) * 128],
                                        ident_s[:, :])
                                mi.then_inc(te, 1)
                            if own:
                                tensor.wait_ge(ve, vaft(ti))
                                for g in range(KT8):
                                    nk = min(8, KT - g * 8)
                                    hidx = KT8 * lt + g
                                    if hidx >= 1:
                                        tensor.wait_ge(ev2, hidx)
                                    mi = None
                                    for j in range(nk):
                                        k = g * 8 + j
                                        mi = tensor.transpose(
                                            ps_tr2[:, j * 128:(j + 1) * 128],
                                            q1_b[:, k * 128:(k + 1) * 128],
                                            ident_s[:, :])
                                    mi.then_inc(te, 1)

                    @block.gpsimd
                    def _(gpsimd):
                        for lt in range(TTO):
                            gpsimd.wait_ge(ev2, KT8 * (lt + 1))
                            stt_.inc(gpsimd.dma_start(
                                out=q1t_sp[lt],
                                in_=q1st[lt % 2][:, :]), lt)
                        stt_.wait_all(gpsimd, TTO)

            with contextlib.ExitStack() as sp:
                psb = lambda n, s, d=F32: sp.enter_context(
                    nc.sbuf_tensor(n, s, d))
                pps = lambda n, s, d=F32: sp.enter_context(
                    nc.psum_tensor(n, s, d))
                wf = [psb(f"p2_wf{i}", [128, KT * 128], F32) for i in range(2)]
                wb = [psb(f"p2_wb{i}", [128, KT * 128], BF16)
                      for i in range(2)]
                xb = [psb(f"p2_xb{i}", [128, DC - 1 + L], BF16)
                      for i in range(2)]
                cvac = psb("p2_cvac", [128, L], F32)
                xld = [psb(f"p2_xld{i}", [128, L], BF16) for i in range(2)]
                szb = [psb(f"p2_szb{i}", [128, LOWN], BF16) for i in range(2)]
                ps_in = pps("p2_psin", [128, 4 * 512], F32)
                RTD = FF // 128
                cdb = [psb(f"p2_cd{i}", [128, DM], F32) for i in range(2)]
                pb_d = psb("p2_pbd", [128, DM], BF16)
                qb_d = psb("p2_qbd", [128, DM], BF16)
                tb_d = [psb(f"p2_tbd{i}", [128, DM], BF16)
                        for i in range(2)]
                lda = DStream(sem, "p2_ld")
                ldd2 = DStream(sem, "p2_ldd2")
                std2 = DStream(sem, "p2_std2")
                vd = sem("p2_vd")
                gp = sem("p2_gp")
                te = sem("p2_te")
                ac = sem("p2_ac")
                ve = sem("p2_ve")
                stx = DStream(sem, "p2_stx")
                stz = DStream(sem, "p2_stz")

                def chunks_of(m):
                    return LC if m < MT else LCO

                def te_aft(m):
                    return 0 if m < 0 else sum(chunks_of(i)
                                               for i in range(m + 1))

                def ac_aft(m):
                    if m < 0:
                        return 0
                    tot = 0
                    for i in range(m + 1):
                        tot += (LC + 1) if i < MT else LCO
                    return tot

                evac_done = []
                for m in range(2 * MT):
                    base = ac_aft(m - 1)
                    for c in range(chunks_of(m)):
                        evac_done.append(base + c + 1)

                with nc.Block("p2") as block:
                    @block.sync
                    def _(sync):
                        for m in range(2 * MT):
                            if m >= 2:
                                sync.wait_ge(gp, m - 1)
                            lda.inc(sync.dma_start(
                                out=wf[m % 2][:, :],
                                in_=w_in4[m].rearrange("p k m -> p (k m)")), m)
                            if m < RTD:
                                if m >= 2:
                                    sync.wait_ge(vd, m - 1)
                                ldd2.inc(sync.dma_start(
                                    out=cdb[m % 2][:, :],
                                    in_=w_dT[m * 128:(m + 1) * 128, :]), m)

                    @block.gpsimd
                    def _(gpsimd):
                        for mm in range(2 * MT):
                            if mm < RTD:
                                gpsimd.wait_ge(vd, mm + 1)
                                std2.inc(gpsimd.dma_start(
                                    out=tern_d[mm * 128:(mm + 1) * 128, :],
                                    in_=tb_d[mm % 2][:, :]), mm)
                            gpsimd.wait_ge(ac, ac_aft(mm))
                            if mm < MT:
                                stx.inc(gpsimd.dma_start(
                                    out=xld_sp[mm],
                                    in_=xld[mm % 2][:, :]), mm)
                            else:
                                stz.inc(gpsimd.dma_start(
                                    out=sz_sp[mm - MT],
                                    in_=szb[mm % 2][:, :]), mm - MT)
                        stx.wait_all(gpsimd, MT)
                        stz.wait_all(gpsimd, MT)
                        std2.wait_all(gpsimd, RTD)

                    @block.tensor
                    def _(tensor):
                        gchunk = 0
                        for m in range(2 * MT):
                            tensor.wait_ge(gp, m + 1)
                            for c in range(chunks_of(m)):
                                if gchunk >= 4:
                                    tensor.wait_ge(ac,
                                                   evac_done[gchunk - 4])
                                cw = CW_L if m < MT else CW_LO
                                coloff = (c * CW_L if m < MT
                                          else LOWN + c * CW_LO)
                                bank = gchunk % 4
                                mi = None
                                for k in range(KT):
                                    mi = tensor.matmul(
                                        ps_in[:, bank * 512:
                                              bank * 512 + cw],
                                        wb[m % 2][:, k * 128:(k + 1) * 128],
                                        xnT_v[:, k, coloff:coloff + cw],
                                        start=(k == 0), stop=(k == KT - 1))
                                mi.then_inc(te, 1)
                                gchunk += 1

                    @block.scalar
                    def _(scalar):
                        gchunk = 0
                        for m in range(2 * MT):
                            if m < MT:
                                for c in range(LC):
                                    scalar.wait_ge(te, gchunk + 1)
                                    if m >= 2 and c == 0:
                                        scalar.wait_ge(ve, m - 1)
                                    bank = gchunk % 4
                                    scalar.activation(
                                        xb[m % 2][:, DC - 1 + c * CW_L:
                                                  DC - 1 + (c + 1) * CW_L],
                                        ps_in[:, bank * 512:
                                              bank * 512 + CW_L],
                                        AF.Copy)
                                    scalar.drain().then_inc(ac, 1)
                                    gchunk += 1
                                scalar.wait_ge(ve, m + 1)
                                if m >= 2:
                                    stx.wait(scalar, m - 2)
                                scalar.activation(
                                    xld[m % 2][:, :], cvac[:, :], AF.Silu,
                                    bias=convb_s[:, m:m + 1])
                                scalar.drain().then_inc(ac, 1)
                            else:
                                for c in range(LCO):
                                    scalar.wait_ge(te, gchunk + 1)
                                    if m >= MT + 2 and c == 0:
                                        stz.wait(scalar, m - MT - 2)
                                    bank = gchunk % 4
                                    scalar.activation(
                                        szb[m % 2][:, c * CW_LO:
                                                  (c + 1) * CW_LO],
                                        ps_in[:, bank * 512:
                                              bank * 512 + CW_LO],
                                        AF.Silu)
                                    scalar.drain().then_inc(ac, 1)
                                    gchunk += 1

                    @block.vector
                    def _(vector):
                        def conv(m):
                            vector.wait_ge(ac, ac_aft(m - 1) + LC)
                            if m >= 1:
                                vector.wait_ge(ac, ac_aft(m - 1))
                            b = xb[m % 2]
                            if m < 2:
                                vector.memset(b[:, 0:DC - 1], 0.0)
                                vector.drain()
                            vector.tensor_scalar_mul(
                                cvac[:, :], b[:, DC - 1:DC - 1 + L],
                                convw_s[:, m * DC + DC - 1:m * DC + DC])
                            vector.drain()
                            for kk in range(DC - 1):
                                vector.scalar_tensor_tensor(
                                    cvac[:, :], b[:, kk:kk + L],
                                    convw_s[:, m * DC + kk:
                                            m * DC + kk + 1],
                                    cvac[:, :], OP.mult, OP.add)
                                vector.drain()
                            vector.drain().then_inc(ve, 1)

                        for m in range(2 * MT):
                            lda.wait(vector, m)
                            if m >= 2:
                                vector.wait_ge(te, te_aft(m - 2))
                            vector.tensor_copy(wb[m % 2][:, :],
                                               wf[m % 2][:, :])
                            vector.drain().then_inc(gp, 1)
                            if m < RTD:
                                ldd2.wait(vector, m)
                                vector.tensor_scalar(pb_d[:, :],
                                                     cdb[m % 2][:, :],
                                                     swcol_d[:, 0:1], 0.5,
                                                     OP.mult, OP.is_gt)
                                vector.tensor_scalar(qb_d[:, :],
                                                     cdb[m % 2][:, :],
                                                     swcol_d[:, 0:1], -0.5,
                                                     OP.mult, OP.is_lt)
                                vector.drain()
                                if m >= 2:
                                    std2.wait(vector, m - 2)
                                vector.tensor_sub(tb_d[m % 2][:, :],
                                                  pb_d[:, :], qb_d[:, :])
                                vector.drain().then_inc(vd, 1)
                            if 1 <= m <= MT:
                                conv(m - 1)

        sbst = contextlib.ExitStack()
        sbb = lambda n, s, d=F32: sbst.enter_context(nc.sbuf_tensor(n, s, d))
        wdt_b = sbb("wdt_b", [128, DI], BF16)
        dtloT = sbb("dtloT", [128, L], BF16)
        sbc = contextlib.ExitStack()
        bc_f = sbc.enter_context(nc.sbuf_tensor("bc_f", [2 * DS, L], F32))
        bc_b = sbc.enter_context(nc.sbuf_tensor("bc_b", [2 * DS, L], BF16))
        with sbst:
            with contextlib.ExitStack() as sp:
                psb = lambda n, s, d=F32: sp.enter_context(
                    nc.sbuf_tensor(n, s, d))
                pps = lambda n, s, d=F32: sp.enter_context(
                    nc.psum_tensor(n, s, d))
                wxf = psb("p2b_wxf", [128, MT * WXC], F32)
                wxb = psb("p2b_wxb", [128, MT * WXC], BF16)
                wdtf2 = psb("p2b_wdtf", [128, DI], F32)
                xlb = [psb(f"p2b_xl{i}", [128, L], BF16) for i in range(2)]
                ps_dt = pps("p2b_psdt", [128, L], F32)
                ps_bc2 = pps("p2b_psbc", [2 * DS, L], F32)
                ldw = sem("p2b_ldw")
                lda = DStream(sem, "p2b_ld")
                gp = sem("p2b_gp")
                te = sem("p2b_te")
                ve = sem("p2b_ve")
                ac = sem("p2b_ac")
                with nc.Block("p2b") as block:
                    @block.sync
                    def _(sync):
                        sync.dma_start(out=wxf[:, :], in_=w_x4
                                       ).then_inc(ldw, 16)
                        sync.dma_start(out=wdtf2[:, :], in_=w_dtT
                                       ).then_inc(ldw, 16)
                        for t in range(MT):
                            if t >= 2:
                                sync.wait_ge(te, t - 1)
                            lda.inc(sync.dma_start(out=xlb[t % 2][:, :],
                                                   in_=xld_sp[t]), t)

                    @block.gpsimd
                    def _(gpsimd):
                        gpsimd.wait_ge(ldw, 32)
                        gpsimd.tensor_copy(wxb[:, :], wxf[:, :])
                        gpsimd.drain().then_inc(gp, 1)

                    @block.tensor
                    def _(tensor):
                        tensor.wait_ge(gp, 1)
                        for t in range(MT):
                            lda.wait(tensor, t)
                            mi = None
                            for c in range(LC):
                                tensor.matmul(
                                    ps_dt[:, c * CW_L:(c + 1) * CW_L],
                                    wxb[:, t * WXC:t * WXC + 128],
                                    xlb[t % 2][:, c * CW_L:(c + 1) * CW_L],
                                    start=(t == 0), stop=(t == MT - 1))
                                mi = tensor.matmul(
                                    ps_bc2[0:2 * DS,
                                           c * CW_L:(c + 1) * CW_L],
                                    wxb[:, t * WXC + 128:t * WXC + WXC],
                                    xlb[t % 2][:, c * CW_L:(c + 1) * CW_L],
                                    start=(t == 0), stop=(t == MT - 1))
                            mi.then_inc(te, 1)

                    @block.scalar
                    def _(scalar):
                        scalar.wait_ge(ldw, 32)
                        scalar.activation(wdt_b[:, :], wdtf2[:, :], AF.Copy)
                        scalar.drain()
                        scalar.wait_ge(te, MT)
                        scalar.activation(dtloT[:, :], ps_dt[:, :], AF.Copy)
                        scalar.drain().then_inc(ac, 1)

                    @block.vector
                    def _(vector):
                        vector.wait_ge(te, MT)
                        vector.tensor_mul(bc_f[:, :], ps_bc2[0:2 * DS, :],
                                          mask32[:, :])
                        vector.drain()
                        vector.tensor_copy(bc_b[:, :], bc_f[:, :])
                        vector.drain().then_inc(ve, 1)

            with contextlib.ExitStack() as sp:
                psb = lambda n, s, d=F32: sp.enter_context(
                    nc.sbuf_tensor(n, s, d))
                pps = lambda n, s, d=F32: sp.enter_context(
                    nc.psum_tensor(n, s, d))
                ps_br = pps("p2c_psbr", [128, 2 * 512], F32)
                bcrow = [psb(f"p2c_row{i}", [1, L], BF16) for i in range(2)]
                bst = [psb(f"p2c_bst{i}", [128, 512], BF16) for i in range(2)]
                te = sem("p2c_te")
                ac = sem("p2c_ac")
                gs = DStream(sem, "p2c_gs")
                ldr = DStream(sem, "p2c_ldr")
                rows = []
                for n in range(DS):
                    rows.append((n, 0, L, True, n))
                    rows.append((DS + n, LOWN, LOWN, False, n))
                nch_of = lambda ri: LC if rows[ri][3] else LCO

                def te_aft(ri):
                    return 0 if ri < 0 else sum(nch_of(i)
                                                for i in range(ri + 1))

                with nc.Block("p2c") as block:
                    @block.sync
                    def _(sync):
                        for ri, (p, off, ln, isb, n) in enumerate(rows):
                            if ri >= 2:
                                sync.wait_ge(te, te_aft(ri - 2))
                            ldr.inc(sync.dma_start(
                                out=bcrow[ri % 2][0:1, 0:ln],
                                in_=bc_b[p:p + 1, off:off + ln]), ri)

                    @block.tensor
                    def _(tensor):
                        ci = 0
                        for ri, (p, off, ln, isb, n) in enumerate(rows):
                            ldr.wait(tensor, ri)
                            cwr = CW_L if rows[ri][3] else CW_LO
                            for c in range(nch_of(ri)):
                                if ci >= 2:
                                    tensor.wait_ge(ac, ci - 1)
                                tensor.matmul(
                                    ps_br[:, (ci % 2) * 512:
                                          (ci % 2) * 512 + cwr],
                                    ones_b[0:1, :],
                                    bcrow[ri % 2][0:1,
                                                  c * cwr:(c + 1) * cwr],
                                    start=True, stop=True).then_inc(te, 1)
                                ci += 1

                    @block.scalar
                    def _(scalar):
                        ci = 0
                        for ri, (p, off, ln, isb, n) in enumerate(rows):
                            cwr = CW_L if isb else CW_LO
                            for c in range(nch_of(ri)):
                                scalar.wait_ge(te, ci + 1)
                                if ci >= 2:
                                    gs.wait(scalar, ci - 2)
                                scalar.activation(
                                    bst[ci % 2][:, 0:cwr],
                                    ps_br[:, (ci % 2) * 512:
                                          (ci % 2) * 512 + cwr], AF.Copy)
                                scalar.drain().then_inc(ac, 1)
                                ci += 1

                    @block.gpsimd
                    def _(gpsimd):
                        ci = 0
                        for ri, (p, off, ln, isb, n) in enumerate(rows):
                            cwr = CW_L if isb else CW_LO
                            for c in range(nch_of(ri)):
                                gpsimd.wait_ge(ac, ci + 1)
                                if isb:
                                    dst = brep_sp[n][:, c * cwr:
                                                     (c + 1) * cwr]
                                else:
                                    dst = crep_sp[n][:, c * cwr:
                                                     (c + 1) * cwr]
                                gs.inc(gpsimd.dma_start(
                                    out=dst, in_=bst[ci % 2][:, 0:cwr]), ci)
                                ci += 1
                        gs.wait_all(gpsimd, ci)

            sbc.close()

            with nc.Block("p4s") as block:
                @block.vector
                def _(vector):
                    vector.tensor_scalar_mul(gg_s[:, :], teff_s[:, :],
                                             mxg_c[:, 0:1])
                    vector.tensor_scalar_mul(kk_s[:, :], teff_s[:, :],
                                             mxu_c[:, 0:1])
                    vector.drain()
                    vector.tensor_scalar_mul(gg_s[:, :], gg_s[:, :],
                                             1.0 / 128.0)
                    vector.tensor_scalar_mul(kk_s[:, :], kk_s[:, :],
                                             1.0 / (128.0 * math.sqrt(FF)))
                    vector.drain()

            with contextlib.ExitStack() as sp:
                psb = lambda n, s, d=F32: sp.enter_context(
                    nc.sbuf_tensor(n, s, d))
                pps = lambda n, s, d=F32: sp.enter_context(
                    nc.psum_tensor(n, s, d))
                NP = DS // 2
                brg = [psb(f"p3_brg{i}", [128, 2 * L], BF16)
                       for i in range(2)]
                crg = [psb(f"p3_crg{i}", [128, 2 * LOWN], BF16)
                       for i in range(2)]
                xls = [psb(f"p3_xl{i}", [128, L], BF16) for i in range(2)]
                q1all = psb("p3_q1all", [128, TTO * KT * 128], BF16)
                tgb = psb("p3_tg", [128, KT * CW_FF], BF16)
                tub = psb("p3_tu", [128, KT * CW_FF], BF16)
                mst = [psb(f"p3_mst{i}", [128, CW_FF], BF16)
                       for i in range(2)]
                szs = [psb(f"p3_sz{i}", [128, LOWN], BF16)
                       for i in range(2)]
                ebuf = psb("p3_e", [128, L], F32)
                dab = [psb(f"p3_da{i}", [128, 2 * L], F32) for i in range(2)]
                pb = psb("p3_p", [128, L], BF16)
                ub = psb("p3_u", [128, 2 * L], BF16)
                hb = psb("p3_h", [128, 2 * L], BF16)
                tmpb = [psb(f"p3_tmp{i}", [128, 2 * LOWN], BF16)
                        for i in range(2)]
                yac2 = psb("p3_yac2", [128, 2 * LOWN], F32)
                y2b = [psb(f"p3_y2{i}", [128, LOWN], BF16) for i in range(2)]
                ps_dl = pps("p3_psdl", [128, L], F32)
                ps_m = [pps(f"p3_psm{i}", [128, CW_FF], F32)
                        for i in range(4)]
                lda = DStream(sem, "p3_ld")
                ldz = DStream(sem, "p3_ldz")
                ldb = DStream(sem, "p3_ldb")
                mq = sem("p3_mq")
                mlg = sem("p3_mlg")
                mlu = sem("p3_mlu")
                tem = sem("p3_tem")
                acm = sem("p3_acm")
                gsm = DStream(sem, "p3_gsm")
                NTILE = 2 * TTO
                NTT = FFC * NTILE
                EV_T = max(1, NTT // MT)
                te = sem("p3_te")
                ac = sem("p3_ac")
                ve = sem("p3_ve")
                stt_ = DStream(sem, "p3_st")

                def ac_t(t):
                    return 0 if t < 0 else (2 + NP) * (t + 1)

                def ve_t(t):
                    return 0 if t < 0 else (3 + NP) * (t + 1)

                ub_v = ub.ap().rearrange("p (m l) -> p m l", m=2)
                hb_v = hb.ap().rearrange("p (m l) -> p m l", m=2)

                with nc.Block("p3") as block:
                    @block.sync
                    def _(sync):
                        sync.dma_start(
                            out=q1all[:, :].rearrange(
                                "p (t c) -> p t c", t=TTO),
                            in_=q1t_sp.rearrange("t p c -> p t c")
                        ).then_inc(mq, 16)
                        for t in range(MT):
                            if t % 2 == 0 and t // 2 < FFC:
                                cc = t // 2
                                if cc >= 1:
                                    sync.wait_ge(tem, cc * NTILE - TTO)
                                sync.dma_start(
                                    out=tgb[:, :],
                                    in_=tern_g[cc].rearrange(
                                        "p k f -> p (k f)")).then_inc(mlg, 16)
                                if cc >= 1:
                                    sync.wait_ge(tem, cc * NTILE)
                                sync.dma_start(
                                    out=tub[:, :],
                                    in_=tern_u[cc].rearrange(
                                        "p k f -> p (k f)")).then_inc(mlu, 16)
                            if t >= 2:
                                sync.wait_ge(ve, ve_t(t - 2))
                            lda.inc(sync.dma_start(out=xls[t % 2][:, :],
                                                   in_=xld_sp[t]), t)
                            ldz.inc(sync.dma_start(out=szs[t % 2][:, :],
                                                   in_=sz_sp[t]), t)
                            def spill(j):
                                midx = t * EV_T + j
                                if midx >= NTT:
                                    return
                                cc2, rem = divmod(midx, NTILE)
                                gu, lt2 = divmod(rem, TTO)
                                sync.wait_ge(acm, midx + 1)
                                dstt = (sg_sp if gu == 0 else up_sp)
                                gsm.inc(sync.dma_start(
                                    out=dstt[lt2][:, cc2 * CW_FF:
                                                  (cc2 + 1) * CW_FF],
                                    in_=mst[midx % 2][:, :]), midx)

                            for g in range(NP):
                                gb = t * NP + g
                                if gb >= 2:
                                    tprev, gprev = divmod(gb - 2, NP)
                                    sync.wait_ge(
                                        ve, ve_t(tprev - 1) + 1
                                        + (gprev + 1))
                                sync.dma_start(
                                    out=brg[gb % 2][:, :].rearrange(
                                        "p (m l) -> p m l", m=2),
                                    in_=brep_sp[2 * g:2 * g + 2].rearrange(
                                        "m p l -> p m l")
                                ).then_inc(ldb.sems[gb % 2], 16)
                                sync.dma_start(
                                    out=crg[gb % 2][:, :].rearrange(
                                        "p (m l) -> p m l", m=2),
                                    in_=crep_sp[2 * g:2 * g + 2].rearrange(
                                        "m p l -> p m l")
                                ).then_inc(ldb.sems[gb % 2], 16)
                                if g < EV_T:
                                    spill(g)
                            for j in range(NP, EV_T):
                                spill(j)
                        gsm.wait_all(sync, NTT)

                    @block.tensor
                    def _(tensor):
                        for t in range(MT):
                            if t >= 1:
                                tensor.wait_ge(ac, ac_t(t - 2) + 1)
                            mi = None
                            for c in range(LC):
                                mi = tensor.matmul(
                                    ps_dl[:, c * CW_L:(c + 1) * CW_L],
                                    wdt_b[:, t * 128:(t + 1) * 128],
                                    dtloT[:, c * CW_L:(c + 1) * CW_L],
                                    start=True, stop=True)
                            mi.then_inc(te, 1)
                            if t == 0:
                                tensor.wait_ge(mq, 16)
                            for j in range(EV_T):
                                midx = t * EV_T + j
                                if midx >= NTT:
                                    continue
                                cc2, rem = divmod(midx, NTILE)
                                gu, lt2 = divmod(rem, TTO)
                                if j == 0 or lt2 == 0:
                                    tensor.wait_ge(
                                        mlg if gu == 0 else mlu,
                                        16 * (cc2 + 1))
                                src = tgb if gu == 0 else tub
                                if midx >= 4:
                                    tensor.wait_ge(acm, midx - 3)
                                mi = None
                                for k in range(KT):
                                    mi = tensor.matmul(
                                        ps_m[midx % 4][:, :],
                                        q1all[:, (lt2 * KT + k) * 128:
                                              (lt2 * KT + k + 1) * 128],
                                        src[:, k * CW_FF:
                                            (k + 1) * CW_FF],
                                        start=(k == 0),
                                        stop=(k == KT - 1))
                                mi.then_inc(tem, 1)

                    @block.scalar
                    def _(scalar):
                        for t in range(MT):
                            scalar.wait_ge(te, t + 1)
                            if t >= 1:
                                scalar.wait_ge(ve, ve_t(t - 2) + 1)
                            scalar.activation(ebuf[:, :], ps_dl[:, :],
                                              AF.Exp,
                                              bias=dtb_s[:, t:t + 1])
                            scalar.drain().then_inc(ac, 1)
                            scalar.activation(ebuf[:, :], ebuf[:, :], AF.Ln,
                                              bias=1.0)
                            scalar.drain().then_inc(ac, 1)
                            for p in range(NP):
                                if p >= 2:
                                    scalar.wait_ge(ve, ve_t(t - 1) + p)
                                elif t >= 1:
                                    scalar.wait_ge(ve, ve_t(t - 2) + NP + p)
                                for h in range(2):
                                    n = 2 * p + h
                                    scalar.activation(
                                        dab[p % 2][:, h * L:(h + 1) * L],
                                        ebuf[:, :], AF.Exp,
                                        scale=nega[:, t * DS + n:
                                                   t * DS + n + 1])
                                scalar.drain().then_inc(ac, 1)
                                for j2 in range(EV_T // NP if EV_T >= NP
                                                else 1):
                                    midx = t * EV_T + p * (EV_T // NP) + j2 \
                                        if EV_T >= NP else t * EV_T + p
                                    if EV_T < NP and p >= EV_T:
                                        continue
                                    if midx >= NTT or midx >= (t + 1) * EV_T:
                                        continue
                                    cc2, rem = divmod(midx, NTILE)
                                    gu, lt2 = divmod(rem, TTO)
                                    scalar.wait_ge(tem, midx + 1)
                                    if midx >= 2:
                                        gsm.wait(scalar, midx - 2)
                                    if gu == 0:
                                        scalar.activation(
                                            mst[midx % 2][:, :],
                                            ps_m[midx % 4][:, :], AF.Copy,
                                            scale=gg_s[:, lt2:lt2 + 1])
                                    else:
                                        scalar.activation(
                                            mst[midx % 2][:, :],
                                            ps_m[midx % 4][:, :], AF.Copy)
                                    scalar.drain().then_inc(acm, 1)

                    @block.vector
                    def _(vector):
                        for t in range(MT):
                            lda.wait(vector, t)
                            vector.wait_ge(ac, ac_t(t - 1) + 2)
                            vector.tensor_mul(pb[:, :], ebuf[:, :],
                                              xls[t % 2][:, :])
                            vector.drain().then_inc(ve, 1)
                            for p in range(NP):
                                vector.wait_ge(ac, ac_t(t - 1) + 2 + p + 1)
                                vector.memset(dab[p % 2][:, L:L + 1], 0.0)
                                gb = t * NP + p
                                vector.wait_ge(ldb.sems[gb % 2],
                                               32 * (gb // 2 + 1))
                                for h in range(2):
                                    vector.tensor_mul(
                                        ub[:, h * L:(h + 1) * L], pb[:, :],
                                        brg[gb % 2][:, h * L:(h + 1) * L])
                                vector.drain()
                                vector.tensor_tensor_scan(
                                    hb[:, :], dab[p % 2][:, :], ub[:, :],
                                    0.0, OP.mult, OP.add)
                                vector.drain()
                                if p == 0:
                                    for h in range(2):
                                        vector.tensor_mul(
                                            yac2[:, h * LOWN:
                                                 (h + 1) * LOWN],
                                            hb[:, h * L + LOWN:(h + 1) * L],
                                            crg[gb % 2][
                                                :, h * LOWN:(h + 1) * LOWN])
                                    vector.drain().then_inc(ve, 1)
                                    continue
                                for h in range(2):
                                    vector.tensor_mul(
                                        tmpb[p % 2][:, h * LOWN:
                                                    (h + 1) * LOWN],
                                        hb[:, h * L + LOWN:(h + 1) * L],
                                        crg[gb % 2][
                                            :, h * LOWN:(h + 1) * LOWN])
                                vector.drain()
                                vector.tensor_add(yac2[:, :], yac2[:, :],
                                                  tmpb[p % 2][:, :])
                                vector.drain().then_inc(ve, 1)
                            ldz.wait(vector, t)
                            vector.scalar_tensor_tensor(
                                yac2[:, 0:LOWN], xls[t % 2][:, LOWN:L],
                                dvec_s[:, t:t + 1], yac2[:, 0:LOWN],
                                OP.mult, OP.add)
                            vector.drain()
                            vector.tensor_add(yac2[:, 0:LOWN],
                                              yac2[:, 0:LOWN],
                                              yac2[:, LOWN:2 * LOWN])
                            vector.drain().then_inc(ve, 1)
                            if t >= 2:
                                stt_.wait(vector, t - 2)
                            vector.tensor_mul(y2b[t % 2][:, :],
                                              yac2[:, 0:LOWN],
                                              szs[t % 2][:, :])
                            vector.drain().then_inc(ve, 1)

                    @block.gpsimd
                    def _(gpsimd):
                        for t in range(MT):
                            gpsimd.wait_ge(ve, ve_t(t))
                            ins = gpsimd.dma_start(
                                out=y2_sp[t], in_=y2b[t % 2][:, :])
                            stt_.inc(ins, t)
                        stt_.wait_all(gpsimd, MT)

        s_early.close()

        with contextlib.ExitStack() as sp:
            psb = lambda n, s, d=F32: sp.enter_context(nc.sbuf_tensor(n, s, d))
            pps = lambda n, s, d=F32: sp.enter_context(nc.psum_tensor(n, s, d))
            NDQ = DM // CW_DM
            y2all = psb("p3b_y2all", [128, MT * LOWN], BF16)
            wos = [psb(f"p3b_wo{i}", [128, CW_DM], BF16) for i in range(3)]
            xrs = [psb(f"p3b_xr{i}", [128, CW_DM], F32) for i in range(2)]
            omb = [psb(f"p3b_om{i}", [128, CW_DM], F32) for i in range(2)]
            ps8 = pps("p3b_ps8", [128, TTO * 512], F32)
            ldy = sem("p3b_ldy")
            ldw = DStream(sem, "p3b_ldw", 3)
            ldx = DStream(sem, "p3b_ldx")
            te = sem("p3b_te")
            ve = sem("p3b_ve")
            stt_ = DStream(sem, "p3b_st")
            with nc.Block("p3b") as block:
                @block.sync
                def _(sync):
                    sync.dma_start(
                        out=y2all[:, :].rearrange("p (t l) -> p t l", t=MT),
                        in_=y2_sp.rearrange("t p l -> p t l")
                    ).then_inc(ldy, 16)
                    for dq in range(NDQ):
                        for k in range(MT):
                            gi = dq * MT + k
                            if gi >= 3:
                                sync.wait_ge(te, gi - 2)
                            ldw.inc(sync.dma_start(
                                out=wos[gi % 3][:, :],
                                in_=wo_bf[k * 128:(k + 1) * 128,
                                          dq * CW_DM:(dq + 1) * CW_DM]), gi)
                        for lt in range(TTO):
                            gx = dq * TTO + lt
                            if gx >= 2:
                                sync.wait_ge(ve, gx - 1)
                            ldx.inc(sync.dma_start(
                                out=xrs[gx % 2][:, :],
                                in_=x_in[LOWN + lt * 128:
                                         LOWN + (lt + 1) * 128,
                                         dq * CW_DM:(dq + 1) * CW_DM]), gx)

                @block.tensor
                def _(tensor):
                    tensor.wait_ge(ldy, 16)
                    for dq in range(NDQ):
                        if dq >= 1:
                            tensor.wait_ge(ve, dq * TTO)
                        for k in range(MT):
                            gi = dq * MT + k
                            ldw.wait(tensor, gi)
                            mi = None
                            for lt in range(TTO):
                                mi = tensor.matmul(
                                    ps8[:, lt * 512:lt * 512 + CW_DM],
                                    y2all[:, k * LOWN + lt * 128:
                                          k * LOWN + (lt + 1) * 128],
                                    wos[gi % 3][:, :],
                                    start=(k == 0), stop=(k == MT - 1))
                            mi.then_inc(te, 1)

                @block.vector
                def _(vector):
                    for dq in range(NDQ):
                        vector.wait_ge(te, (dq + 1) * MT)
                        for lt in range(TTO):
                            gx = dq * TTO + lt
                            ldx.wait(vector, gx)
                            if gx >= 2:
                                stt_.wait(vector, gx - 2)
                            vector.tensor_add(
                                omb[gx % 2][:, :],
                                ps8[:, lt * 512:lt * 512 + CW_DM],
                                xrs[gx % 2][:, :])
                            vector.drain().then_inc(ve, 1)

                @block.gpsimd
                def _(gpsimd):
                    for dq in range(NDQ):
                        for lt in range(TTO):
                            gx = dq * TTO + lt
                            gpsimd.wait_ge(ve, gx + 1)
                            stt_.inc(gpsimd.dma_start(
                                out=outm_sp[lt][:, dq * CW_DM:
                                                (dq + 1) * CW_DM],
                                in_=omb[gx % 2][:, :]), gx)
                    stt_.wait_all(gpsimd, NDQ * TTO)

        with contextlib.ExitStack() as sp:
            psb = lambda n, s, d=F32: sp.enter_context(nc.sbuf_tensor(n, s, d))
            pps = lambda n, s, d=F32: sp.enter_context(nc.psum_tensor(n, s, d))
            dsc_s = psb("dsc_s", [128, TTO], F32)
            sgl = [psb(f"p4_sg{i}", [128, FF], BF16) for i in range(2)]
            upl = [psb(f"p4_up{i}", [128, FF], BF16) for i in range(2)]
            midb = psb("p4_mid", [128, FF], F32)
            pongb = psb("p4_pong", [128, FF], F32)
            q3b = psb("p4_q3", [128, FF], BF16)
            q3t = psb("p4_q3t", [128, FFT * 128], BF16)
            NDB = 4
            tdb = [psb(f"p4_td{i}", [128, DM], BF16) for i in range(NDB)]
            finb = [psb(f"p4_fin{i}", [128, DM], F32) for i in range(2)]
            QCH = min(512, FF // 4)
            NQ = FF // QCH
            qscr = psb("p4_qscr", [128, 3 * QCH], F32)
            c_am = psb("p4_cam", [128, 1], F32)
            c_vs = psb("p4_cvs", [128, 1], F32)
            ps_d = pps("p4_psd", [128, DM], F32)
            ps_t = pps("p4_pst", [128, 8 * 128], BF16).ap()
            ldg = DStream(sem, "p4_ldg")
            ldu = DStream(sem, "p4_ldu")
            ldd = DStream(sem, "p4_ldd", nbuf=NDB)
            ldo = DStream(sem, "p4_ldo")
            tr = sem("p4_tr")
            ted = sem("p4_ted")
            ve = sem("p4_ve")
            ac = sem("p4_ac")
            asg = sem("p4_asg")
            aq = sem("p4_aq")
            qfull = pongb if NSTG % 2 == 0 else midb
            fwb = midb if NSTG % 2 == 0 else pongb
            stp4 = DStream(sem, "p4_st")

            def ve_mid(lt):
                return 1 if lt == 0 else 4 * lt

            def ve_scales(lt):
                return 2 if lt == 0 else 4 * lt + 1

            def ve_q3(lt):
                return 3 if lt == 0 else 4 * lt + 2

            def ve_fin(lt):
                return 4 * TTO if lt == TTO - 1 else 4 * (lt + 1) + 3

            with nc.Block("p4") as block:
                @block.sync
                def _(sync):
                    for lt in range(TTO):
                        for k in range(FFT):
                            gi2 = lt * FFT + k
                            if gi2 >= NDB:
                                sync.wait_ge(ted, gi2 - NDB + 1)
                            ldd.inc(sync.dma_start(
                                out=tdb[gi2 % NDB][:, :],
                                in_=tern_d[k * 128:(k + 1) * 128, :]), gi2)

                @block.tensor
                def _(tensor):
                    for lt in range(TTO):
                        tensor.wait_ge(ve, ve_q3(lt))
                        for g in range(FFT8):
                            nk = min(8, FFT - g * 8)
                            if g >= 1:
                                tensor.wait_ge(ac, lt * FFT8 + g)
                            mi = None
                            for j in range(nk):
                                k = g * 8 + j
                                mi = tensor.transpose(
                                    ps_t[:, j * 128:(j + 1) * 128],
                                    q3b[:, k * 128:(k + 1) * 128],
                                    ident_s[:, :])
                            mi.then_inc(tr, 1)
                        tensor.wait_ge(ac, (lt + 1) * FFT8)
                        if lt >= 1:
                            tensor.wait_ge(ve, ve_fin(lt - 1))
                        for k in range(FFT):
                            gi2 = lt * FFT + k
                            ldd.wait(tensor, gi2)
                            mi = None
                            for c in range(DMC):
                                mi = tensor.matmul(
                                    ps_d[:, c * CW_DM:(c + 1) * CW_DM],
                                    q3t[:, k * 128:(k + 1) * 128],
                                    tdb[gi2 % NDB][:, c * CW_DM:
                                                   (c + 1) * CW_DM],
                                    start=(k == 0), stop=(k == FFT - 1))
                            mi.then_inc(ted, 1)

                @block.scalar
                def _(scalar):
                    def silu(lt):
                        ldg.wait(scalar, lt)
                        scalar.activation(sgl[lt % 2][:, :],
                                          sgl[lt % 2][:, :], AF.Silu)
                        scalar.drain().then_inc(asg, 1)

                    for lt in range(min(2, TTO)):
                        silu(lt)
                    for lt in range(TTO):
                        scalar.wait_ge(ve, ve_scales(lt))
                        scalar.activation(qfull[:, :], fwb[:, :], AF.Copy,
                                          scale=c_vs[:, 0:1])
                        scalar.drain().then_inc(aq, 1)
                        if lt >= 1:
                            scalar.wait_ge(ted, lt * FFT)
                        for g in range(FFT8):
                            nk = min(8, FFT - g * 8)
                            scalar.wait_ge(tr, lt * FFT8 + g + 1)
                            scalar.activation(
                                q3t[:, g * 8 * 128:(g * 8 + nk) * 128],
                                ps_t[:, 0:nk * 128], AF.Copy)
                            scalar.drain().then_inc(ac, 1)
                        if lt + 2 < TTO:
                            silu(lt + 2)

                def fin(vector, lt):
                    vector.wait_ge(ted, (lt + 1) * FFT)
                    ldo.wait(vector, lt)
                    vector.scalar_tensor_tensor(
                        finb[lt % 2][:, :], ps_d[:, :],
                        dsc_s[:, lt:lt + 1], finb[lt % 2][:, :],
                        OP.mult, OP.add)
                    vector.drain().then_inc(ve, 1)

                @block.vector
                def _(vector):
                    for lt in range(TTO):
                        vector.wait_ge(asg, lt + 1)
                        ldu.wait(vector, lt)
                        vector.tensor_mul(midb[:, :], sgl[lt % 2][:, :],
                                          upl[lt % 2][:, :])
                        vector.drain().then_inc(ve, 1)
                        cur, nxt = midb, pongb
                        for st_i in range(NSTG):
                            h = 1 << st_i
                            a = cur[:, :].rearrange(
                                "p (nb two h) -> p nb two h", two=2, h=h)
                            o = nxt[:, :].rearrange(
                                "p (nb two h) -> p nb two h", two=2, h=h)
                            vector.tensor_add(o[:, :, 0, :], a[:, :, 0, :],
                                              a[:, :, 1, :])
                            vector.tensor_sub(o[:, :, 1, :], a[:, :, 0, :],
                                              a[:, :, 1, :])
                            vector.drain()
                            cur, nxt = nxt, cur
                        fw = cur
                        vector.reduce_max(c_am[:, :], fw[:, :], AX.X,
                                          apply_absolute_value=True)
                        vector.drain()
                        vector.tensor_mul(c_am[:, :], c_am[:, :],
                                          kk_s[:, lt:lt + 1])
                        vector.drain()
                        vector.tensor_scalar_max(c_am[:, :], c_am[:, :],
                                                 1e-5)
                        vector.drain()
                        vector.tensor_mul(dsc_s[:, lt:lt + 1], c_am[:, :],
                                          mxd_c[:, 0:1])
                        vector.drain()
                        vector.tensor_scalar_mul(dsc_s[:, lt:lt + 1],
                                                 dsc_s[:, lt:lt + 1],
                                                 1.0 / 128.0)
                        vector.reciprocal(c_vs[:, :], c_am[:, :])
                        vector.drain()
                        vector.tensor_scalar_mul(c_vs[:, :], c_vs[:, :],
                                                 128.0)
                        vector.drain()
                        vector.tensor_mul(c_vs[:, :], c_vs[:, :],
                                          kk_s[:, lt:lt + 1])
                        vector.drain().then_inc(ve, 1)
                        qib = qscr[:, 0:QCH].bitcast(I32)
                        q0b = qscr[:, 1 * QCH:2 * QCH]
                        qrb = qscr[:, 2 * QCH:3 * QCH]
                        if lt >= 1:
                            vector.wait_ge(tr, lt * FFT8)
                        vector.wait_ge(aq, lt + 1)
                        for qc in range(NQ):
                            sl = slice(qc * QCH, (qc + 1) * QCH)
                            qvb = qfull[:, sl]
                            vector.tensor_copy(qib, qvb)
                            vector.drain()
                            vector.tensor_copy(q0b, qib)
                            vector.drain()
                            vector.tensor_sub(qrb, qvb, q0b)
                            vector.drain()
                            vector.scalar_tensor_tensor(
                                q0b, qrb, 0.5, q0b, OP.is_ge, OP.add)
                            vector.drain()
                            vector.scalar_tensor_tensor(
                                q0b, qrb, -0.5, q0b, OP.is_gt, OP.add)
                            vector.drain()
                            vector.tensor_scalar(q3b[:, sl], q0b,
                                                 -1.0, 127.0,
                                                 OP.add, OP.min)
                            vector.drain()
                        vector.drain().then_inc(ve, 1)
                        if lt >= 1:
                            fin(vector, lt - 1)
                    fin(vector, TTO - 1)

                @block.gpsimd
                def _(gpsimd):
                    for lt in range(min(2, TTO)):
                        ldg.inc(gpsimd.dma_start(out=sgl[lt % 2][:, :],
                                                 in_=sg_sp[lt]), lt)
                        ldu.inc(gpsimd.dma_start(out=upl[lt % 2][:, :],
                                                 in_=up_sp[lt]), lt)
                        ldo.inc(gpsimd.dma_start(out=finb[lt % 2][:, :],
                                                 in_=outm_sp[lt]), lt)
                    for lt in range(TTO):
                        if lt + 2 < TTO:
                            gpsimd.wait_ge(ve, ve_mid(lt))
                            ldg.inc(gpsimd.dma_start(
                                out=sgl[lt % 2][:, :],
                                in_=sg_sp[lt + 2]), lt + 2)
                            ldu.inc(gpsimd.dma_start(
                                out=upl[lt % 2][:, :],
                                in_=up_sp[lt + 2]), lt + 2)
                        gpsimd.wait_ge(ve, ve_fin(lt))
                        stp4.inc(gpsimd.dma_start(
                            out=out_t[lt * 128:(lt + 1) * 128, :],
                            in_=finb[lt % 2][:, :]), lt)
                        if lt + 2 < TTO:
                            stp4.wait(gpsimd, lt)
                            ldo.inc(gpsimd.dma_start(
                                out=finb[lt % 2][:, :],
                                in_=outm_sp[lt + 2]), lt + 2)
                    stp4.wait_all(gpsimd, TTO)

    return nc



_BUILD_CACHE = {}


def _get_program(C):
    key = tuple(sorted(C.items()))
    if key not in _BUILD_CACHE:
        _BUILD_CACHE[key] = build_program(C)
    return _BUILD_CACHE[key]


def _hadamard_ok():
    pass


def make_in_maps(C, inputs):
    DM, DI, DS, DC = C["DM"], C["DI"], C["DS"], C["DC"]
    FF, LOWN = C["FF"], C["LOWN"]
    L = 2 * LOWN
    KT = DM // 128
    MT = DI // 128

    f32 = lambda a: np.ascontiguousarray(a, dtype=np.float32)
    x = f32(inputs["L0_attn_hidden_states"])
    B = x.shape[0]
    in_w = f32(inputs["L0_attn_in_proj_w"])
    convw = f32(inputs["L0_attn_conv_w"])
    convb = f32(inputs["L0_attn_conv_b"])
    xw = f32(inputs["L0_attn_x_proj_w"])
    dtw = f32(inputs["L0_attn_dt_proj_w"])
    dtb = f32(inputs["L0_attn_dt_proj_b"])
    alog = f32(inputs["L0_attn_A_log"])
    dv = f32(inputs["L0_attn_D"])
    ow = f32(inputs["L0_attn_out_proj_w"])
    wg = f32(inputs["L0_ffn_w_gate"])
    wu = f32(inputs["L0_ffn_w_up"])
    wd = f32(inputs["L0_ffn_w_down"])

    w_in4 = np.ascontiguousarray(
        in_w.T.reshape(KT, 128, 2 * MT, 128).transpose(2, 1, 0, 3))
    conv_w4 = np.ascontiguousarray(
        convw.reshape(MT, 128, DC).transpose(1, 0, 2).reshape(128, MT * DC))
    conv_b4 = np.ascontiguousarray(convb.reshape(MT, 128).T)
    w_x4 = np.ascontiguousarray(
        xw.T.reshape(MT, 128, -1).transpose(1, 0, 2).reshape(128, -1))
    w_dtT = np.ascontiguousarray(dtw.T)
    dt_b4 = np.ascontiguousarray(dtb.reshape(MT, 128).T)
    a_log4 = np.ascontiguousarray(
        alog.reshape(MT, 128, DS).transpose(1, 0, 2).reshape(128, MT * DS))
    d_vec4 = np.ascontiguousarray(dv.reshape(MT, 128).T)
    w_oT = np.ascontiguousarray(ow.T)
    w_gT = np.ascontiguousarray(wg.T)
    w_uT = np.ascontiguousarray(wu.T)
    w_dT = np.ascontiguousarray(wd.T)
    identm = np.eye(128, dtype=ml_dtypes.bfloat16)

    in_maps = []
    for c in range(2 * B):
        b, h = c // 2, c % 2
        if h == 0:
            pre = np.zeros((LOWN, DM), np.float32)
            own = x[b, 0:LOWN]
            mrow = np.concatenate([np.zeros(LOWN, np.float32),
                                   np.ones(LOWN, np.float32)])
        else:
            pre = x[b, 0:LOWN]
            own = x[b, LOWN:L]
            mrow = np.ones(L, np.float32)
        x_core = np.ascontiguousarray(np.concatenate([pre, own], axis=0))
        in_maps.append({
            "x_in": x_core,
            "w_in4": w_in4,
            "conv_w": conv_w4,
            "conv_b": conv_b4,
            "w_x4": w_x4,
            "w_dtT": w_dtT,
            "dt_b": dt_b4,
            "a_log": a_log4,
            "d_vec": d_vec4,
            "w_oT": w_oT,
            "w_gT": w_gT,
            "w_uT": w_uT,
            "w_dT": w_dT,
            "ident": identm,
            "maskrow": mrow.reshape(1, L),
        })
    return in_maps


def kernel(**inputs):
    C = full_cfg()
    nc = _get_program(C)
    in_maps = make_in_maps(C, inputs)
    res = run_bass_kernel_spmd(nc, in_maps, core_ids=list(range(8)))
    B = inputs["L0_attn_hidden_states"].shape[0]
    LOWN = C["LOWN"]
    DM = C["DM"]
    out = np.empty((B, 2 * LOWN, DM), np.float32)
    for c in range(2 * B):
        b, h = c // 2, c % 2
        out[b, h * LOWN:(h + 1) * LOWN] = res.results[c]["out"]
    return out
